# version 07
# baseline (speedup 1.0000x reference)
import numpy as np
import ml_dtypes
from contextlib import ExitStack

import concourse.bass as bass
import concourse.mybir as mybir
from concourse.bass_core_rust import IndirectOffsetOnAxis
from concourse.bass_utils import run_bass_kernel_spmd

F32 = mybir.dt.float32
BF16 = mybir.dt.bfloat16
I32 = mybir.dt.int32
U32 = mybir.dt.uint32
AX = mybir.AxisListType.X
AF = mybir.ActivationFunctionType
OP = mybir.AluOpType

P = 128
S = 8192
D = 1024
HD = 64
NH = 8
NBLK = 511
NSEL = 1024
WIN = 512
DIN = 2048
IFF = 4096
HALF = S // 2
NG = 16
GT = 512

MAGICF = 1.6154731354610118e+38
NPIPE = 20
PAIRS = [[0, 1], [2, 3], [4, 5], [6, 7]]
ALL8 = [[0, 1, 2, 3, 4, 5, 6, 7]]

DEBUG_TAPS: list = []
BUILD_UPTO = 99


class B:
    def __init__(self):
        self.nc = bass.Bass("TRN2", target_bir_lowering=False)
        self.ctx = ExitStack()
        self.taps = {}
        self._sems = {}
        self.scopes = [self.ctx]

    def din(self, name, shape, dt=F32):
        return self.nc.dram_tensor(name, shape, dt, kind="ExternalInput").ap()

    def sb(self, name, shape, dt):
        return self.scopes[-1].enter_context(self.nc.sbuf_tensor(name, shape, dt))

    def ps(self, name, shape, dt=F32):
        return self.scopes[-1].enter_context(self.nc.psum_tensor(name, shape, dt))

    def push(self):
        es = ExitStack()
        self.scopes.append(es)
        return es

    def pop(self):
        self.scopes.pop().close()

    def sem(self, name):
        if name not in self._sems:
            self._sems[name] = self.ctx.enter_context(self.nc.semaphore(name))
        return self._sems[name]

    def tap(self, name, shape, dt=F32):
        if name in DEBUG_TAPS:
            self.taps[name] = self.nc.dram_tensor("tap_" + name, shape, dt,
                                                  kind="ExternalOutput").ap()
        return self.taps.get(name)


def build_nc():
    b = B()
    nc = b.nc

    x_d = b.din("x", [S, D])
    wqkv_d = b.din("wqkv", [D, 1536])
    wck_d = b.din("wck", [2048, HD])
    wcv_d = b.din("wcv", [2048, HD])
    bckv_d = b.din("bckv", [HD, 2])
    wg_d = b.din("wg", [HD, 3])
    bg8_d = b.din("bg8", [NH, 3])
    wo_d = b.din("wo", [D, D])
    w0t_d = b.din("w0t", [D, D])
    w_in_d = b.din("w_in", [D, DIN])
    w_g1_d = b.din("w_g1", [DIN, DIN])
    w_out_d = b.din("w_out", [DIN, D])
    b_in_d = b.din("b_in", [P, 16])
    b_g1_d = b.din("b_g1", [P, 16])
    b_sp_d = b.din("b_sp", [P, 16])
    b_out_d = b.din("b_out", [P, 8])
    nw_d = b.din("nw", [P, 8])
    nb_d = b.din("nb", [P, 8])
    gate_sl_d = b.din("gate_sl", [512, D])
    up_sl_d = b.din("up_sl", [512, D])
    down_sl_d = b.din("down_sl", [P, IFF])
    sins_d = b.din("sinsd", [1, 4])
    eye_d = b.din("eyed", [P, P])
    ar64_d = b.din("ar64", [1, HD])
    posidx_d = b.din("posidx", [P, 4], I32)

    out_d = nc.dram_tensor("out", [HALF, D], F32, kind="ExternalOutput").ap()

    qbuf = nc.dram_tensor("qbuf", [65, NH * S], BF16).ap()
    kbuf = nc.dram_tensor("kbuf", [HD, NH * S], BF16).ap()
    vbuf = nc.dram_tensor("vbuf", [HD, NH * S], BF16).ap()
    compbuf = nc.dram_tensor("compbuf", [HD, NH * S], BF16).ap()
    attfull = nc.dram_tensor("attfull", [512, S], BF16).ap()
    wbfg = nc.dram_tensor("wbfg", [P, 16 * DIN], BF16).ap()
    wbfi = nc.dram_tensor("wbfi", [P, 8 * DIN], BF16).ap()
    wbfo = nc.dram_tensor("wbfo", [P, 16 * D], BF16).ap()
    bi_in = nc.dram_tensor("bi_in", [1, 512], F32).ap()
    bi_out = nc.dram_tensor("bi_out", [1, 512], F32).ap()
    am_in = nc.dram_tensor("am_in", [1, 4], F32).ap()
    am_out = nc.dram_tensor("am_out", [1, 4], F32).ap()
    gtmp = nc.dram_tensor("gtmp", [NH, 3], BF16).ap()
    mtmp = nc.dram_tensor("mtmp", [NH, 2], F32).ap()
    agg_in = nc.dram_tensor("agg_in", [D, 512], BF16).ap()
    agg_out = nc.dram_tensor("agg_out", [8 * D, 512], BF16).ap()
    agu_in = nc.dram_tensor("agu_in", [D, 512], BF16).ap()
    agu_out = nc.dram_tensor("agu_out", [8 * D, 512], BF16).ap()
    agd_in = nc.dram_tensor("agd_in", [IFF, P], BF16).ap()
    agd_out = nc.dram_tensor("agd_out", [8 * IFF, P], BF16).ap()
    hqbuf = nc.dram_tensor("hqbuf", [IFF, HALF], BF16).ap()
    amsc = nc.dram_tensor("amsc", [1, 16], F32).ap()
    att_in = nc.dram_tensor("att_in", [512, HALF], BF16).ap()
    att_out = nc.dram_tensor("att_out", [1024, HALF], BF16).ap()

    b.tap("qkv", [192, S], BF16)
    b.tap("gates", [HD, 24])
    b.tap("comp", [65, NH * NBLK], BF16)
    b.tap("mks", [1, NH])
    b.tap("kwin", [65, NH * WIN], BF16)
    b.tap("cvtok", [P, NH * 4 * 65], BF16)
    b.tap("wvtok", [P, NH * 4 * 65], BF16)
    b.tap("svtok", [P, NH * 8 * 65], BF16)
    b.tap("p1", [HD, NH * S], BF16)
    b.tap("bi", [P, 4])
    b.tap("birow", [1, 512])
    b.tap("sel", [P, 8], I32)
    b.tap("att", [512, S], BF16)
    b.tap("stream0", [P, 8 * HALF], BF16)
    b.tap("stream1", [P, 8 * HALF], BF16)
    b.tap("stream2", [P, 8 * HALF], BF16)
    b.tap("wq", [P, 12 * 1024], BF16)
    b.tap("sccolt", [P, 16])
    b.tap("stream3", [P, 8 * HALF], BF16)

    eye = b.sb("eye_sb", [P, P], BF16)
    hsel = b.sb("hsel", [P, 2], BF16)
    ones_row = b.sb("ones_row", [1, P], BF16)
    ones1k = b.sb("ones1k", [1, 1024], BF16)
    gcol = b.sb("gcol", [HD, 24], F32)
    grow = b.sb("grow", [1, 24], BF16)
    mksr = b.sb("mksr", [1, NH], F32)
    negmk = b.sb("negmk", [1, NH], F32)
    gsum = b.sb("gsum", [P, 4 * NG], F32)
    gsum4 = b.sb("gsum4", [P, 4], F32)
    qmean8 = b.sb("qmean8", [HD, NH], BF16)
    gates_s = b.sb("gates_s", [NH, 3], F32)
    gates_sb = b.sb("gates_sb", [NH, 3], BF16)
    swg = b.sb("swg", [HD, 3], BF16)
    sbg8 = b.sb("sbg8", [NH, 3], F32)
    sbckv = b.sb("sbckv", [HD, 2], F32)
    sins = b.sb("sins", [1, 4], F32)
    smeta = b.sb("smeta", [P, 4], I32)
    sar64 = b.sb("sar64", [1, HD], F32)
    kmax = b.sb("kmax", [HD, NH], F32)
    ckmax = b.sb("ckmax", [HD, NH], F32)
    kmax_b = b.sb("kmax_b", [HD, NH], BF16)
    mkst = b.sb("mkst", [NH, 2], F32)
    selidx = b.sb("selidx", [P, 8], I32)
    b_in_sb = b.sb("b_in_sb", [P, 16], F32)
    b_g1_sb = b.sb("b_g1_sb", [P, 16], F32)
    b_sp_sb = b.sb("b_sp_sb", [P, 16], F32)
    b_out_sb = b.sb("b_out_sb", [P, 8], F32)
    nw_sb = b.sb("nw_sb", [P, 8], F32)
    nb_sb = b.sb("nb_sb", [P, 8], F32)
    ones_c128 = b.sb("ones_c128", [P, 1], BF16)
    mgkc = b.sb("mgkc", [P, 64], F32)
    sccol = b.sb("sccol", [P, 16], F32)
    bi4 = b.sb("bi4", [P, 4], F32)
    b.push()
    selpos = b.sb("selpos", [1, NSEL], F32)
    bi_row = b.sb("bi_row", [1, 512], F32)
    bi_row2 = b.sb("bi_row2", [1, 512], F32)
    ckTaE = b.sb("ckTaE", [65, 4 * NBLK], BF16)
    ckTaO = b.sb("ckTaO", [65, 4 * NBLK], BF16)
    cvtok = b.sb("cvtok", [P, NH * 4 * 65], BF16)
    kwinE = b.sb("kwinE", [65, 4 * WIN], BF16)
    kwinO = b.sb("kwinO", [65, 4 * WIN], BF16)
    wvtok = b.sb("wvtok", [P, NH * 4 * 65], BF16)
    kselE = b.sb("kselE", [65, 4 * NSEL], BF16)
    kselO = b.sb("kselO", [65, 4 * NSEL], BF16)
    svtok = b.sb("svtok", [P, NH * 8 * 65], BF16)
    bislot = b.sb("bislot", [P, 4 * NG], F32)

    sW = b.sem("sW")
    sQ = b.sem("sQ")
    sG = b.sem("sG")
    sT = b.sem("sT")
    sV = b.sem("sV")
    sV2 = b.sem("sV2")
    sV3 = b.sem("sV3")
    sS = b.sem("sS")
    sS2 = b.sem("sS2")
    sC = b.sem("sC")
    sX0 = b.sem("sX0")
    sX1 = b.sem("sX1")
    sK0 = b.sem("sK0")
    sWQ0 = b.sem("sWQ0")
    sWQ1 = b.sem("sWQ1")
    sX2 = b.sem("sX2")
    sX3 = b.sem("sX3")
    sK1 = b.sem("sK1")

    blk = b.ctx.enter_context(nc.Block())
    sB = b.sem("sB")
    barn = [0]
    DMAT = {}

    def bar():
        barn[0] += 1
        tgt = 5 * barn[0]
        waits = list(DMAT.items())

        @blk.sync
        def _(e):
            for sem_, val_ in waits:
                e.wait_ge(b._sems[sem_], val_)
            e.drain(); e.sem_inc(sB, 1); e.wait_ge(sB, tgt)

        @blk.vector
        def _(e):
            e.drain(); e.sem_inc(sB, 1); e.wait_ge(sB, tgt)

        @blk.scalar
        def _(e):
            e.drain(); e.sem_inc(sB, 1); e.wait_ge(sB, tgt)

        @blk.tensor
        def _(e):
            e.drain(); e.sem_inc(sB, 1); e.wait_ge(sB, tgt)

        @blk.gpsimd
        def _(e):
            e.drain(); e.sem_inc(sB, 1); e.wait_ge(sB, tgt)

    N = dict(W=0, Q=0, G=0, T=0, V=0, V2=0, V3=0, S=0, S2=0, C=0)

    def _qpipe_v(v, pos0, ax, c1, c2, qa, scb, qi1, qi2, sc8q, dqq, rdqq, dqhq,
                 mgk, sc6ap, rgsap, nb, outap, sgnap, sgn_sem, sgn_val):
        wid = nb * 16
        r32 = qi1[:, 0:nb].bitcast(F32)
        t32 = qi2[:, 0:nb].bitcast(F32)
        v.wait_ge(sV, pos0)
        v.tensor_reduce(qa[:, 0:nb], ax[:, 0:wid].rearrange("p (n b) -> p n b", b=16),
                        AX, OP.max).then_inc(sV, 1)
        v.wait_ge(sV, pos0 + 1)
        v.tensor_scalar(scb[:, 0:nb], qa[:, 0:nb], sc6ap, 448.0, OP.mult,
                        OP.min).then_inc(sV, 1)
        v.wait_ge(sV, pos0 + 2)
        v.tensor_scalar(qi1[:, 0:nb], scb[:, 0:nb].bitcast(I32), 20, 1,
                        OP.logical_shift_right, OP.bitwise_and).then_inc(sV, 1)
        v.wait_ge(sV, pos0 + 3)
        v.scalar_tensor_tensor(qi2[:, 0:nb], scb[:, 0:nb].bitcast(I32), 524287,
                               qi1[:, 0:nb], OP.add, OP.add).then_inc(sV, 1)
        v.wait_ge(sV, pos0 + 4)
        v.tensor_single_scalar(sc8q[:, 0:nb].bitcast(I32), qi2[:, 0:nb], -1048576,
                               OP.bitwise_and).then_inc(sV, 1)
        v.wait_ge(sV, pos0 + 5)
        v.tensor_scalar_mul(dqq[:, 0:nb], sc8q[:, 0:nb], rgsap).then_inc(sV, 1)
        v.wait_ge(sV, pos0 + 6)
        v.tensor_scalar_max(rdqq[:, 0:nb], dqq[:, 0:nb], 1e-30).then_inc(sV, 1)
        v.wait_ge(sV, pos0 + 7)
        v.reciprocal(rdqq[:, 0:nb], rdqq[:, 0:nb]).then_inc(sV, 1)
        v.wait_ge(sV, pos0 + 6)
        v.tensor_scalar_mul(dqhq[:, 0:nb], dqq[:, 0:nb], 0.5).then_inc(sV, 1)
        v.wait_ge(sV, pos0 + 9)
        v.tensor_tensor(ax[:, 0:wid].rearrange("p (n b) -> p n b", b=16),
                        ax[:, 0:wid].rearrange("p (n b) -> p n b", b=16),
                        rdqq[:, 0:nb].unsqueeze(2).broadcast_to([P, nb, 16]),
                        OP.mult).then_inc(sV, 1)
        v.wait_ge(sV, pos0 + 10)
        v.tensor_single_scalar(c1[:, 0:wid], ax[:, 0:wid], 0.25, OP.is_ge).then_inc(sV, 1)
        for ki, mid in enumerate((0.75, 1.25, 1.75)):
            v.wait_ge(sV, pos0 + 11 + ki)
            v.scalar_tensor_tensor(c1[:, 0:wid], ax[:, 0:wid], mid, c1[:, 0:wid],
                                   OP.is_ge, OP.add).then_inc(sV, 1)
        v.wait_ge(sV, pos0 + 10)
        v.tensor_scalar(c2[:, 0:wid], ax[:, 0:wid], 5.0, 2.0, OP.is_ge,
                        OP.mult).then_inc(sV, 1)
        for ki, mid in enumerate((3.5, 2.5)):
            v.wait_ge(sV, pos0 + 15 + ki)
            v.scalar_tensor_tensor(c2[:, 0:wid], ax[:, 0:wid], mid, c2[:, 0:wid],
                                   OP.is_ge, OP.add).then_inc(sV, 1)
        v.wait_ge(sV, pos0 + 17)
        v.scalar_tensor_tensor(c1[:, 0:wid], c2[:, 0:wid], 2.0, c1[:, 0:wid],
                               OP.mult, OP.add).then_inc(sV, 1)
        v.wait_ge(sV, pos0 + 18)
        v.tensor_tensor(c1[:, 0:wid].rearrange("p (n b) -> p n b", b=16),
                        c1[:, 0:wid].rearrange("p (n b) -> p n b", b=16),
                        dqhq[:, 0:nb].unsqueeze(2).broadcast_to([P, nb, 16]),
                        OP.mult).then_inc(sV, 1)
        v.wait_ge(sV, pos0 + 19)
        v.wait_ge(sgn_sem, sgn_val)
        v.tensor_tensor(outap, c1[:, 0:wid], sgnap, OP.mult).then_inc(sV, 1)
        return pos0 + NPIPE



    wckv_sb = b.sb("wckv_sb", [HD, 2 * 32 * HD], BF16)
    b.push()
    stg0 = b.sb("stg0", [P, 2304], F32)

    @blk.sync
    def _(sync):
        for ap_out, ap_in in [
            (smeta[:], posidx_d[:]), (sins[:], sins_d[:]), (sar64[:], ar64_d[:]),
        ]:
            sync.dma_start(out=ap_out, in_=ap_in).then_inc(sW, 16); N['W'] += 16
        sync.dma_start(out=stg0[:, 0:P], in_=eye_d[:]).then_inc(sW, 16); N['W'] += 16
        for c in range(16):
            sync.dma_start(out=stg0[:, P + c * HD:P + (c + 1) * HD],
                           in_=wck_d[c * P:(c + 1) * P, :]).then_inc(sW, 16); N['W'] += 16
            sync.dma_start(out=stg0[:, P + (16 + c) * HD:P + (17 + c) * HD],
                           in_=wcv_d[c * P:(c + 1) * P, :]).then_inc(sW, 16); N['W'] += 16
        sync.dma_start(out=stg0[0:HD, 2176:2178], in_=bckv_d[:]).then_inc(sW, 16); N['W'] += 16
        sync.dma_start(out=stg0[0:HD, 2178:2181], in_=wg_d[:]).then_inc(sW, 16); N['W'] += 16
        sync.dma_start(out=stg0[0:NH, 2181:2184], in_=bg8_d[:]).then_inc(sW, 16); N['W'] += 16
        for ap_out, ap_in in [(b_in_sb[:], b_in_d[:]), (b_g1_sb[:], b_g1_d[:]),
                              (b_sp_sb[:], b_sp_d[:]), (b_out_sb[:], b_out_d[:]),
                              (nw_sb[:], nw_d[:]), (nb_sb[:], nb_d[:])]:
            sync.dma_start(out=ap_out, in_=ap_in).then_inc(sW, 16); N['W'] += 16

    @blk.vector
    def _(v):
        v.wait_ge(sW, N['W'])
        v.tensor_copy(eye[:], stg0[:, 0:P])
        for j in range(32):
            v.tensor_copy(wckv_sb[0:HD, j * HD:(j + 1) * HD],
                          stg0[(j % 2) * HD:(j % 2) * HD + HD, P + (j // 2) * HD:P + (j // 2 + 1) * HD])
            v.tensor_copy(wckv_sb[0:HD, 2048 + j * HD:2048 + (j + 1) * HD],
                          stg0[(j % 2) * HD:(j % 2) * HD + HD, P + (16 + j // 2) * HD:P + (17 + j // 2) * HD])
        v.tensor_copy(sbckv[:], stg0[0:HD, 2176:2178])
        v.tensor_copy(swg[:], stg0[0:HD, 2178:2181])
        v.tensor_copy(sbg8[:], stg0[0:NH, 2181:2184])
        v.memset(hsel[0:64, 0:1], 1.0)
        v.memset(hsel[64:128, 0:1], 0.0)
        v.memset(hsel[0:64, 1:2], 0.0)
        v.memset(hsel[64:128, 1:2], 1.0)
        v.memset(ones_row[:], 1.0)
        v.memset(ones1k[:], 1.0)
        v.memset(ones_c128[:], 1.0)
        v.memset(mgkc[:], MAGICF)
        v.memset(cvtok[:].rearrange("p (s c) -> p s c", c=65)[:, :, 64:65], 1.0)
        v.memset(cvtok[:].rearrange("p (s c) -> p s c", c=65)[:, :, 0:64], 0.0)
        v.memset(svtok[:].rearrange("p (s c) -> p s c", c=65)[:, :, 0:64], 0.0)
        v.memset(wvtok[:].rearrange("p (s c) -> p s c", c=65)[:, :, 64:65], 1.0)
        v.memset(svtok[:].rearrange("p (s c) -> p s c", c=65)[:, :, 64:65], 1.0)

    bar()
    b.pop()

    if BUILD_UPTO >= 1:
        b.push()
        xstg = b.sb("xstg", [P, 4 * D], F32)
        xb = b.sb("xb", [P, 4 * D], BF16)
        xTg = b.sb("xTg", [P, 2 * 8 * GT], BF16)
        qstg = b.sb("qstg", [P, 4 * GT], BF16)
        kstg = b.sb("kstg", [P, 4 * GT], BF16)
        vstg = b.sb("vstg", [P, 4 * GT], BF16)
        qnstg = b.sb("qnstg", [2, 4 * GT], BF16)
        qsqs = b.sb("qsqs", [P, 2 * GT], BF16)
        wqkv_sb = b.sb("wqkv_sb", [P, 8 * 1536], BF16)
        wq_st = b.sb("wq_st", [P, 2 * 1536], F32)

        ptr = b.ps("ptr", [P, 4 * 1024], BF16)
        pmm0 = b.ps("pmm0", [P, GT], F32)
        pmm1 = b.ps("pmm1", [P, GT], F32)
        pqn0 = b.ps("pqn0", [2, GT], F32)
        pqn1 = b.ps("pqn1", [2, GT], F32)

        W0 = N['W']; V0 = N['V']; T0 = N['T']; V20 = N['V2']; V30 = N['V3']
        S0 = N['S']; S20 = N['S2']; Q0 = N['Q']
        PH1_TE_ORDER = [('c', 0), ('c', 1), ('c', 2), ('q', 0), ('c', 3), ('q', 1),
                        ('c', 4), ('q', 2), ('c', 5), ('q', 3), ('c', 6), ('c', 7),
                        ('c', 8), ('c', 9), ('c', 10), ('c', 11)]
        _chain_pos = {oc: 33 + i for i, (k, oc) in enumerate(PH1_TE_ORDER) if k == 'c'}
        _qn_pos = {oc: 33 + i for i, (k, oc) in enumerate(PH1_TE_ORDER) if k == 'q'}

        def TPc(g, oc):
            return T0 + g * 48 + _chain_pos[oc]

        def TPq(g, oc):
            return T0 + g * 48 + _qn_pos[oc]

        @blk.sync
        def _(sync):
            for c in range(8):
                if c >= 2:
                    sync.wait_ge(sV, V0 + (c - 2) + 1)
                sync.dma_start(out=wq_st[:, (c % 2) * 1536:(c % 2 + 1) * 1536],
                               in_=wqkv_d[c * P:(c + 1) * P, :]).then_inc(sWQ0 if c % 2 == 0 else sWQ1, 16)
            xsems = [sX0, sX1, sX2, sX3]
            for g in range(NG):
                for tt in range(4):
                    i = 4 * g + tt
                    if i >= 4:
                        sync.wait_ge(sV, V0 + 8 + (i - 4) + 1)
                    sync.dma_start(out=xstg[:, (i % 4) * D:(i % 4 + 1) * D],
                                   in_=x_d[i * P:(i + 1) * P, :]).then_inc(xsems[i % 4], 16)

        @blk.vector
        def _(v):
            for c in range(8):
                v.wait_ge(sWQ0 if c % 2 == 0 else sWQ1, 16 * (c // 2 + 1))
                v.tensor_copy(wqkv_sb[:, c * 1536:(c + 1) * 1536],
                              wq_st[:, (c % 2) * 1536:(c % 2 + 1) * 1536]).then_inc(sV, 1)
            xsems = [sX0, sX1, sX2, sX3]
            for g in range(NG):
                gs = g % 2
                for tt in range(4):
                    i = 4 * g + tt
                    v.wait_ge(xsems[i % 4], 16 * (i // 4 + 1))
                    if i >= 4:
                        gp, tp_ = divmod(i - 4, 4)
                        v.wait_ge(sT, T0 + gp * 48 + tp_ * 8 + 8)
                    v.tensor_copy(xb[:, (i % 4) * D:(i % 4 + 1) * D],
                                  xstg[:, (i % 4) * D:(i % 4 + 1) * D]).then_inc(sV, 1)
                for m in range(8):
                    v.wait_ge(sT, T0 + g * 48 + 4 * (m + 1))
                    if g >= 2:
                        v.wait_ge(sT, TPc(g - 2, 11))
                    tt0 = (4 * m) // 8
                    dc0 = (4 * m) % 8
                    src = ptr[:].rearrange("p (s q) -> p s q", q=1024)[:, :, 0:P]
                    dst = xTg[:].rearrange("p (s q) -> p s q", q=P)[
                        :, gs * 32 + dc0 * 4 + tt0: gs * 32 + (dc0 + 3) * 4 + tt0 + 1: 4, :]
                    v.tensor_copy(dst, src).then_inc(sV2, 1)
        N['V'] = V0 + 8 + 64

        @blk.tensor
        def _(t):
            for g in range(NG):
                gs = g % 2
                for tt in range(4):
                    i = 4 * g + tt
                    t.wait_ge(sV, V0 + 8 + i + 1)
                    for dc in range(8):
                        k = tt * 8 + dc
                        j = g * 32 + k
                        slot = j % 4
                        if j >= 4:
                            t.wait_ge(sV2, V20 + (j - 4) // 4 + 1)
                        t.transpose(ptr[:, slot * 1024:slot * 1024 + P],
                                    xb[:, (i % 4) * D + dc * P:(i % 4) * D + (dc + 1) * P],
                                    eye[:]).then_inc(sT, 1)
                t.wait_ge(sV2, V20 + g * 8 + 8)
                if g == 0:
                    t.wait_ge(sV, V0 + 8)
                for item in PH1_TE_ORDER:
                    kind, oc = item
                    if kind == 'c':
                        c_g = g * 12 + oc
                        if c_g >= 2:
                            pg, poc = divmod(c_g - 2, 12)
                            t.wait_ge(sS, S0 + pg * 12 + poc + 1)
                            if poc < 4:
                                t.wait_ge(sV3, V30 + pg * 4 + poc + 1)
                        pm = pmm0 if (oc % 2 == 0) else pmm1
                        for dc in range(8):
                            mi = t.matmul(pm[:],
                                          wqkv_sb[:, dc * 1536 + oc * P:dc * 1536 + (oc + 1) * P],
                                          xTg[:, gs * 8 * GT + dc * GT:(gs * 8 + dc + 1) * GT],
                                          start=(dc == 0), stop=(dc == 7))
                        mi.then_inc(sT, 1)
                    else:
                        qn_g = g * 4 + oc
                        t.wait_ge(sV3, V30 + qn_g + 1)
                        if qn_g >= 2:
                            t.wait_ge(sS2, S20 + (qn_g - 2) + 1)
                        pq = pqn0 if (oc % 2 == 0) else pqn1
                        t.matmul(pq[:], hsel[:], qsqs[:, (oc % 2) * GT:(oc % 2 + 1) * GT],
                                 start=True, stop=True).then_inc(sT, 1)
        N['T'] = T0 + NG * 48

        N['V2'] = V20 + NG * 8

        PH1_SC_ORDER = [('o', 0), ('o', 1), ('o', 2), ('n', 0), ('o', 3), ('n', 1),
                        ('o', 4), ('n', 2), ('o', 5), ('n', 3), ('o', 6), ('o', 7),
                        ('o', 8), ('o', 9), ('o', 10), ('o', 11)]

        @blk.scalar
        def _(sc):
            nS = S0
            nS2 = S20
            for g in range(NG):
                if g >= 1:
                    sc.wait_ge(sQ, Q0 + 16 * 32 * g)
                for kind, oc in PH1_SC_ORDER:
                    if kind == 'o':
                        sc.wait_ge(sT, TPc(g, oc))
                        pm = pmm0 if (oc % 2 == 0) else pmm1
                        if oc < 4:
                            sc.activation(qstg[:, oc * GT:(oc + 1) * GT], pm[:], AF.Copy,
                                          scale=0.125,
                                          accum_out=gsum[:, oc * NG + g:oc * NG + g + 1]
                                          ).then_inc(sS, 1)
                            nS += 1
                            qn_g = g * 4 + oc
                            if qn_g >= 2:
                                gp2, op2 = divmod(qn_g - 2, 4)
                                sc.wait_ge(sT, TPq(gp2, op2))
                            sc.activation(qsqs[:, (oc % 2) * GT:(oc % 2 + 1) * GT],
                                          pm[:], AF.Square).then_inc(sV3, 1)
                            sc.wait_ge(sS, nS)
                            sc.dma_start(out=qbuf[0:64, (2 * oc) * S + g * GT:(2 * oc) * S + (g + 1) * GT],
                                         in_=qstg[0:64, oc * GT:(oc + 1) * GT]).then_inc(sQ, 16)
                            sc.dma_start(out=qbuf[0:64, (2 * oc + 1) * S + g * GT:(2 * oc + 1) * S + (g + 1) * GT],
                                         in_=qstg[64:128, oc * GT:(oc + 1) * GT]).then_inc(sQ, 16)
                        else:
                            kv = oc - 4
                            stg, buf = (kstg, kbuf) if kv < 4 else (vstg, vbuf)
                            h2 = kv % 4
                            sc.activation(stg[:, h2 * GT:(h2 + 1) * GT], pm[:], AF.Copy).then_inc(sS, 1)
                            nS += 1
                            sc.wait_ge(sS, nS)
                            sc.dma_start(out=buf[0:64, (2 * h2) * S + g * GT:(2 * h2) * S + (g + 1) * GT],
                                         in_=stg[0:64, h2 * GT:(h2 + 1) * GT]).then_inc(sQ, 16)
                            sc.dma_start(out=buf[0:64, (2 * h2 + 1) * S + g * GT:(2 * h2 + 1) * S + (g + 1) * GT],
                                         in_=stg[64:128, h2 * GT:(h2 + 1) * GT]).then_inc(sQ, 16)
                    else:
                        oc_ = oc
                        sc.wait_ge(sT, TPq(g, oc_))
                        pq = pqn0 if (oc_ % 2 == 0) else pqn1
                        sc.activation(qnstg[:, oc_ * GT:(oc_ + 1) * GT], pq[:], AF.Sqrt).then_inc(sS2, 1)
                        nS2 += 1
                        sc.wait_ge(sS2, nS2)
                        sc.dma_start(out=qbuf[64:65, (2 * oc_) * S + g * GT:(2 * oc_) * S + (g + 1) * GT],
                                     in_=qnstg[0:1, oc_ * GT:(oc_ + 1) * GT]).then_inc(sQ, 16)
                        sc.dma_start(out=qbuf[64:65, (2 * oc_ + 1) * S + g * GT:(2 * oc_ + 1) * S + (g + 1) * GT],
                                     in_=qnstg[1:2, oc_ * GT:(oc_ + 1) * GT]).then_inc(sQ, 16)
        N['S'] = S0 + NG * 12
        N['S2'] = S20 + NG * 4
        N['V3'] = V30 + NG * 4
        N['Q'] = Q0 + 16 * NG * 32
        DMAT.update(sQ=N['Q'], sX0=16 * 16, sX1=16 * 16, sX2=16 * 16, sX3=16 * 16,
                    sWQ0=16 * 4, sWQ1=16 * 4, sW=N['W'])

        bar()
        b.pop()

        b.push()
        pxa = b.ps("pxa", [P, 512], F32)

        @blk.vector
        def _(v):
            v.tensor_reduce(gsum4[:], gsum[:].rearrange("p (o g) -> p o g", g=NG),
                            AX, OP.add).then_inc(sV, 1)
            v.wait_ge(sV, N['V'] + 1)
            for oc in range(4):
                for j in range(2):
                    ci = v.tensor_copy(qmean8[:, 2 * oc + j:2 * oc + j + 1],
                                       gsum4[j * 64:(j + 1) * 64, oc:oc + 1])
            ci.then_inc(sV, 1)
        N['V'] += 2

        @blk.tensor
        def _(t):
            t.wait_ge(sV, N['V'])
            t.matmul(pxa[0:NH, 0:3], qmean8[:], swg[:], start=True, stop=True).then_inc(sT, 1)
        N['T'] += 1

        @blk.vector
        def _(v):
            v.wait_ge(sT, N['T'])
            v.scalar_tensor_tensor(gates_s[:], pxa[0:NH, 0:3], 1.0 / 1024.0, sbg8[:],
                                   OP.mult, OP.add).then_inc(sV, 1)
        N['V'] += 1

        @blk.scalar
        def _(sc):
            sc.wait_ge(sV, N['V'])
            sc.activation(gates_s[:], gates_s[:], AF.Sigmoid).then_inc(sS, 1)
        N['S'] += 1

        @blk.vector
        def _(v):
            v.wait_ge(sS, N['S'])
            v.tensor_copy(gates_sb[:], gates_s[:]).then_inc(sV, 1)
        N['V'] += 1

        @blk.gpsimd
        def _(gp):
            gp.wait_ge(sV, N['V'])
            gp.dma_start(out=gtmp[:], in_=gates_sb[:]).then_inc(sG, 16)
            gp.wait_ge(sG, N['G'] + 16)
            gp.dma_start(out=grow[0:1, :],
                         in_=gtmp[:].rearrange("p q -> (p q)").unsqueeze(0)).then_inc(sG, 16)
        N['G'] += 32

        @blk.tensor
        def _(t):
            t.wait_ge(sG, N['G'])
            t.matmul(pxa[0:HD, 8:32], ones_row[0:1, 0:HD], grow[:],
                     start=True, stop=True).then_inc(sT, 1)
        N['T'] += 1

        @blk.scalar
        def _(sc):
            sc.wait_ge(sT, N['T'])
            sc.activation(gcol[:], pxa[0:HD, 8:32], AF.Copy).then_inc(sS, 1)
        N['S'] += 1

        bar()
        b.pop()

    if BUILD_UPTO >= 2:
        b.push()
        kring = b.sb("kring", [HD, 2 * S], BF16)
        vring = b.sb("vring", [HD, 2 * S], BF16)
        cvst = b.sb("cvst", [HD, 512], BF16)
        pck = b.ps("pck", [HD, NBLK], F32)
        pcv = b.ps("pcv", [HD, NBLK], F32)
        ptp0 = b.ps("ptp0", [P, 1024], BF16)
        ptp1 = b.ps("ptp1", [P, 1024], BF16)

        W0 = N['W']; T0 = N['T']; V0 = N['V']; S0 = N['S']

        def vpos_tp_evac(gt):
            return V0 + (gt // 8) * 11 + 3 + (gt % 8) + 1

        @blk.sync
        def _(sync):
            sync.wait_ge(sQ, N['Q'])
            for h in range(NH):
                if h >= 2:
                    sync.wait_ge(sT, T0 + (h - 1) * 10)
                    sync.wait_ge(sV, V0 + (h - 1) * 11)
                sl = (h % 2) * S
                sk = sK0 if h % 2 == 0 else sK1
                sync.dma_start(out=kring[:, sl:sl + S], in_=kbuf[:, h * S:(h + 1) * S]).then_inc(sk, 16)
                sync.dma_start(out=vring[:, sl:sl + S], in_=vbuf[:, h * S:(h + 1) * S]).then_inc(sk, 16)

        @blk.tensor
        def _(t):
            for h in range(NH):
                sl = (h % 2) * S
                t.wait_ge(sK0 if h % 2 == 0 else sK1, 32 * (h // 2 + 1))
                if h >= 1:
                    t.wait_ge(sS, S0 + (h - 1) * 2 + 1)
                for j in range(32):
                    mi = t.matmul(pck[:],
                                  wckv_sb[0:HD, j * HD:(j + 1) * HD],
                                  kring[:, sl + j:sl + j + 16 * (NBLK - 1) + 1:16],
                                  start=(j == 0), stop=(j == 31))
                mi.then_inc(sT, 1)
                if h >= 1:
                    t.wait_ge(sS, S0 + (h - 1) * 2 + 2)
                for j in range(32):
                    mi = t.matmul(pcv[:],
                                  wckv_sb[0:HD, 2048 + j * HD:2048 + (j + 1) * HD],
                                  vring[:, sl + j:sl + j + 16 * (NBLK - 1) + 1:16],
                                  start=(j == 0), stop=(j == 31))
                mi.then_inc(sT, 1)
                for ch in range(4):
                    gt = h * 8 + ch
                    if gt >= 2:
                        t.wait_ge(sV, vpos_tp_evac(gt - 2))
                    pt = ptp0 if gt % 2 == 0 else ptp1
                    t.transpose(pt[:, 0:HD],
                                vring[0:HD, sl + S - WIN + ch * P:sl + S - WIN + (ch + 1) * P],
                                eye[0:HD, 0:HD]).then_inc(sT, 1)
                t.wait_ge(sS, S0 + h * 2 + 2)
                for ch in range(4):
                    gt = h * 8 + 4 + ch
                    ln = min(P, NBLK - ch * P)
                    if gt >= 2:
                        t.wait_ge(sV, vpos_tp_evac(gt - 2))
                    pt = ptp0 if gt % 2 == 0 else ptp1
                    t.transpose(pt[0:ln, 0:HD], cvst[0:HD, ch * P:ch * P + ln],
                                eye[0:HD, 0:HD]).then_inc(sT, 1)
        N['T'] = T0 + NH * 10

        @blk.scalar
        def _(sc):
            for h in range(NH):
                sc.wait_ge(sT, T0 + h * 10 + 1)
                dst = ckTaE if h % 2 == 0 else ckTaO
                sc.activation(dst[0:HD, (h // 2) * NBLK:(h // 2 + 1) * NBLK], pck[:],
                              AF.Identity, bias=sbckv[:, 0:1]).then_inc(sS, 1)
                sc.wait_ge(sT, T0 + h * 10 + 2)
                if h >= 1:
                    sc.wait_ge(sT, T0 + (h - 1) * 10 + 10)
                sc.activation(cvst[0:HD, 0:NBLK], pcv[:], AF.Identity,
                              bias=sbckv[:, 1:2]).then_inc(sS, 1)
        N['S'] = S0 + NH * 2

        @blk.vector
        def _(v):
            for h in range(NH):
                sl = (h % 2) * S
                v.wait_ge(sK0 if h % 2 == 0 else sK1, 32 * (h // 2 + 1))
                dstw = kwinE if h % 2 == 0 else kwinO
                v.tensor_copy(dstw[0:HD, (h // 2) * WIN:(h // 2 + 1) * WIN],
                              kring[:, sl + S - WIN:sl + S]).then_inc(sV, 1)
                v.reduce_max(kmax[:, h:h + 1],
                             kring[:, sl:sl + S].unsqueeze(1), AX,
                             apply_absolute_value=True).then_inc(sV, 1)
                v.wait_ge(sS, S0 + h * 2 + 1)
                dst = ckTaE if h % 2 == 0 else ckTaO
                v.reduce_max(ckmax[:, h:h + 1],
                             dst[0:HD, (h // 2) * NBLK:(h // 2 + 1) * NBLK].unsqueeze(1),
                             AX, apply_absolute_value=True).then_inc(sV, 1)
                for ch in range(4):
                    v.wait_ge(sT, T0 + h * 10 + 2 + ch + 1)
                    pt = ptp0 if (h * 8 + ch) % 2 == 0 else ptp1
                    v.tensor_copy(wvtok[:, (h * 4 + ch) * 65:(h * 4 + ch) * 65 + HD],
                                  pt[:, 0:HD]).then_inc(sV, 1)
                for ch in range(4):
                    ln = min(P, NBLK - ch * P)
                    v.wait_ge(sT, T0 + h * 10 + 6 + ch + 1)
                    pt = ptp0 if (h * 8 + 4 + ch) % 2 == 0 else ptp1
                    v.tensor_copy(cvtok[0:ln, (h * 4 + ch) * 65:(h * 4 + ch) * 65 + HD],
                                  pt[0:ln, 0:HD]).then_inc(sV, 1)
        N['V'] = V0 + NH * 11
        DMAT.update(sK0=32 * 4, sK1=32 * 4)

        bar()
        b.pop()

        b.push()
        pxc = b.ps("pxc", [NH, HD], BF16)

        @blk.vector
        def _(v):
            v.tensor_tensor(kmax[:, 0:NH], kmax[:, 0:NH], ckmax[:, 0:NH], OP.max).then_inc(sV, 1)
            v.wait_ge(sV, N['V'] + 1)
            v.tensor_copy(kmax_b[:], kmax[:, 0:NH]).then_inc(sV, 1)
        N['V'] += 2

        @blk.tensor
        def _(t):
            t.wait_ge(sV, N['V'])
            t.transpose(pxc[:], kmax_b[:], eye[0:HD, 0:HD]).then_inc(sT, 1)
        N['T'] += 1

        @blk.vector
        def _(v):
            v.wait_ge(sT, N['T'])
            v.reduce_max(mkst[:, 0:1], pxc[:].unsqueeze(1), AX).then_inc(sV, 1)
        N['V'] += 1

        @blk.gpsimd
        def _(gp):
            gp.wait_ge(sV, N['V'])
            gp.dma_start(out=mtmp[:, 0:1], in_=mkst[:, 0:1], allow_slow_non_contiguous=True).then_inc(sG, 16)
            gp.wait_ge(sG, N['G'] + 16)
            gp.dma_start(out=mksr[0:1, :],
                         in_=mtmp[:, 0:1].rearrange("p q -> (p q)").unsqueeze(0),
                         allow_slow_non_contiguous=True).then_inc(sG, 16)
        N['G'] += 32

        @blk.vector
        def _(v):
            v.wait_ge(sG, N['G'])
            v.tensor_scalar_mul(negmk[:], mksr[:], -0.125).then_inc(sV, 1)
        N['V'] += 1

        @blk.scalar
        def _(sc):
            sc.wait_ge(sV, N['V'])
            for h in range(NH):
                dst = ckTaE if h % 2 == 0 else ckTaO
                sc.activation(dst[64:65, (h // 2) * NBLK:(h // 2 + 1) * NBLK],
                              ones1k[0:1, 0:NBLK], AF.Copy, scale=negmk[0:1, h:h + 1])
                dstw = kwinE if h % 2 == 0 else kwinO
                ai = sc.activation(dstw[64:65, (h // 2) * WIN:(h // 2 + 1) * WIN],
                                   ones1k[0:1, 0:WIN], AF.Copy, scale=negmk[0:1, h:h + 1])
            ai.then_inc(sS, 1)
        N['S'] += 1

        bar()
        b.pop()


    if BUILD_UPTO >= 3:
        b.push()
        qtbuf = b.sb("qtbuf", [65, 2 * GT], BF16)
        PT = b.sb("PT", [P, 2 * 4 * GT], BF16)
        rz = b.sb("rz", [1, 2 * GT], F32)
        rzb = b.sb("rzb", [1, 2 * GT], BF16)
        sbc = b.sb("sbc", [P, 2 * GT], BF16)
        cmpn = b.sb("cmpn", [HD, 2 * GT], BF16)
        scr = b.sb("scr", [P, 2 * GT], BF16)
        biacc = b.sb("biacc", [P, 64], F32)
        bislot = bislot_ref = None
        bislot = b.sb("bislot3", [P, 64], F32)
        psc0 = b.ps("psc0", [P, GT], F32)
        psc1 = b.ps("psc1", [P, GT], F32)
        pav0 = b.ps("pav0", [65, GT], F32)
        pav1 = b.ps("pav1", [65, GT], F32)
        pbc0 = b.ps("pbc0", [P, GT], F32)
        pbc1 = b.ps("pbc1", [P, GT], F32)

        W0 = N['W']; T0 = N['T']; V0 = N['V']; S0 = N['S']; G0 = N['G']; Q0 = N['Q']
        NIT = NH * 16
        sQT0 = b.sem("sQT0")
        sQT1 = b.sem("sQT1")
        sGC0 = b.sem("sGC0")
        sGC1 = b.sem("sGC1")

        def Tpos(i, j):
            return T0 + i * 6 + j

        def Spos(i, j):
            return S0 + i * 5 + j

        def Vpos(i, j):
            h, qt = divmod(i, 16)
            return V0 + 1 + h * 113 + qt * 7 + j

        def Vheadadd(h):
            return V0 + 1 + h * 113 + 113

        @blk.sync
        def _(sync):
            sync.wait_ge(sQ, N['Q'])
            for i in range(NIT):
                h, qt = divmod(i, 16)
                sq = sQT0 if i % 2 == 0 else sQT1
                if i >= 2:
                    sync.wait_ge(sT, Tpos(i - 2, 4))
                sync.dma_start(out=qtbuf[:, (i % 2) * GT:(i % 2 + 1) * GT],
                               in_=qbuf[0:65, h * S + qt * GT:h * S + (qt + 1) * GT]
                               ).then_inc(sq, 16)

        @blk.tensor
        def _(t):
            for i in range(NIT):
                h, qt = divmod(i, 16)
                ck = ckTaE if h % 2 == 0 else ckTaO
                cb = (h // 2) * NBLK
                sq = sQT0 if i % 2 == 0 else sQT1
                t.wait_ge(sq, 16 * (i // 2 + 1))
                for ch in range(4):
                    ln = min(P, NBLK - ch * P)
                    ps = psc0 if ch % 2 == 0 else psc1
                    if i > 0 or ch >= 2:
                        pi, pch = (i, ch - 2) if ch >= 2 else (i - 1, ch + 2)
                        t.wait_ge(sS, Spos(pi, pch + 1))
                    t.matmul(ps[0:ln, :], ck[0:65, cb + ch * P:cb + ch * P + ln],
                             qtbuf[:, (i % 2) * GT:(i % 2 + 1) * GT],
                             start=True, stop=True).then_inc(sT, 1)
                pav = pav0 if i % 2 == 0 else pav1
                t.wait_ge(sS, Spos(i, 4))
                if i >= 2:
                    t.wait_ge(sV, Vpos(i - 2, 3))
                for ch in range(4):
                    ln = min(P, NBLK - ch * P)
                    mi = t.matmul(pav[:], cvtok[0:ln, (h * 4 + ch) * 65:(h * 4 + ch + 1) * 65],
                                  PT[0:ln, (i % 2) * 4 * GT + ch * GT:(i % 2) * 4 * GT + (ch + 1) * GT],
                                  start=(ch == 0), stop=(ch == 3))
                mi.then_inc(sT, 1)
                pbc = pbc0 if i % 2 == 0 else pbc1
                t.wait_ge(sV, Vpos(i, 2))
                if i >= 2:
                    t.wait_ge(sS, Spos(i - 2, 5))
                t.matmul(pbc[:], ones_row[0:1, 0:P],
                         rzb[0:1, (i % 2) * GT:(i % 2 + 1) * GT],
                         start=True, stop=True).then_inc(sT, 1)
        N['T'] = T0 + NIT * 6

        @blk.scalar
        def _(sc):
            for i in range(NIT):
                for ch in range(4):
                    ln = min(P, NBLK - ch * P)
                    ps = psc0 if ch % 2 == 0 else psc1
                    sc.wait_ge(sT, Tpos(i, ch + 1))
                    if i >= 2:
                        sc.wait_ge(sT, Tpos(i - 2, 5))
                        sc.wait_ge(sV, Vpos(i - 2, 7))
                    sc.activation(PT[0:ln, (i % 2) * 4 * GT + ch * GT:(i % 2) * 4 * GT + (ch + 1) * GT],
                                  ps[0:ln, :], AF.Exp).then_inc(sS, 1)
                pbc = pbc0 if i % 2 == 0 else pbc1
                sc.wait_ge(sT, Tpos(i, 6))
                if i >= 2:
                    sc.wait_ge(sV, Vpos(i - 2, 7))
                sc.activation(sbc[:, (i % 2) * GT:(i % 2 + 1) * GT], pbc[:],
                              AF.Copy).then_inc(sS, 1)
        N['S'] = S0 + NIT * 5

        @blk.vector
        def _(v):
            v.memset(bislot[:], 0.0)
            v.memset(biacc[:], 0.0).then_inc(sV, 1)
            for i in range(NIT):
                h, qt = divmod(i, 16)
                pav = pav0 if i % 2 == 0 else pav1
                v.wait_ge(sT, Tpos(i, 5))
                if i >= 2:
                    v.wait_ge(sT, Tpos(i - 2, 6))
                v.reciprocal(rz[0:1, (i % 2) * GT:(i % 2 + 1) * GT],
                             pav[64:65, :]).then_inc(sV, 1)
                v.wait_ge(sV, Vpos(i, 1))
                v.tensor_copy(rzb[0:1, (i % 2) * GT:(i % 2 + 1) * GT],
                              rz[0:1, (i % 2) * GT:(i % 2 + 1) * GT]).then_inc(sV, 1)
                v.wait_ge(sS, Spos(i, 5))
                if i >= 2:
                    v.wait_ge(sGC0 if i % 2 == 0 else sGC1, 16 * ((i - 2) // 2 + 1))
                v.tensor_tensor(cmpn[:, (i % 2) * GT:(i % 2 + 1) * GT],
                                pav[0:HD, :], sbc[0:HD, (i % 2) * GT:(i % 2 + 1) * GT],
                                OP.mult).then_inc(sV, 1)
                if i == 0:
                    v.wait_ge(sV, V0 + 1)
                for ch in range(4):
                    ln = min(P, NBLK - ch * P)
                    if ch >= 1:
                        v.wait_ge(sV, Vpos(i, 3 + ch))
                    v.scalar_tensor_tensor(
                        scr[0:ln, (i % 2) * GT:(i % 2 + 1) * GT],
                        PT[0:ln, (i % 2) * 4 * GT + ch * GT:(i % 2) * 4 * GT + (ch + 1) * GT],
                        1.0,
                        sbc[0:ln, (i % 2) * GT:(i % 2 + 1) * GT],
                        OP.mult, OP.mult,
                        accum_out=bislot[0:ln, ch * 16 + qt:ch * 16 + qt + 1]).then_inc(sV, 1)
                if qt == 15:
                    v.wait_ge(sV, Vpos(i, 7))
                    v.tensor_tensor(biacc[:], biacc[:], bislot[:], OP.add).then_inc(sV, 1)
        N['V'] = V0 + 1 + NH * 113

        @blk.gpsimd
        def _(gp):
            for i in range(NIT):
                h, qt = divmod(i, 16)
                gp.wait_ge(sV, Vpos(i, 3))
                gp.dma_start(out=compbuf[0:HD, h * S + qt * GT:h * S + (qt + 1) * GT],
                             in_=cmpn[:, (i % 2) * GT:(i % 2 + 1) * GT]
                             ).then_inc(sGC0 if i % 2 == 0 else sGC1, 16)
        DMAT.update(sGC0=16 * (NIT // 2), sGC1=16 * (NIT // 2),
                    sQT0=16 * (NIT // 2), sQT1=16 * (NIT // 2))

        bar()
        b.pop()

        b.push()
        m8a = b.sb("m8a", [1, 8], F32)
        m8b = b.sb("m8b", [1, 8], F32)
        i8 = b.sb("i8", [1, 16], U32)
        idxf = b.sb("idxf", [1, 16], F32)
        t16 = b.sb("t16", [1, 16], F32)
        msk = b.sb("msk", [1, 512], F32)
        selposi = b.sb("selposi", [1, NSEL], I32)
        bid = nc.dram_tensor("bid", [P, 4], F32).ap()


        @blk.vector
        def _(v):
            v.tensor_reduce(bi4[:], biacc[:].rearrange("p (c q) -> p c q", q=16),
                            AX, OP.add).then_inc(sV, 1)
        N['V'] += 1

        @blk.gpsimd
        def _(gp):
            gp.wait_ge(sV, N['V'])
            gp.dma_start(out=bid[:], in_=bi4[:]).then_inc(sG, 16)
            gp.wait_ge(sG, N['G'] + 16)
            gp.dma_start(out=bi_row[:].rearrange("o (a p) -> o a p", a=4),
                         in_=bid[:].transpose([1, 0]).unsqueeze(0),
                         allow_slow_non_contiguous=True).then_inc(sG, 16)
        N['G'] += 32

        @blk.vector
        def _(v):
            v.wait_ge(sG, N['G'])
            v.memset(bi_row[0:1, NBLK:512], 0.0).then_inc(sV, 1)
        N['V'] += 1

        @blk.gpsimd
        def _(gp):
            gp.wait_ge(sV, N['V'])
            gp.dma_start(out=bi_in[:], in_=bi_row[:]).then_inc(sG, 16)
            gp.wait_ge(sG, N['G'] + 16)
            gp.collective_compute("AllReduce", OP.add, replica_groups=PAIRS,
                                  ins=[bi_in.opt()], outs=[bi_out.opt()]).then_inc(sC, 1)
            gp.wait_ge(sC, N['C'] + 1)
            gp.dma_start(out=bi_row2[:], in_=bi_out[:]).then_inc(sG, 16)
        N['G'] += 32
        N['C'] += 1

        @blk.vector
        def _(v):
            base = N['V']
            v.wait_ge(sG, N['G'])
            v.max(m8a[:], bi_row2[0:1, 0:NBLK]).then_inc(sV, 1)
            v.wait_ge(sV, base + 1)
            v.max_index(i8[0:1, 0:8], m8a[:], bi_row2[0:1, 0:NBLK]).then_inc(sV, 1)
            v.wait_ge(sV, base + 2)
            v.tensor_scalar(msk[0:1, 0:NBLK], bi_row2[0:1, 0:NBLK], m8a[0:1, 7:8],
                            -1e30, OP.is_ge, OP.mult).then_inc(sV, 1)
            v.wait_ge(sV, base + 3)
            v.tensor_tensor(msk[0:1, 0:NBLK], msk[0:1, 0:NBLK], bi_row2[0:1, 0:NBLK],
                            OP.add).then_inc(sV, 1)
            v.wait_ge(sV, base + 4)
            v.max(m8b[:], msk[0:1, 0:NBLK]).then_inc(sV, 1)
            v.wait_ge(sV, base + 5)
            v.max_index(i8[0:1, 8:16], m8b[:], msk[0:1, 0:NBLK]).then_inc(sV, 1)
            v.wait_ge(sV, base + 6)
            v.tensor_copy(idxf[:], i8[:]).then_inc(sV, 1)
            v.wait_ge(sV, base + 7)
            v.tensor_scalar_mul(t16[:], idxf[:], 16.0).then_inc(sV, 1)
            v.wait_ge(sV, base + 8)
            v.tensor_tensor(selpos[:].rearrange("o (a r) -> o a r", r=HD),
                            t16[:].unsqueeze(2).broadcast_to([1, 16, HD]),
                            sar64[:].unsqueeze(1).broadcast_to([1, 16, HD]),
                            OP.add).then_inc(sV, 1)
            v.wait_ge(sV, base + 9)
            v.tensor_scalar_min(selpos[:], selpos[:], float(S - 1)).then_inc(sV, 1)
            v.wait_ge(sV, base + 10)
            v.tensor_copy(selposi[:], selpos[:]).then_inc(sV, 1)
        N['V'] += 11

        @blk.gpsimd
        def _(gp):
            gp.wait_ge(sV, N['V'])
            for c in range(8):
                gp.dma_start(out=selidx[:, c:c + 1],
                             in_=selposi[0:1, c * P:(c + 1) * P].rearrange("o (p q) -> o p q", q=1),
                             allow_slow_non_contiguous=True).then_inc(sG, 16)
        N['G'] += 16 * 8

        bar()
        b.pop()


    if BUILD_UPTO >= 4:
        b.push()
        xss = b.sb("xss", [P, 2 * D], F32)
        xsb = b.sb("xsb", [P, 2 * D], BF16)
        xselT = b.sb("xselT", [P, 8 * NSEL], BF16)
        wkv_sb = b.sb("wkv_sb", [P, 8 * NSEL], BF16)
        wkv_st = b.sb("wkv_st", [P, 2 * NSEL], F32)
        vst4 = b.sb("vst4", [HD, 4 * GT], BF16)
        ptr4 = b.ps("ptr4", [P, 4 * 1024], BF16)
        pk40 = b.ps("pk40", [P, GT], F32)
        pk41 = b.ps("pk41", [P, GT], F32)
        ptv40 = b.ps("ptv40", [P, 1024], BF16)
        ptv41 = b.ps("ptv41", [P, 1024], BF16)
        sW40 = b.sem("sW40")
        sW41 = b.sem("sW41")
        sGX0 = b.sem("sGX0")
        sGX1 = b.sem("sGX1")
        T0 = N['T']; V0 = N['V']; S0 = N['S']; G0 = N['G']; V20 = N['V2']

        def T4tp(c, dc):
            return T0 + c * 8 + dc + 1

        PH4_TE = []
        for _j in range(10):
            PH4_TE.append(('c', _j))
        for _jv in range(6):
            PH4_TE.append(('v', _jv))
            PH4_TE.append(('c', 10 + _jv))
        PH4_TE.append(('v', 6))
        PH4_TE.append(('v', 7))
        _pos4 = {}
        _p = 64
        for _it in PH4_TE:
            if _it[0] == 'c':
                _p += 1
                _pos4[_it] = _p
            else:
                _pos4[_it] = _p
                _p += 8

        def T4chain(j):
            return T0 + _pos4[('c', j)]

        def T4vtp(jv, k):
            return T0 + _pos4[('v', jv)] + k + 1

        def S4(j):
            return S0 + j + 1

        def V4evac(g):
            return V20 + 16 + g + 1

        @blk.sync
        def _(sync):
            for c in range(8):
                if c >= 2:
                    sync.wait_ge(sV, V0 + (c - 2) + 1)
                sync.dma_start(out=wkv_st[:, (c % 2) * D:(c % 2 + 1) * D],
                               in_=wqkv_d[c * P:(c + 1) * P, 512:1536]
                               ).then_inc(sW40 if c % 2 == 0 else sW41, 16)

        @blk.gpsimd
        def _(gp):
            gp.wait_ge(sG, G0)
            for c in range(8):
                if c >= 2:
                    gp.wait_ge(sV, V0 + 8 + (c - 2) + 1)
                gp.indirect_dma_start(
                    out=xss[:, (c % 2) * D:(c % 2 + 1) * D], out_offset=None,
                    in_=x_d[:], in_offset=IndirectOffsetOnAxis(selidx[:, c:c + 1], 0),
                ).then_inc(sGX0 if c % 2 == 0 else sGX1, 16)

        @blk.vector
        def _(v):
            for c in range(8):
                v.wait_ge(sW40 if c % 2 == 0 else sW41, 16 * (c // 2 + 1))
                v.tensor_copy(wkv_sb[:, c * NSEL:(c + 1) * NSEL],
                              wkv_st[:, (c % 2) * D:(c % 2 + 1) * D]).then_inc(sV, 1)
            for c in range(9):
                if c < 8:
                    v.wait_ge(sGX0 if c % 2 == 0 else sGX1, 16 * (c // 2 + 1))
                    if c >= 2:
                        v.wait_ge(sT, T4tp(c - 2, 7))
                    v.tensor_copy(xsb[:, (c % 2) * D:(c % 2 + 1) * D],
                                  xss[:, (c % 2) * D:(c % 2 + 1) * D]).then_inc(sV, 1)
                if c >= 1:
                    ce = c - 1
                    for m in range(2):
                        v.wait_ge(sT, T4tp(ce, 4 * m + 3))
                        src = ptr4[:].rearrange("p (s q) -> p s q", q=1024)[:, :, 0:P]
                        dst = xselT[:].rearrange("p (dc n) -> p dc n", n=NSEL)[
                            :, 4 * m:4 * m + 4, ce * P:(ce + 1) * P]
                        v.tensor_copy(dst, src).then_inc(sV2, 1)
            for jv in range(8):
                for k in range(8):
                    g = jv * 8 + k
                    hh, ch = divmod(k, 4)
                    ocv, half = divmod(jv, 2)
                    h = 2 * ocv + hh
                    v.wait_ge(sT, T4vtp(jv, k))
                    pt = ptv40 if g % 2 == 0 else ptv41
                    v.tensor_copy(svtok[:, (h * 8 + half * 4 + ch) * 65:
                                        (h * 8 + half * 4 + ch) * 65 + HD],
                                  pt[:, 0:HD]).then_inc(sV2, 1)
        N['V'] = V0 + 16

        @blk.tensor
        def _(t):
            for c in range(8):
                t.wait_ge(sV, V0 + 8 + c + 1)
                for dc in range(8):
                    j = c * 8 + dc
                    if j >= 4:
                        t.wait_ge(sV2, V20 + (j - 4) // 4 + 1)
                    t.transpose(ptr4[:, (j % 4) * 1024:(j % 4) * 1024 + P],
                                xsb[:, (c % 2) * D + dc * P:(c % 2) * D + (dc + 1) * P],
                                eye[:]).then_inc(sT, 1)
            t.wait_ge(sV, V0 + 8)
            t.wait_ge(sV2, V20 + 16)
            for kind, j in PH4_TE:
                if kind == 'c':
                    oc, half = divmod(j, 2)
                    if j >= 2:
                        t.wait_ge(sS, S4(j - 2))
                    pm = pk40 if j % 2 == 0 else pk41
                    for dc in range(8):
                        mi = t.matmul(pm[:],
                                      wkv_sb[:, dc * NSEL + oc * P:dc * NSEL + (oc + 1) * P],
                                      xselT[:, dc * NSEL + half * GT:dc * NSEL + (half + 1) * GT],
                                      start=(dc == 0), stop=(dc == 7))
                    mi.then_inc(sT, 1)
                else:
                    jv = j
                    t.wait_ge(sS, S4(8 + jv))
                    for k in range(8):
                        g = jv * 8 + k
                        hh, ch = divmod(k, 4)
                        if g >= 2:
                            t.wait_ge(sV2, V4evac(g - 2))
                        sp = (jv % 2) * 2 + hh
                        pt = ptv40 if g % 2 == 0 else ptv41
                        t.transpose(pt[:, 0:HD],
                                    vst4[0:HD, sp * GT + ch * P:sp * GT + ch * P + P],
                                    eye[0:HD, 0:HD]).then_inc(sT, 1)
        N['T'] = T0 + 64 + 16 + 64

        @blk.scalar
        def _(sc):
            for j in range(16):
                oc, half = divmod(j, 2)
                sc.wait_ge(sT, T4chain(j))
                pm = pk40 if j % 2 == 0 else pk41
                if oc < 4:
                    sc.activation(kselE[0:HD, oc * NSEL + half * GT:oc * NSEL + (half + 1) * GT],
                                  pm[0:64, :], AF.Copy)
                    sc.activation(kselO[0:HD, oc * NSEL + half * GT:oc * NSEL + (half + 1) * GT],
                                  pm[64:128, :], AF.Copy).then_inc(sS, 1)
                else:
                    jv = (oc - 4) * 2 + half
                    if jv >= 2:
                        sc.wait_ge(sT, T4vtp(jv - 2, 7))
                    sc.activation(vst4[0:HD, ((jv % 2) * 2) * GT:((jv % 2) * 2 + 1) * GT],
                                  pm[0:64, :], AF.Copy)
                    sc.activation(vst4[0:HD, ((jv % 2) * 2 + 1) * GT:((jv % 2) * 2 + 2) * GT],
                                  pm[64:128, :], AF.Copy).then_inc(sS, 1)
            for h in range(NH):
                dst = kselE if h % 2 == 0 else kselO
                ai = sc.activation(dst[64:65, (h // 2) * NSEL:(h // 2 + 1) * NSEL],
                                   ones1k[0:1, 0:NSEL], AF.Copy, scale=negmk[0:1, h:h + 1])
            ai.then_inc(sS, 1)
        N['S'] = S0 + 16 + 1
        N['V2'] = V20 + 16 + 64
        DMAT.update(sW40=16 * 4, sW41=16 * 4, sGX0=16 * 4, sGX1=16 * 4, sG=N['G'])

        bar()
        b.pop()
    if BUILD_UPTO >= 5:
        b.push()
        qtb5 = b.sb("qtb5", [65, 2 * GT], BF16)
        cmpl = b.sb("cmpl", [HD, 2 * GT], BF16)
        PTW = b.sb("PTW", [P, 2 * 4 * GT], BF16)
        PTS = b.sb("PTS", [P, 2 * 8 * GT], BF16)
        rzw = b.sb("rzw", [1, 2 * GT], F32)
        rgw = b.sb("rgw", [1, 2 * GT], F32)
        rbw = b.sb("rbw", [1, 2 * GT], BF16)
        rzs = b.sb("rzs", [1, 2 * GT], F32)
        rgs = b.sb("rgs", [1, 2 * GT], F32)
        rbs = b.sb("rbs", [1, 2 * GT], BF16)
        sbcw = b.sb("sbcw", [HD, 2 * GT], BF16)
        sbcs = b.sb("sbcs", [HD, 2 * GT], BF16)
        t1b = b.sb("t1b", [HD, 2 * GT], BF16)
        w1b = b.sb("w1b", [HD, 2 * GT], BF16)
        a2b = b.sb("a2b", [HD, 2 * GT], BF16)
        s1b = b.sb("s1b", [HD, 2 * GT], BF16)
        attb = b.sb("attb", [HD, 2 * GT], BF16)
        psc0 = b.ps("psc0_5", [P, GT], F32)
        psc1 = b.ps("psc1_5", [P, GT], F32)
        pavw0 = b.ps("pavw0", [65, GT], F32)
        pavw1 = b.ps("pavw1", [65, GT], F32)
        pavs0 = b.ps("pavs0", [65, GT], F32)
        pavs1 = b.ps("pavs1", [65, GT], F32)
        pbcw = b.ps("pbcw", [HD, GT], F32)
        pbcs = b.ps("pbcs", [HD, GT], F32)
        sQU0 = b.sem("sQU0")
        sQU1 = b.sem("sQU1")
        sCL0 = b.sem("sCL0")
        sCL1 = b.sem("sCL1")
        sAT0 = b.sem("sAT0")
        sAT1 = b.sem("sAT1")
        T0 = N['T']; V0 = N['V']; S0 = N['S']; G0 = N['G']
        NIT = NH * 16

        def Tp(i, j):
            return T0 + i * 16 + j

        def Sp(i, j):
            return S0 + i * 14 + j

        def Vp(i, j):
            return V0 + i * 11 + j

        def SposExpU(u):
            return S0 + (u // 12) * 14 + (u % 12) + 1

        @blk.sync
        def _(sync):
            for i in range(NIT):
                h, qt = divmod(i, 16)
                if i >= 2:
                    sync.wait_ge(sT, Tp(i - 2, 12))
                sync.dma_start(out=qtb5[:, (i % 2) * GT:(i % 2 + 1) * GT],
                               in_=qbuf[0:65, h * S + qt * GT:h * S + (qt + 1) * GT]
                               ).then_inc(sQU0 if i % 2 == 0 else sQU1, 16)

        @blk.gpsimd
        def _(gp):
            gp.wait_ge(b._sems["sGC0"], 16 * (NIT // 2))
            gp.wait_ge(b._sems["sGC1"], 16 * (NIT // 2))
            for i in range(NIT):
                h, qt = divmod(i, 16)
                if i >= 2:
                    gp.wait_ge(sV, Vp(i - 2, 7))
                gp.dma_start(out=cmpl[:, (i % 2) * GT:(i % 2 + 1) * GT],
                             in_=compbuf[0:HD, h * S + qt * GT:h * S + (qt + 1) * GT]
                             ).then_inc(sCL0 if i % 2 == 0 else sCL1, 16)
                gp.wait_ge(sV, Vp(i, 11))
                gp.dma_start(out=attfull[h * HD:(h + 1) * HD, qt * GT:(qt + 1) * GT],
                             in_=attb[:, (i % 2) * GT:(i % 2 + 1) * GT]
                             ).then_inc(sAT0 if i % 2 == 0 else sAT1, 16)

        @blk.tensor
        def _(t):
            for i in range(NIT):
                h, qt = divmod(i, 16)
                kw = kwinE if h % 2 == 0 else kwinO
                ks = kselE if h % 2 == 0 else kselO
                t.wait_ge(sQU0 if i % 2 == 0 else sQU1, 16 * (i // 2 + 1))
                for ch in range(4):
                    u = i * 12 + ch
                    if u >= 2:
                        t.wait_ge(sS, SposExpU(u - 2))
                    ps = psc0 if u % 2 == 0 else psc1
                    t.matmul(ps[:], kw[0:65, (h // 2) * WIN + ch * P:(h // 2) * WIN + (ch + 1) * P],
                             qtb5[:, (i % 2) * GT:(i % 2 + 1) * GT],
                             start=True, stop=True).then_inc(sT, 1)
                for ch in range(8):
                    u = i * 12 + 4 + ch
                    if u >= 2:
                        t.wait_ge(sS, SposExpU(u - 2))
                    ps = psc0 if u % 2 == 0 else psc1
                    t.matmul(ps[:], ks[0:65, (h // 2) * NSEL + ch * P:(h // 2) * NSEL + (ch + 1) * P],
                             qtb5[:, (i % 2) * GT:(i % 2 + 1) * GT],
                             start=True, stop=True).then_inc(sT, 1)
                pavw = pavw0 if i % 2 == 0 else pavw1
                t.wait_ge(sS, Sp(i, 4))
                if i >= 2:
                    t.wait_ge(sV, Vp(i - 2, 8))
                for ch in range(4):
                    mi = t.matmul(pavw[:], wvtok[:, (h * 4 + ch) * 65:(h * 4 + ch + 1) * 65],
                                  PTW[:, (i % 2) * 4 * GT + ch * GT:(i % 2) * 4 * GT + (ch + 1) * GT],
                                  start=(ch == 0), stop=(ch == 3))
                mi.then_inc(sT, 1)
                pavs = pavs0 if i % 2 == 0 else pavs1
                t.wait_ge(sS, Sp(i, 12))
                if i >= 2:
                    t.wait_ge(sV, Vp(i - 2, 10))
                for ch in range(8):
                    mi = t.matmul(pavs[:], svtok[:, (h * 8 + ch) * 65:(h * 8 + ch + 1) * 65],
                                  PTS[:, (i % 2) * 8 * GT + ch * GT:(i % 2) * 8 * GT + (ch + 1) * GT],
                                  start=(ch == 0), stop=(ch == 7))
                mi.then_inc(sT, 1)
                t.wait_ge(sV, Vp(i, 3))
                if i >= 1:
                    t.wait_ge(sS, Sp(i - 1, 13))
                t.matmul(pbcw[:], ones_row[0:1, 0:HD], rbw[0:1, (i % 2) * GT:(i % 2 + 1) * GT],
                         start=True, stop=True).then_inc(sT, 1)
                t.wait_ge(sV, Vp(i, 6))
                if i >= 1:
                    t.wait_ge(sS, Sp(i - 1, 14))
                t.matmul(pbcs[:], ones_row[0:1, 0:HD], rbs[0:1, (i % 2) * GT:(i % 2 + 1) * GT],
                         start=True, stop=True).then_inc(sT, 1)
        N['T'] = T0 + NIT * 16

        @blk.scalar
        def _(sc):
            for i in range(NIT):
                for ch in range(4):
                    sc.wait_ge(sT, Tp(i, ch + 1))
                    if i >= 2 and ch == 0:
                        sc.wait_ge(sT, Tp(i - 2, 13))
                    ps = psc0 if (i * 12 + ch) % 2 == 0 else psc1
                    sc.activation(PTW[:, (i % 2) * 4 * GT + ch * GT:(i % 2) * 4 * GT + (ch + 1) * GT],
                                  ps[:], AF.Exp).then_inc(sS, 1)
                for ch in range(8):
                    sc.wait_ge(sT, Tp(i, 4 + ch + 1))
                    if i >= 2 and ch == 0:
                        sc.wait_ge(sT, Tp(i - 2, 14))
                    ps = psc0 if (i * 12 + 4 + ch) % 2 == 0 else psc1
                    sc.activation(PTS[:, (i % 2) * 8 * GT + ch * GT:(i % 2) * 8 * GT + (ch + 1) * GT],
                                  ps[:], AF.Exp).then_inc(sS, 1)
                sc.wait_ge(sT, Tp(i, 15))
                if i >= 2:
                    sc.wait_ge(sV, Vp(i - 2, 8))
                sc.activation(sbcw[:, (i % 2) * GT:(i % 2 + 1) * GT], pbcw[:],
                              AF.Copy).then_inc(sS, 1)
                sc.wait_ge(sT, Tp(i, 16))
                if i >= 2:
                    sc.wait_ge(sV, Vp(i - 2, 10))
                sc.activation(sbcs[:, (i % 2) * GT:(i % 2 + 1) * GT], pbcs[:],
                              AF.Copy).then_inc(sS, 1)
        N['S'] = S0 + NIT * 14

        @blk.vector
        def _(v):
            for i in range(NIT):
                h, qt = divmod(i, 16)
                sl = (i % 2) * GT
                pavw = pavw0 if i % 2 == 0 else pavw1
                pavs = pavs0 if i % 2 == 0 else pavs1
                v.wait_ge(sT, Tp(i, 13))
                v.reciprocal(rzw[0:1, sl:sl + GT], pavw[64:65, :]).then_inc(sV, 1)
                v.wait_ge(sV, Vp(i, 1))
                v.tensor_scalar_mul(rgw[0:1, sl:sl + GT], rzw[0:1, sl:sl + GT],
                                    gcol[0:1, h * 3 + 2:h * 3 + 3]).then_inc(sV, 1)
                v.wait_ge(sV, Vp(i, 2))
                v.tensor_copy(rbw[0:1, sl:sl + GT], rgw[0:1, sl:sl + GT]).then_inc(sV, 1)
                v.wait_ge(sT, Tp(i, 14))
                v.reciprocal(rzs[0:1, sl:sl + GT], pavs[64:65, :]).then_inc(sV, 1)
                v.wait_ge(sV, Vp(i, 4))
                v.tensor_scalar_mul(rgs[0:1, sl:sl + GT], rzs[0:1, sl:sl + GT],
                                    gcol[0:1, h * 3 + 1:h * 3 + 2]).then_inc(sV, 1)
                v.wait_ge(sV, Vp(i, 5))
                v.tensor_copy(rbs[0:1, sl:sl + GT], rgs[0:1, sl:sl + GT]).then_inc(sV, 1)
                v.wait_ge(sCL0 if i % 2 == 0 else sCL1, 16 * (i // 2 + 1))
                v.tensor_scalar_mul(t1b[:, sl:sl + GT], cmpl[:, sl:sl + GT],
                                    gcol[0:HD, h * 3:h * 3 + 1]).then_inc(sV, 1)
                v.wait_ge(sS, Sp(i, 13))
                v.tensor_tensor(w1b[:, sl:sl + GT], pavw[0:HD, :],
                                sbcw[:, sl:sl + GT], OP.mult).then_inc(sV, 1)
                v.wait_ge(sV, Vp(i, 8))
                v.tensor_tensor(a2b[:, sl:sl + GT], t1b[:, sl:sl + GT],
                                w1b[:, sl:sl + GT], OP.add).then_inc(sV, 1)
                v.wait_ge(sS, Sp(i, 14))
                v.tensor_tensor(s1b[:, sl:sl + GT], pavs[0:HD, :],
                                sbcs[:, sl:sl + GT], OP.mult).then_inc(sV, 1)
                v.wait_ge(sV, Vp(i, 10))
                if i >= 2:
                    v.wait_ge(sAT0 if i % 2 == 0 else sAT1, 16 * ((i - 2) // 2 + 1))
                v.tensor_tensor(attb[:, sl:sl + GT], a2b[:, sl:sl + GT],
                                s1b[:, sl:sl + GT], OP.add).then_inc(sV, 1)
        N['V'] = V0 + NIT * 11
        N['G'] = G0
        DMAT.update(sQU0=16 * (NIT // 2), sQU1=16 * (NIT // 2),
                    sCL0=16 * (NIT // 2), sCL1=16 * (NIT // 2),
                    sAT0=16 * (NIT // 2), sAT1=16 * (NIT // 2))

        bar()
        b.pop()

        attx = b.sb("attx", [P, HALF], BF16)

        @blk.gpsimd
        def _(gp):
            gp.wait_ge(sAT0, 16 * (NIT // 2))
            gp.wait_ge(sAT1, 16 * (NIT // 2))
            ng = N['G']
            for c4 in range(4):
                gp.indirect_dma_start(
                    out=attx[:, :], out_offset=None,
                    in_=attfull[:],
                    in_offset=IndirectOffsetOnAxis(smeta[:, 2:3], 1),
                    element_offset=c4 * P * S,
                ).then_inc(sG, 16)
                ng += 16
                gp.wait_ge(sG, ng)
                gp.dma_start(out=att_in[c4 * P:(c4 + 1) * P, :], in_=attx[:, :]).then_inc(sG, 16)
                ng += 16
                gp.wait_ge(sG, ng)
            gp.collective_compute("AllGather", OP.bypass, replica_groups=PAIRS,
                                  ins=[att_in.opt()], outs=[att_out.opt()]).then_inc(sC, 1)
            gp.wait_ge(sC, N['C'] + 1)
        N['G'] += 128
        N['C'] += 1

    if BUILD_UPTO >= 9:
        b.push()
        seyef = b.sb("seyef", [P, P], F32)
        wslf = b.sb("wslf", [P, 2 * D], F32)
        wqsl = b.sb("wqsl", [P, 12 * D], BF16)
        amx = b.sb("amx", [P, 3], F32)
        amc = b.sb("amc", [P, 1], F32)
        arow = b.sb("arow", [1, 4], F32)
        srow = b.sb("srow", [1, 16], F32)
        zrow = b.sb("zrow", [1, 16], F32)
        ax85 = b.sb("ax85", [P, D], F32)
        sg85 = b.sb("sg85", [P, D], BF16)
        c185 = b.sb("c185", [P, D], F32)
        c285 = b.sb("c285", [P, D], F32)
        qa8 = b.sb("qa8", [P, 64], F32)
        scb8 = b.sb("scb8", [P, 64], F32)
        qi18 = b.sb("qi18", [P, 64], I32)
        qi28 = b.sb("qi28", [P, 64], I32)
        sc88 = b.sb("sc88", [P, 64], F32)
        dq8 = b.sb("dq8", [P, 64], F32)
        rdq8 = b.sb("rdq8", [P, 64], F32)
        dqh8 = b.sb("dqh8", [P, 64], F32)
        gstg = b.sb("gstg", [P, 2 * 512], BF16)
        dstg = b.sb("dstg", [P, 2 * P], BF16)
        pamr = b.ps("pamr", [3, P], F32)
        pt80 = b.ps("pt80", [P, P], BF16)
        pt81 = b.ps("pt81", [P, P], BF16)
        sA0 = b.sem("sA0")
        sA1 = b.sem("sA1")
        sEY = b.sem("sEY")
        sAW = b.sem("sAW")
        sAR = b.sem("sAR")
        sBC = b.sem("sBC")
        sAG0 = b.sem("sAG0")
        sAG1 = b.sem("sAG1")
        T0 = N['T']; V0 = N['V']; S0 = N['S']; C0 = N['C']
        CHSRC = ([gate_sl_d[t * P:(t + 1) * P, :] for t in range(4)] +
                 [up_sl_d[t * P:(t + 1) * P, :] for t in range(4)] +
                 [down_sl_d[:, t * D:(t + 1) * D] for t in range(4)])

        def VQ(ch, k):
            return V0 + 31 + NPIPE * ch + k

        def sAw(ld):
            return (sA0 if ld % 2 == 0 else sA1), 16 * (ld // 2 + 1)

        @blk.sync
        def _(sync):
            sync.dma_start(out=seyef[:], in_=eye_d[:]).then_inc(sEY, 16)
            for ld in range(12):
                if ld >= 2:
                    sync.wait_ge(sV, V0 + 2 + 2 * (ld - 2))
                sem_, _ = sAw(ld)
                sync.dma_start(out=wslf[:, (ld % 2) * D:(ld % 2 + 1) * D],
                               in_=CHSRC[ld]).then_inc(sem_, 16)
            sync.wait_ge(sC, C0 + 1)
            sync.dma_start(out=arow[:], in_=am_out[:]).then_inc(sAW, 16)
            for ch in range(12):
                ld = 12 + ch
                if ch < 2:
                    sync.wait_ge(sV, V0 + 25)
                else:
                    sync.wait_ge(sS, S0 + 2 * (ch - 2) + 2)
                sem_, _ = sAw(ld)
                sync.dma_start(out=wslf[:, (ch % 2) * D:(ch % 2 + 1) * D],
                               in_=CHSRC[ch]).then_inc(sem_, 16)
            for gu in range(16):
                sync.wait_ge(sS, S0 + 24 + gu * 4 + 4)
                dst = agg_in if gu < 8 else agu_in
                hc = gu % 8
                sync.dma_start(out=dst[hc * P:(hc + 1) * P, :],
                               in_=gstg[:, (gu % 2) * 512:(gu % 2 + 1) * 512]
                               ).then_inc(sAG0 if gu % 2 == 0 else sAG1, 16)
            for ic in range(32):
                sync.wait_ge(sS, S0 + 24 + 64 + ic + 1)
                sync.dma_start(out=agd_in[ic * P:(ic + 1) * P, :],
                               in_=dstg[:, (ic % 2) * P:(ic % 2 + 1) * P]
                               ).then_inc(sAG0 if ic % 2 == 0 else sAG1, 16)

        @blk.vector
        def _(v):
            v.memset(zrow[:], 0.0).then_inc(sV, 1)
            for ch in range(12):
                sem_, cnt = sAw(ch)
                v.wait_ge(sem_, cnt)
                if ch >= 1:
                    v.wait_ge(sV, V0 + 3 + 2 * (ch - 1))
                v.tensor_reduce(amc[:], wslf[:, (ch % 2) * D:(ch % 2 + 1) * D],
                                AX, OP.max, apply_absolute_value=True).then_inc(sV, 1)
                v.wait_ge(sV, V0 + 2 + 2 * ch)
                j = ch // 4
                if ch % 4 == 0:
                    v.tensor_copy(amx[:, j:j + 1], amc[:]).then_inc(sV, 1)
                else:
                    v.tensor_tensor(amx[:, j:j + 1], amx[:, j:j + 1], amc[:],
                                    OP.max).then_inc(sV, 1)
            v.wait_ge(sT, T0 + 1)
            v.tensor_reduce(amc[0:3, 0:1], pamr[:], AX, OP.max).then_inc(sV, 1)
            v.wait_ge(sAW, 16)
            v.reciprocal(srow[0:1, 10:13], arow[0:1, 0:3]).then_inc(sV, 1)
            v.wait_ge(sV, V0 + 27)
            v.tensor_scalar_mul(srow[0:1, 0:3], srow[0:1, 10:13], 448.0).then_inc(sV, 1)
            v.tensor_scalar_mul(srow[0:1, 3:6], arow[0:1, 0:3],
                                1.0 / 2688.0).then_inc(sV, 1)
            v.tensor_scalar_mul(srow[0:1, 6:8], sins[0:1, 0:2], 1.0 / 6.0).then_inc(sV, 1)
            v.reciprocal(srow[0:1, 8:10], sins[0:1, 0:2]).then_inc(sV, 1)
            for ch in range(12):
                j = ch // 4
                if ch == 0:
                    v.wait_ge(sBC, 16)
                v.wait_ge(sS, S0 + 2 * ch + 1)
                v.wait_ge(sS, S0 + 2 * ch + 2)
                _qpipe_v(v, VQ(ch, 0), ax85, c185, c285, qa8, scb8, qi18, qi28,
                         sc88, dq8, rdq8, dqh8, mgkc,
                         sccol[:, j:j + 1], sccol[:, 3 + j:4 + j], 64,
                         wqsl[:, ch * D:(ch + 1) * D], sg85[:],
                         sS, S0 + 2 * ch + 2)
        N['V'] = V0 + 31 + 12 * NPIPE

        @blk.scalar
        def _(sc):
            sc.wait_ge(sV, V0 + 31)
            sc.dma_start(out=amsc[0:1, 0:10], in_=srow[0:1, 0:10]).then_inc(sAW, 16)
            for ch in range(12):
                sem_, cnt = sAw(12 + ch)
                sc.wait_ge(sem_, cnt)
                if ch >= 1:
                    sc.wait_ge(sV, VQ(ch - 1, 17))
                sc.activation(ax85[:], wslf[:, (ch % 2) * D:(ch % 2 + 1) * D],
                              AF.Abs).then_inc(sS, 1)
                if ch >= 1:
                    sc.wait_ge(sV, VQ(ch - 1, NPIPE))
                sc.activation(sg85[:], wslf[:, (ch % 2) * D:(ch % 2 + 1) * D],
                              AF.Sign).then_inc(sS, 1)
            for gu in range(16):
                hc = gu % 8
                for t in range(4):
                    i = gu * 4 + t
                    sc.wait_ge(sT, T0 + 1 + i + 1)
                    if t == 0 and gu >= 2:
                        sc.wait_ge(sAG0 if gu % 2 == 0 else sAG1,
                                   16 * (gu // 2))
                    pm = pt80 if i % 2 == 0 else pt81
                    sc.activation(gstg[:, (gu % 2) * 512 + t * P:(gu % 2) * 512 + (t + 1) * P],
                                  pm[0:P, 0:P], AF.Copy).then_inc(sS, 1)
            for ic in range(32):
                i = 64 + ic
                sc.wait_ge(sT, T0 + 1 + i + 1)
                if ic >= 2:
                    sc.wait_ge(sAG0 if ic % 2 == 0 else sAG1,
                               16 * (8 + ic // 2))
                pm = pt80 if i % 2 == 0 else pt81
                sc.activation(dstg[:, (ic % 2) * P:(ic % 2 + 1) * P],
                              pm[0:P, 0:P], AF.Copy).then_inc(sS, 1)
        N['S'] = S0 + 24 + 96

        @blk.tensor
        def _(t):
            t.wait_ge(sV, V0 + 25)
            t.wait_ge(sEY, 16)
            t.transpose(pamr[:], amx[:], seyef[:]).then_inc(sT, 1)
            for i in range(96):
                if i < 64:
                    gu, tt = divmod(i, 4)
                    ch = (gu % 8) // 2 * 0 + (tt if gu < 8 else 4 + tt)
                    hc = gu % 8
                    src = wqsl[:, ch * D + hc * P:ch * D + (hc + 1) * P]
                else:
                    ic = i - 64
                    ch = 8 + ic // 8
                    src = wqsl[:, ch * D + (ic % 8) * P:ch * D + (ic % 8 + 1) * P]
                t.wait_ge(sV, VQ(ch, NPIPE))
                if i >= 2:
                    t.wait_ge(sS, S0 + 24 + (i - 2) + 1)
                pm = pt80 if i % 2 == 0 else pt81
                t.transpose(pm[0:P, 0:P], src, eye[:]).then_inc(sT, 1)
        N['T'] = T0 + 1 + 96

        @blk.gpsimd
        def _(gp):
            gp.wait_ge(sV, V0 + 1)
            gp.dma_start(out=am_in[0:1, 3:4], in_=zrow[0:1, 0:1]).then_inc(sAR, 16)
            gp.wait_ge(sV, V0 + 26)
            gp.dma_start(out=am_in[0:1, 0:3].rearrange("o p -> p o"),
                         in_=amc[0:3, 0:1],
                         allow_slow_non_contiguous=True).then_inc(sAR, 16)
            gp.wait_ge(sAR, 16 * 2)
            gp.collective_compute("AllReduce", OP.max, replica_groups=ALL8,
                                  ins=[am_in.opt()], outs=[am_out.opt()]).then_inc(sC, 1)
            gp.wait_ge(sAW, 16 * 2)
            gp.dma_start(out=sccol[:, 0:10], in_=amsc[0:1, 0:10].broadcast_to((P, 10)),
                         allow_slow_non_contiguous=True).then_inc(sBC, 16)
            gp.wait_ge(sAG0, 16 * 24)
            gp.wait_ge(sAG1, 16 * 24)
            gp.collective_compute("AllGather", OP.bypass, replica_groups=ALL8,
                                  ins=[agg_in.opt()], outs=[agg_out.opt()]).then_inc(sC, 1)
            gp.collective_compute("AllGather", OP.bypass, replica_groups=ALL8,
                                  ins=[agu_in.opt()], outs=[agu_out.opt()]).then_inc(sC, 1)
            gp.collective_compute("AllGather", OP.bypass, replica_groups=ALL8,
                                  ins=[agd_in.opt()], outs=[agd_out.opt()]).then_inc(sC, 1)
        N['C'] = C0 + 4
        N['CAGU'] = C0 + 3
        N['CAGD'] = C0 + 4
        DMAT.update(sA0=16 * 12, sA1=16 * 12, sAR=16 * 2, sAW=16 * 2, sEY=16, sBC=16,
                    sAG0=16 * 24, sAG1=16 * 24)

        b.pop()

        bar()
    if BUILD_UPTO >= 6:
        b.pop()
        stream = b.sb("stream", [P, 8 * HALF], BF16)
        b.push()
        attall = b.sb("attall", [P, 8 * HALF], BF16)
        wo_sb = b.sb("wo_sb", [P, 8 * D], BF16)
        wo_st = b.sb("wo_st", [P, 2 * D], F32)
        xst6 = b.sb("xst6", [P, 2 * D], F32)
        xb6 = b.sb("xb6", [P, 2 * D], BF16)
        ptr6 = b.ps("ptr6", [P, 4 * 1024], BF16)
        pw0 = b.ps("pw0", [P, GT], F32)
        pw1 = b.ps("pw1", [P, GT], F32)
        sX60 = b.sem("sX60")
        sX61 = b.sem("sX61")
        sW60 = b.sem("sW60")
        sW61 = b.sem("sW61")
        sAL0 = b.sem("sAL0")
        T0 = N['T']; V0 = N['V']; S0 = N['S']; G0 = N['G']
        NT6 = HALF // P

        def T6tp(i, dc):
            return T0 + i * 8 + dc + 1

        def T6chain(u):
            return T0 + NT6 * 8 + u + 1

        @blk.gpsimd
        def _(gp):
            for hc in range(4):
                gp.indirect_dma_start(
                    out=attall[:, hc * HALF:(hc + 1) * HALF], out_offset=None,
                    in_=attfull[:], in_offset=IndirectOffsetOnAxis(smeta[:, 3:4], 1),
                    element_offset=hc * P * S).then_inc(sAL0, 16)
            for hc in range(4):
                gp.indirect_dma_start(
                    out=attall[:, (4 + hc) * HALF:(5 + hc) * HALF], out_offset=None,
                    in_=att_out[:], in_offset=IndirectOffsetOnAxis(smeta[:, 1:2], 0),
                    element_offset=hc * P * HALF).then_inc(sAL0, 16)
            for i in range(NT6):
                if i >= 2:
                    gp.wait_ge(sV, V0 + 8 + (i - 2) + 1)
                gp.indirect_dma_start(
                    out=xst6[:, (i % 2) * D:(i % 2 + 1) * D], out_offset=None,
                    in_=x_d[:], in_offset=IndirectOffsetOnAxis(smeta[:, 0:1], 0),
                    element_offset=i * P * D).then_inc(sX60 if i % 2 == 0 else sX61, 16)

        @blk.sync
        def _(sync):
            for c in range(8):
                if c >= 2:
                    sync.wait_ge(sV, V0 + (c - 2) + 1)
                sync.dma_start(out=wo_st[:, (c % 2) * D:(c % 2 + 1) * D],
                               in_=wo_d[c * P:(c + 1) * P, :]
                               ).then_inc(sW60 if c % 2 == 0 else sW61, 16)

        @blk.vector
        def _(v):
            for c in range(8):
                v.wait_ge(sW60 if c % 2 == 0 else sW61, 16 * (c // 2 + 1))
                v.tensor_copy(wo_sb[:, c * D:(c + 1) * D],
                              wo_st[:, (c % 2) * D:(c % 2 + 1) * D]).then_inc(sV, 1)
            V20 = N['V2']
            for i in range(NT6 + 1):
                if i < NT6:
                    v.wait_ge(sX60 if i % 2 == 0 else sX61, 16 * (i // 2 + 1))
                    if i >= 2:
                        v.wait_ge(sT, T6tp(i - 2, 7))
                    v.tensor_copy(xb6[:, (i % 2) * D:(i % 2 + 1) * D],
                                  xst6[:, (i % 2) * D:(i % 2 + 1) * D]).then_inc(sV, 1)
                if i >= 1:
                    ce = i - 1
                    for m in range(2):
                        v.wait_ge(sT, T6tp(ce, 4 * m + 3))
                        src = ptr6[:].rearrange("p (s q) -> p s q", q=1024)[:, :, 0:P]
                        dst = stream[:].rearrange("p (dc n) -> p dc n", n=HALF)[
                            :, 4 * m:4 * m + 4, ce * P:(ce + 1) * P]
                        v.tensor_copy(dst, src).then_inc(sV2, 1)
            for u in range(64):
                tg, dc = divmod(u, 8)
                v.wait_ge(sT, T6chain(u))
                pm = pw0 if u % 2 == 0 else pw1
                v.tensor_tensor(stream[:, dc * HALF + tg * GT:dc * HALF + (tg + 1) * GT],
                                stream[:, dc * HALF + tg * GT:dc * HALF + (tg + 1) * GT],
                                pm[:], OP.add).then_inc(sV, 1)
        N['V'] = V0 + 8 + NT6 + 64

        @blk.tensor
        def _(t):
            V20 = N['V2']
            for i in range(NT6):
                t.wait_ge(sV, V0 + 8 + i + 1)
                for dc in range(8):
                    j = i * 8 + dc
                    if j >= 4:
                        t.wait_ge(sV2, V20 + (j - 4) // 4 + 1)
                    t.transpose(ptr6[:, (j % 4) * 1024:(j % 4) * 1024 + P],
                                xb6[:, (i % 2) * D + dc * P:(i % 2) * D + (dc + 1) * P],
                                eye[:]).then_inc(sT, 1)
            t.wait_ge(sV, V0 + 8)
            t.wait_ge(sAL0, 16 * 8)
            for u in range(64):
                tg, dc = divmod(u, 8)
                if u >= 2:
                    t.wait_ge(sV, V0 + 8 + NT6 + (u - 2) + 1)
                if u == 0:
                    t.wait_ge(sV2, V20 + 2 * NT6)
                pm = pw0 if u % 2 == 0 else pw1
                for hc in range(8):
                    mi = t.matmul(pm[:], wo_sb[:, hc * D + dc * P:hc * D + (dc + 1) * P],
                                  attall[:, hc * HALF + tg * GT:hc * HALF + (tg + 1) * GT],
                                  start=(hc == 0), stop=(hc == 7))
                mi.then_inc(sT, 1)
        N['T'] = T0 + NT6 * 8 + 64
        N['V2'] = N['V2'] + 2 * NT6
        DMAT.update(sX60=16 * (NT6 // 2), sX61=16 * (NT6 // 2), sW60=16 * 4, sW61=16 * 4,
                    sAL0=16 * 8)

        bar()
        b.pop()

    if BUILD_UPTO >= 7:
        b.push()
        w0sb = b.sb("w0sb", [P, 8 * D], BF16)
        w0st = b.sb("w0st", [P, 2 * D], F32)
        xsq7 = b.sb("xsq7", [P, 8 * GT], BF16)
        xhat7 = b.sb("xhat7", [P, 8 * GT], BF16)
        mrow = b.sb("mrow", [1, GT], F32)
        vrow = b.sb("vrow", [1, GT], F32)
        varr = b.sb("varr", [1, GT], F32)
        mrr = b.sb("mrr", [1, GT], F32)
        rstdf = b.sb("rstdf", [1, GT], F32)
        sqrow = b.sb("sqrow", [1, GT], F32)
        rbr = b.sb("rbr", [1, GT], BF16)
        mbr = b.sb("mbr", [1, GT], BF16)
        rstdb = b.sb("rstdb", [P, GT], BF16)
        mrb = b.sb("mrb", [P, GT], BF16)
        psA = b.ps("psA", [1, GT], F32)
        psB = b.ps("psB", [1, GT], F32)
        pr1 = b.ps("pr1", [P, GT], F32)
        pr2 = b.ps("pr2", [P, GT], F32)
        pff0 = b.ps("pff0", [P, GT], F32)
        pff1 = b.ps("pff1", [P, GT], F32)
        sW70 = b.sem("sW70")
        sW71 = b.sem("sW71")
        T0 = N['T']; V0 = N['V']; S0 = N['S']
        NG7 = HALF // GT

        def V7(g, j):
            return V0 + g * 40 + j

        def T7(g, j):
            return T0 + g * 12 + j

        def S7(g, j):
            return S0 + 8 + g * 3 + j

        @blk.sync
        def _(sync):
            for c in range(8):
                if c >= 2:
                    sync.wait_ge(sS, S0 + (c - 2) + 1)
                sync.dma_start(out=w0st[:, (c % 2) * D:(c % 2 + 1) * D],
                               in_=w0t_d[c * P:(c + 1) * P, :]
                               ).then_inc(sW70 if c % 2 == 0 else sW71, 16)

        @blk.scalar
        def _(sc):
            for c in range(8):
                sc.wait_ge(sW70 if c % 2 == 0 else sW71, 16 * (c // 2 + 1))
                sc.activation(w0sb[:, c * D:(c + 1) * D],
                              w0st[:, (c % 2) * D:(c % 2 + 1) * D], AF.Sign).then_inc(sS, 1)
            for g in range(NG7):
                sc.wait_ge(sV, V7(g, 12))
                sc.activation(sqrow[:], varr[:], AF.Sqrt).then_inc(sS, 1)
                sc.wait_ge(sT, T7(g, 3))
                if g >= 1:
                    sc.wait_ge(sV, V7(g - 1, 32))
                sc.activation(rstdb[:], pr1[:], AF.Copy).then_inc(sS, 1)
                sc.wait_ge(sT, T7(g, 4))
                sc.activation(mrb[:], pr2[:], AF.Copy).then_inc(sS, 1)
        N['S'] = S0 + 8 + NG7 * 3

        @blk.vector
        def _(v):
            for g in range(NG7):
                for dc in range(8):
                    if g >= 1 and dc == 0:
                        v.wait_ge(sT, T7(g - 1, 2))
                    src = stream[:, dc * HALF + g * GT:dc * HALF + (g + 1) * GT]
                    v.tensor_tensor(xsq7[:, dc * GT:(dc + 1) * GT], src, src,
                                    OP.mult).then_inc(sV, 1)
                v.wait_ge(sT, T7(g, 1))
                v.tensor_scalar_mul(mrow[:], psA[:], 1.0 / 1024.0).then_inc(sV, 1)
                v.wait_ge(sT, T7(g, 2))
                v.tensor_scalar_mul(vrow[:], psB[:], 1.0 / 1024.0).then_inc(sV, 1)
                v.wait_ge(sV, V7(g, 10))
                v.tensor_tensor(mrr[:], mrow[:], mrow[:], OP.mult).then_inc(sV, 1)
                v.wait_ge(sV, V7(g, 11))
                v.scalar_tensor_tensor(varr[:], vrow[:], 1e-5, mrr[:],
                                       OP.add, OP.subtract).then_inc(sV, 1)
                v.wait_ge(sS, S7(g, 1))
                v.reciprocal(rstdf[:], sqrow[:]).then_inc(sV, 1)
                v.wait_ge(sV, V7(g, 13))
                v.tensor_tensor(mrr[:], mrow[:], rstdf[:], OP.mult).then_inc(sV, 1)
                v.wait_ge(sV, V7(g, 14))
                v.tensor_copy(rbr[:], rstdf[:]).then_inc(sV, 1)
                v.tensor_copy(mbr[:], mrr[:]).then_inc(sV, 1)
                v.wait_ge(sS, S7(g, 3))
                for dc in range(8):
                    src = stream[:, dc * HALF + g * GT:dc * HALF + (g + 1) * GT]
                    v.tensor_tensor(xhat7[:, dc * GT:(dc + 1) * GT], src, rstdb[:],
                                    OP.mult).then_inc(sV, 1)
                    v.wait_ge(sV, V7(g, 17 + 2 * dc))
                    v.tensor_tensor(xhat7[:, dc * GT:(dc + 1) * GT],
                                    xhat7[:, dc * GT:(dc + 1) * GT], mrb[:],
                                    OP.subtract).then_inc(sV, 1)
                for oc in range(8):
                    v.wait_ge(sT, T7(g, 5 + oc))
                    pm = pff0 if oc % 2 == 0 else pff1
                    dst = stream[:, oc * HALF + g * GT:oc * HALF + (g + 1) * GT]
                    v.tensor_tensor(dst, dst, pm[:], OP.add).then_inc(sV, 1)
        N['V'] = V0 + NG7 * 40

        @blk.tensor
        def _(t):
            for g in range(NG7):
                if g >= 1:
                    t.wait_ge(sV, V7(g - 1, 9))
                for dc in range(8):
                    mi = t.matmul(psA[:], ones_c128[:],
                                  stream[:, dc * HALF + g * GT:dc * HALF + (g + 1) * GT],
                                  start=(dc == 0), stop=(dc == 7))
                mi.then_inc(sT, 1)
                t.wait_ge(sV, V7(g, 8))
                if g >= 1:
                    t.wait_ge(sV, V7(g - 1, 10))
                for dc in range(8):
                    mi = t.matmul(psB[:], ones_c128[:], xsq7[:, dc * GT:(dc + 1) * GT],
                                  start=(dc == 0), stop=(dc == 7))
                mi.then_inc(sT, 1)
                t.wait_ge(sV, V7(g, 15))
                if g >= 1:
                    t.wait_ge(sS, S7(g - 1, 2))
                t.matmul(pr1[:], ones_row[0:1, 0:P], rbr[:], start=True, stop=True).then_inc(sT, 1)
                t.wait_ge(sV, V7(g, 16))
                if g >= 1:
                    t.wait_ge(sS, S7(g - 1, 3))
                t.matmul(pr2[:], ones_row[0:1, 0:P], mbr[:], start=True, stop=True).then_inc(sT, 1)
                t.wait_ge(sV, V7(g, 32))
                if g == 0:
                    t.wait_ge(sS, S0 + 8)
                for oc in range(8):
                    if g * 8 + oc >= 2:
                        gp_, op_ = divmod(g * 8 + oc - 2, 8)
                        t.wait_ge(sV, V7(gp_, 33 + op_))
                    pm = pff0 if oc % 2 == 0 else pff1
                    for ic in range(8):
                        mi = t.matmul(pm[:], w0sb[:, ic * D + oc * P:ic * D + (oc + 1) * P],
                                      xhat7[:, ic * GT:(ic + 1) * GT],
                                      start=(ic == 0), stop=(ic == 7))
                    mi.then_inc(sT, 1)
        N['T'] = T0 + NG7 * 12
        DMAT.update(sW70=16 * 4, sW71=16 * 4)

        bar()
        b.pop()

    if BUILD_UPTO >= 8:
        b.push()
        wcst = b.sb("wcst", [P, 2 * DIN], F32)
        wcb = b.sb("wcb", [P, 2 * DIN], BF16)
        sW80 = b.sem("sW80")
        sW81 = b.sem("sW81")
        sQ80 = b.sem("sQ80")
        sQ81 = b.sem("sQ81")
        V0 = N['V']
        WCHUNKS = ([(w_in_d, wbfi, c, DIN) for c in range(8)] +
                   [(w_g1_d, wbfg, c, DIN) for c in range(16)] +
                   [(w_out_d, wbfo, c, D) for c in range(16)])

        @blk.sync
        def _(sync):
            for j, (src, dst, c, wd) in enumerate(WCHUNKS):
                if j >= 2:
                    sync.wait_ge(sV, V0 + (j - 2) + 1)
                sync.dma_start(out=wcst[:, (j % 2) * DIN:(j % 2) * DIN + wd],
                               in_=src[c * P:(c + 1) * P, :]
                               ).then_inc(sW80 if j % 2 == 0 else sW81, 16)

        @blk.vector
        def _(v):
            for j, (src, dst, c, wd) in enumerate(WCHUNKS):
                v.wait_ge(sW80 if j % 2 == 0 else sW81, 16 * (j // 2 + 1))
                if j >= 2:
                    v.wait_ge(sQ80 if j % 2 == 0 else sQ81, 16 * (j // 2))
                v.tensor_copy(wcb[:, (j % 2) * DIN:(j % 2) * DIN + wd],
                              wcst[:, (j % 2) * DIN:(j % 2) * DIN + wd]).then_inc(sV, 1)
        N['V'] = V0 + len(WCHUNKS)

        @blk.scalar
        def _(sc):
            for j, (src, dst, c, wd) in enumerate(WCHUNKS):
                sc.wait_ge(sV, V0 + j + 1)
                sc.dma_start(out=dst[:, c * wd:(c + 1) * wd],
                             in_=wcb[:, (j % 2) * DIN:(j % 2) * DIN + wd]
                             ).then_inc(sQ80 if j % 2 == 0 else sQ81, 16)
        DMAT.update(sW80=16 * 20, sW81=16 * 20, sQ80=16 * 20, sQ81=16 * 20)

        bar()
        b.pop()

    if BUILD_UPTO >= 8:
        b.push()
        xh2 = b.sb("xh2", [P, 8 * GT], BF16)
        xin = b.sb("xin", [P, 16 * GT], BF16)
        tsl = b.sb("tsl", [P, 16 * GT], BF16)
        vbuf8 = b.sb("vbuf8", [P, 8 * GT], BF16)
        silus = b.sb("silus", [P, 2 * GT], BF16)
        wosc = b.sb("wosc", [P, 2 * GT], BF16)
        win8 = b.sb("win8", [P, 2 * 8 * P], BF16)
        wgg8 = b.sb("wgg8", [P, 2 * 16 * P], BF16)
        wou8 = b.sb("wou8", [P, 2 * 16 * P], BF16)
        xsq8 = b.sb("xsq8", [P, 8 * GT], BF16)
        mrow = b.sb("mrow8", [1, GT], F32)
        vrow = b.sb("vrow8", [1, GT], F32)
        varr = b.sb("varr8", [1, GT], F32)
        mrr = b.sb("mrr8", [1, GT], F32)
        rstdf = b.sb("rstdf8", [1, GT], F32)
        sqrow = b.sb("sqrow8", [1, GT], F32)
        rbr = b.sb("rbr8", [1, GT], BF16)
        mbr = b.sb("mbr8", [1, GT], BF16)
        rstdb = b.sb("rstdb8", [P, GT], BF16)
        mrb = b.sb("mrb8", [P, GT], BF16)
        psA = b.ps("psA8", [1, GT], F32)
        psB = b.ps("psB8", [1, GT], F32)
        pr1 = b.ps("pr18", [P, GT], F32)
        pr2 = b.ps("pr28", [P, GT], F32)
        pm0 = b.ps("pm08", [P, GT], F32)
        pm1 = b.ps("pm18", [P, GT], F32)
        sWI0 = b.sem("sWI0")
        sWI1 = b.sem("sWI1")
        T0 = N['T']; V0 = N['V']; S0 = N['S']
        NG8 = HALF // GT

        NVG = 104
        NTG = 48
        NSG = 46

        def V8(g, j):
            return V0 + g * NVG + j

        def T8(g, j):
            return T0 + g * NTG + j

        def S8(g, j):
            return S0 + g * NSG + j

        @blk.sync
        def _(sync):
            for g in range(NG8):
                for u in range(40):
                    gl = g * 40 + u
                    if gl >= 2:
                        pu = gl - 2
                        pg_, uu = divmod(pu, 40)
                        if uu < 16:
                            sync.wait_ge(sT, T8(pg_, 5 + uu))
                        elif uu < 32:
                            sync.wait_ge(sT, T8(pg_, 21 + (uu - 16)))
                        else:
                            sync.wait_ge(sT, T8(pg_, 37 + (uu - 32)))
                    sl = gl % 2
                    if u < 16:
                        io = u
                        sync.dma_start(
                            out=win8[:, sl * 8 * P:(sl + 1) * 8 * P],
                            in_=wbfi[:].rearrange("p (dc q) -> p dc q", q=DIN)[:, :, io * P:(io + 1) * P]
                        ).then_inc(sWI0 if sl == 0 else sWI1, 16)
                    elif u < 32:
                        oo = u - 16
                        sync.dma_start(
                            out=wgg8[:, sl * 16 * P:(sl + 1) * 16 * P],
                            in_=wbfg[:].rearrange("p (ic q) -> p ic q", q=DIN)[:, :, oo * P:(oo + 1) * P]
                        ).then_inc(sWI0 if sl == 0 else sWI1, 16)
                    else:
                        dc = u - 32
                        sync.dma_start(
                            out=wou8[:, sl * 16 * P:(sl + 1) * 16 * P],
                            in_=wbfo[:].rearrange("p (ic q) -> p ic q", q=D)[:, :, dc * P:(dc + 1) * P]
                        ).then_inc(sWI0 if sl == 0 else sWI1, 16)

        @blk.vector
        def _(v):
            for g in range(NG8):
                for dc in range(8):
                    if g >= 1 and dc == 0:
                        v.wait_ge(sT, T8(g - 1, 2))
                        v.wait_ge(sV, V8(g - 1, 104))
                    src = stream[:, dc * HALF + g * GT:dc * HALF + (g + 1) * GT]
                    v.tensor_tensor(xsq8[:, dc * GT:(dc + 1) * GT], src, src,
                                    OP.mult).then_inc(sV, 1)
                v.wait_ge(sT, T8(g, 1))
                v.tensor_scalar_mul(mrow[:], psA[:], 1.0 / 1024.0).then_inc(sV, 1)
                v.wait_ge(sT, T8(g, 2))
                v.tensor_scalar_mul(vrow[:], psB[:], 1.0 / 1024.0).then_inc(sV, 1)
                v.wait_ge(sV, V8(g, 10))
                v.tensor_tensor(mrr[:], mrow[:], mrow[:], OP.mult).then_inc(sV, 1)
                v.wait_ge(sV, V8(g, 11))
                v.scalar_tensor_tensor(varr[:], vrow[:], 1e-5, mrr[:],
                                       OP.add, OP.subtract).then_inc(sV, 1)
                v.wait_ge(sS, S8(g, 1))
                v.reciprocal(rstdf[:], sqrow[:]).then_inc(sV, 1)
                v.wait_ge(sV, V8(g, 13))
                v.tensor_tensor(mrr[:], mrow[:], rstdf[:], OP.mult).then_inc(sV, 1)
                v.wait_ge(sV, V8(g, 14))
                v.tensor_copy(rbr[:], rstdf[:]).then_inc(sV, 1)
                v.tensor_copy(mbr[:], mrr[:]).then_inc(sV, 1)
                v.wait_ge(sS, S8(g, 3))
                for dc in range(8):
                    src = stream[:, dc * HALF + g * GT:dc * HALF + (g + 1) * GT]
                    v.tensor_tensor(xh2[:, dc * GT:(dc + 1) * GT], src, rstdb[:],
                                    OP.mult).then_inc(sV, 1)
                    v.wait_ge(sV, V8(g, 17 + 2 * dc))
                    v.tensor_tensor(xh2[:, dc * GT:(dc + 1) * GT],
                                    xh2[:, dc * GT:(dc + 1) * GT], mrb[:],
                                    OP.subtract).then_inc(sV, 1)
                for oo in range(16):
                    v.wait_ge(sS, S8(g, 20 + oo))
                    v.tensor_scalar_mul(tsl[:, oo * GT:(oo + 1) * GT],
                                        silus[:, (oo % 2) * GT:(oo % 2 + 1) * GT],
                                        b_sp_sb[:, oo:oo + 1]).then_inc(sV, 1)
                for dc in range(8):
                    v.wait_ge(sS, S8(g, 36 + dc))
                    v.tensor_tensor(vbuf8[:, dc * GT:(dc + 1) * GT],
                                    xh2[:, dc * GT:(dc + 1) * GT],
                                    wosc[:, (dc % 2) * GT:(dc % 2 + 1) * GT],
                                    OP.add).then_inc(sV, 1)
                v.wait_ge(sV, V8(g, 56))
                for dc in range(8):
                    v.tensor_tensor(xsq8[:, dc * GT:(dc + 1) * GT],
                                    vbuf8[:, dc * GT:(dc + 1) * GT],
                                    vbuf8[:, dc * GT:(dc + 1) * GT], OP.mult).then_inc(sV, 1)
                v.wait_ge(sT, T8(g, 45))
                v.tensor_scalar_mul(mrow[:], psA[:], 1.0 / 1024.0).then_inc(sV, 1)
                v.wait_ge(sT, T8(g, 46))
                v.tensor_scalar_mul(vrow[:], psB[:], 1.0 / 1024.0).then_inc(sV, 1)
                v.wait_ge(sV, V8(g, 66))
                v.tensor_tensor(mrr[:], mrow[:], mrow[:], OP.mult).then_inc(sV, 1)
                v.wait_ge(sV, V8(g, 67))
                v.scalar_tensor_tensor(varr[:], vrow[:], 1e-5, mrr[:],
                                       OP.add, OP.subtract).then_inc(sV, 1)
                v.wait_ge(sS, S8(g, 44))
                v.reciprocal(rstdf[:], sqrow[:]).then_inc(sV, 1)
                v.wait_ge(sV, V8(g, 69))
                v.tensor_tensor(mrr[:], mrow[:], rstdf[:], OP.mult).then_inc(sV, 1)
                v.wait_ge(sV, V8(g, 70))
                v.tensor_copy(rbr[:], rstdf[:]).then_inc(sV, 1)
                v.tensor_copy(mbr[:], mrr[:]).then_inc(sV, 1)
                v.wait_ge(sS, S8(g, 46))
                for dc in range(8):
                    base = V8(g, 73 + 4 * dc)
                    v.tensor_tensor(xsq8[:, dc * GT:(dc + 1) * GT],
                                    vbuf8[:, dc * GT:(dc + 1) * GT], rstdb[:],
                                    OP.mult).then_inc(sV, 1)
                    v.wait_ge(sV, base)
                    v.tensor_tensor(xsq8[:, dc * GT:(dc + 1) * GT],
                                    xsq8[:, dc * GT:(dc + 1) * GT], mrb[:],
                                    OP.subtract).then_inc(sV, 1)
                    v.wait_ge(sV, base + 1)
                    v.tensor_scalar(xsq8[:, dc * GT:(dc + 1) * GT],
                                    xsq8[:, dc * GT:(dc + 1) * GT],
                                    nw_sb[:, dc:dc + 1], nb_sb[:, dc:dc + 1],
                                    OP.mult, OP.add).then_inc(sV, 1)
                    v.wait_ge(sV, base + 2)
                    dst = stream[:, dc * HALF + g * GT:dc * HALF + (g + 1) * GT]
                    v.tensor_tensor(dst, dst, xsq8[:, dc * GT:(dc + 1) * GT],
                                    OP.add).then_inc(sV, 1)
        N['V'] = V0 + NG8 * NVG

        @blk.scalar
        def _(sc):
            for g in range(NG8):
                sc.wait_ge(sV, V8(g, 12))
                sc.activation(sqrow[:], varr[:], AF.Sqrt).then_inc(sS, 1)
                sc.wait_ge(sT, T8(g, 3))
                if g >= 1:
                    sc.wait_ge(sV, V8(g - 1, 32))
                sc.activation(rstdb[:], pr1[:], AF.Copy).then_inc(sS, 1)
                sc.wait_ge(sT, T8(g, 4))
                sc.activation(mrb[:], pr2[:], AF.Copy).then_inc(sS, 1)
                for io in range(16):
                    sc.wait_ge(sT, T8(g, 5 + io))
                    pm = pm0 if io % 2 == 0 else pm1
                    sc.activation(xin[:, io * GT:(io + 1) * GT], pm[:], AF.Identity,
                                  bias=b_in_sb[:, io:io + 1]).then_inc(sS, 1)
                for oo in range(16):
                    sc.wait_ge(sT, T8(g, 21 + oo))
                    if oo >= 2:
                        sc.wait_ge(sV, V8(g, 33 + oo - 2))
                    pm = pm0 if oo % 2 == 0 else pm1
                    sc.activation(silus[:, (oo % 2) * GT:(oo % 2 + 1) * GT], pm[:],
                                  AF.Silu, bias=b_g1_sb[:, oo:oo + 1]).then_inc(sS, 1)
                for dc in range(8):
                    sc.wait_ge(sT, T8(g, 37 + dc))
                    if dc >= 2:
                        sc.wait_ge(sV, V8(g, 49 + dc - 2))
                    pm = pm0 if dc % 2 == 0 else pm1
                    sc.activation(wosc[:, (dc % 2) * GT:(dc % 2 + 1) * GT], pm[:],
                                  AF.Identity, bias=b_out_sb[:, dc:dc + 1]).then_inc(sS, 1)
                sc.wait_ge(sV, V8(g, 68))
                sc.activation(sqrow[:], varr[:], AF.Sqrt).then_inc(sS, 1)
                sc.wait_ge(sT, T8(g, 47))
                if g >= 1:
                    sc.wait_ge(sV, V8(g - 1, 73 + 4 * 7 + 1))
                sc.activation(rstdb[:], pr1[:], AF.Copy).then_inc(sS, 1)
                sc.wait_ge(sT, T8(g, 48))
                sc.activation(mrb[:], pr2[:], AF.Copy).then_inc(sS, 1)
        N['S'] = S0 + NG8 * NSG

        @blk.tensor
        def _(t):
            for g in range(NG8):
                if g >= 1:
                    t.wait_ge(sV, V8(g - 1, 9))
                for dc in range(8):
                    mi = t.matmul(psA[:], ones_c128[:],
                                  stream[:, dc * HALF + g * GT:dc * HALF + (g + 1) * GT],
                                  start=(dc == 0), stop=(dc == 7))
                mi.then_inc(sT, 1)
                t.wait_ge(sV, V8(g, 8))
                if g >= 1:
                    t.wait_ge(sV, V8(g - 1, 10))
                for dc in range(8):
                    mi = t.matmul(psB[:], ones_c128[:], xsq8[:, dc * GT:(dc + 1) * GT],
                                  start=(dc == 0), stop=(dc == 7))
                mi.then_inc(sT, 1)
                t.wait_ge(sV, V8(g, 15))
                if g >= 1:
                    t.wait_ge(sS, S8(g - 1, 2))
                t.matmul(pr1[:], ones_row[0:1, 0:P], rbr[:], start=True, stop=True).then_inc(sT, 1)
                t.wait_ge(sV, V8(g, 16))
                if g >= 1:
                    t.wait_ge(sS, S8(g - 1, 3))
                t.matmul(pr2[:], ones_row[0:1, 0:P], mbr[:], start=True, stop=True).then_inc(sT, 1)
                t.wait_ge(sV, V8(g, 32))
                for io in range(16):
                    gl = g * 40 + io
                    t.wait_ge(sWI0 if gl % 2 == 0 else sWI1, 16 * (gl // 2 + 1))
                    if io >= 2:
                        t.wait_ge(sS, S8(g, 4 + io - 2) + 0)
                    pm = pm0 if io % 2 == 0 else pm1
                    for dc in range(8):
                        mi = t.matmul(pm[:],
                                      win8[:, (gl % 2) * 8 * P + dc * P:(gl % 2) * 8 * P + (dc + 1) * P],
                                      xh2[:, dc * GT:(dc + 1) * GT],
                                      start=(dc == 0), stop=(dc == 7))
                    mi.then_inc(sT, 1)
                for oo in range(16):
                    gl = g * 40 + 16 + oo
                    t.wait_ge(sWI0 if gl % 2 == 0 else sWI1, 16 * (gl // 2 + 1))
                    if oo == 0:
                        t.wait_ge(sS, S8(g, 19))
                    if oo >= 2:
                        t.wait_ge(sS, S8(g, 20 + oo - 2))
                    pm = pm0 if oo % 2 == 0 else pm1
                    for ic in range(16):
                        mi = t.matmul(pm[:],
                                      wgg8[:, (gl % 2) * 16 * P + ic * P:(gl % 2) * 16 * P + (ic + 1) * P],
                                      xin[:, ic * GT:(ic + 1) * GT],
                                      start=(ic == 0), stop=(ic == 15))
                    mi.then_inc(sT, 1)
                for dc in range(8):
                    gl = g * 40 + 32 + dc
                    t.wait_ge(sWI0 if gl % 2 == 0 else sWI1, 16 * (gl // 2 + 1))
                    if dc == 0:
                        t.wait_ge(sV, V8(g, 48))
                    if dc >= 2:
                        t.wait_ge(sS, S8(g, 36 + dc - 2))
                    pm = pm0 if dc % 2 == 0 else pm1
                    for ic in range(16):
                        mi = t.matmul(pm[:],
                                      wou8[:, (gl % 2) * 16 * P + ic * P:(gl % 2) * 16 * P + (ic + 1) * P],
                                      tsl[:, ic * GT:(ic + 1) * GT],
                                      start=(ic == 0), stop=(ic == 15))
                    mi.then_inc(sT, 1)
                t.wait_ge(sV, V8(g, 64))
                for dc in range(8):
                    mi = t.matmul(psA[:], ones_c128[:], vbuf8[:, dc * GT:(dc + 1) * GT],
                                  start=(dc == 0), stop=(dc == 7))
                mi.then_inc(sT, 1)
                for dc in range(8):
                    mi = t.matmul(psB[:], ones_c128[:], xsq8[:, dc * GT:(dc + 1) * GT],
                                  start=(dc == 0), stop=(dc == 7))
                mi.then_inc(sT, 1)
                t.wait_ge(sV, V8(g, 71))
                if g >= 1:
                    t.wait_ge(sS, S8(g - 1, 45))
                t.matmul(pr1[:], ones_row[0:1, 0:P], rbr[:], start=True, stop=True).then_inc(sT, 1)
                t.wait_ge(sV, V8(g, 72))
                if g >= 1:
                    t.wait_ge(sS, S8(g - 1, 46))
                t.matmul(pr2[:], ones_row[0:1, 0:P], mbr[:], start=True, stop=True).then_inc(sT, 1)
        N['T'] = T0 + NG8 * NTG
        DMAT.update(sWI0=16 * (NG8 * 40 // 2), sWI1=16 * (NG8 * 40 // 2))

        bar()
        b.pop()

    if BUILD_UPTO >= 10:
        for hf in range(2):
            TB = hf * 2048
            b.push()
            xqT9 = b.sb("xqT9_%d" % hf, [P, 8 * 2048], BF16)
            b.push()
            xt9 = b.sb("xt9_%d" % hf, [P, 2 * D], BF16)
            sq9 = b.sb("sq9_%d" % hf, [P, 2 * D], F32)
            xhf9 = b.sb("xhf9_%d" % hf, [P, 2 * D], F32)
            axq9 = b.sb("axq9_%d" % hf, [P, 2 * D], F32)
            sg9 = b.sb("sg9_%d" % hf, [P, 2 * D], BF16)
            c19 = b.sb("c19_%d" % hf, [P, D], F32)
            c29 = b.sb("c29_%d" % hf, [P, D], F32)
            xqt9 = b.sb("xqt9_%d" % hf, [P, 2 * D], BF16)
            nr9 = b.sb("nr9_%d" % hf, [P, 2 * 8], F32)
            qa9 = b.sb("qa9_%d" % hf, [P, 64], F32)
            scb9 = b.sb("scb9_%d" % hf, [P, 64], F32)
            qi19 = b.sb("qi19_%d" % hf, [P, 64], I32)
            qi29 = b.sb("qi29_%d" % hf, [P, 64], I32)
            sc89 = b.sb("sc89_%d" % hf, [P, 64], F32)
            dq9 = b.sb("dq9_%d" % hf, [P, 64], F32)
            rdq9 = b.sb("rdq9_%d" % hf, [P, 64], F32)
            dqh9 = b.sb("dqh9_%d" % hf, [P, 64], F32)
            ptx0 = b.ps("ptx0_%d" % hf, [P, P], BF16)
            ptx1 = b.ps("ptx1_%d" % hf, [P, P], BF16)
            ptq0 = b.ps("ptq0_%d" % hf, [P, P], BF16)
            ptq1 = b.ps("ptq1_%d" % hf, [P, P], BF16)
            T0 = N['T']; V0 = N['V']; S0 = N['S']
            NU = 16
            VLNa = {}; VLNb = {}; VPIPE = {}
            pos = V0
            for it in range(NU + 2):
                if it < NU:
                    VLNa[it] = pos; pos += 6
                if 0 <= it - 1 < NU:
                    VLNb[it - 1] = pos; pos += 3
                if 0 <= it - 2 < NU:
                    VPIPE[it - 2] = pos; pos += NPIPE
            NV_END = pos
            SEV = {}; SSQ = {}; SSQRT = {}; SABS = {}; SSGN = {}; SXQ = {}
            pos = S0
            for it in range(NU + 2):
                if it < NU:
                    SEV[it] = pos; pos += 8
                    SSQ[it] = pos; pos += 1
                if 0 <= it - 1 < NU:
                    SSQRT[it - 1] = pos; pos += 1
                    SABS[it - 1] = pos; pos += 1
                    SSGN[it - 1] = pos; pos += 1
                if 0 <= it - 2 < NU:
                    SXQ[it - 2] = pos; pos += 8
            NS_END = pos
            TXP = {}; TQP = {}
            pos = T0
            for it in range(NU + 2):
                if it < NU:
                    TXP[it] = pos; pos += 8
                if 0 <= it - 2 < NU:
                    TQP[it - 2] = pos; pos += 8
            NT_END = pos

            @blk.tensor
            def _(t, TXP=TXP, TQP=TQP, SEV=SEV, SXQ=SXQ, VPIPE=VPIPE):
                for it in range(NU + 2):
                    if it < NU:
                        u = it
                        gt = hf * 16 + u
                        for dc in range(8):
                            if dc >= 2:
                                t.wait_ge(sS, SEV[u] + (dc - 2) + 1)
                            elif u >= 1:
                                t.wait_ge(sS, SEV[u - 1] + (dc + 6) + 1)
                            pm = ptx0 if dc % 2 == 0 else ptx1
                            t.transpose(pm[:], stream[:, dc * HALF + gt * P:dc * HALF + (gt + 1) * P],
                                        eye[:]).then_inc(sT, 1)
                    if 0 <= it - 2 < NU:
                        u = it - 2
                        t.wait_ge(sV, VPIPE[u] + NPIPE)
                        for hc in range(8):
                            if hc >= 2:
                                t.wait_ge(sS, SXQ[u] + (hc - 2) + 1)
                            elif u >= 1:
                                t.wait_ge(sS, SXQ[u - 1] + (hc + 6) + 1)
                            pm = ptq0 if hc % 2 == 0 else ptq1
                            t.transpose(pm[:], xqt9[:, (u % 2) * D + hc * P:(u % 2) * D + (hc + 1) * P],
                                        eye[:]).then_inc(sT, 1)
            N['T'] = NT_END

            @blk.scalar
            def _(sc, TXP=TXP, TQP=TQP, VLNa=VLNa, VLNb=VLNb, VPIPE=VPIPE,
                  SEV=SEV, SSQ=SSQ):
                for it in range(NU + 2):
                    if it < NU:
                        u = it
                        sl = (u % 2) * D
                        for dc in range(8):
                            sc.wait_ge(sT, TXP[u] + dc + 1)
                            if dc == 0 and u >= 2:
                                sc.wait_ge(sV, VLNb[u - 2] + 3)
                            pm = ptx0 if dc % 2 == 0 else ptx1
                            sc.activation(xt9[:, sl + dc * P:sl + (dc + 1) * P], pm[:],
                                          AF.Copy).then_inc(sS, 1)
                        if u >= 2:
                            sc.wait_ge(sV, VLNa[u - 2] + 3)
                        sc.wait_ge(sS, SEV[u] + 8)
                        sc.activation(sq9[:, sl:sl + D], xt9[:, sl:sl + D],
                                      AF.Square).then_inc(sS, 1)
                    if 0 <= it - 1 < NU:
                        u = it - 1
                        sl = (u % 2) * D
                        ns = (u % 2) * 8
                        sc.wait_ge(sV, VLNa[u] + 6)
                        sc.activation(nr9[:, ns + 6:ns + 7], nr9[:, ns + 5:ns + 6],
                                      AF.Sqrt).then_inc(sS, 1)
                        sc.wait_ge(sV, VLNb[u] + 3)
                        if u >= 2:
                            sc.wait_ge(sV, VPIPE[u - 2] + 17)
                        sc.activation(axq9[:, sl:sl + D], xhf9[:, sl:sl + D],
                                      AF.Abs).then_inc(sS, 1)
                        if u >= 2:
                            sc.wait_ge(sV, VPIPE[u - 2] + NPIPE)
                        sc.activation(sg9[:, sl:sl + D], xhf9[:, sl:sl + D],
                                      AF.Sign).then_inc(sS, 1)
                    if 0 <= it - 2 < NU:
                        u = it - 2
                        for hc in range(8):
                            sc.wait_ge(sT, TQP[u] + hc + 1)
                            pm = ptq0 if hc % 2 == 0 else ptq1
                            sc.activation(xqT9[:, hc * 2048 + u * P:hc * 2048 + (u + 1) * P],
                                          pm[:], AF.Copy).then_inc(sS, 1)
            N['S'] = NS_END

            @blk.vector
            def _(v, VLNa=VLNa, VLNb=VLNb, VPIPE=VPIPE, SEV=SEV, SSQ=SSQ,
                  SSQRT=SSQRT, SABS=SABS, SSGN=SSGN, TQP=TQP):
                for it in range(NU + 2):
                    if it < NU:
                        u = it
                        sl = (u % 2) * D
                        ns = (u % 2) * 8
                        base = VLNa[u]
                        v.wait_ge(sS, SEV[u] + 8)
                        if u >= 2:
                            v.wait_ge(sV, VLNb[u - 2] + 3)
                        v.tensor_reduce(nr9[:, ns:ns + 1], xt9[:, sl:sl + D], AX,
                                        OP.add).then_inc(sV, 1)
                        v.wait_ge(sV, base + 1)
                        v.tensor_scalar_mul(nr9[:, ns + 1:ns + 2], nr9[:, ns:ns + 1],
                                            1.0 / 1024.0).then_inc(sV, 1)
                        v.wait_ge(sS, SSQ[u] + 1)
                        v.tensor_reduce(nr9[:, ns + 2:ns + 3], sq9[:, sl:sl + D], AX,
                                        OP.add).then_inc(sV, 1)
                        v.wait_ge(sV, base + 3)
                        v.tensor_scalar(nr9[:, ns + 3:ns + 4], nr9[:, ns + 2:ns + 3],
                                        1.0 / 1024.0, 1e-5, OP.mult, OP.add).then_inc(sV, 1)
                        v.tensor_tensor(nr9[:, ns + 4:ns + 5], nr9[:, ns + 1:ns + 2],
                                        nr9[:, ns + 1:ns + 2], OP.mult).then_inc(sV, 1)
                        v.wait_ge(sV, base + 5)
                        v.tensor_tensor(nr9[:, ns + 5:ns + 6], nr9[:, ns + 3:ns + 4],
                                        nr9[:, ns + 4:ns + 5], OP.subtract).then_inc(sV, 1)
                    if 0 <= it - 1 < NU:
                        u = it - 1
                        sl = (u % 2) * D
                        ns = (u % 2) * 8
                        base = VLNb[u]
                        v.wait_ge(sS, SSQRT[u] + 1)
                        v.reciprocal(nr9[:, ns + 7:ns + 8], nr9[:, ns + 6:ns + 7]).then_inc(sV, 1)
                        v.wait_ge(sV, base + 1)
                        v.tensor_tensor(nr9[:, ns + 4:ns + 5], nr9[:, ns + 1:ns + 2],
                                        nr9[:, ns + 7:ns + 8], OP.mult).then_inc(sV, 1)
                        v.wait_ge(sV, base + 2)
                        if u >= 1:
                            v.wait_ge(sS, SSGN[u - 1] + 1)
                        v.tensor_scalar(xhf9[:, sl:sl + D], xt9[:, sl:sl + D],
                                        nr9[0:P, ns + 7:ns + 8], nr9[0:P, ns + 4:ns + 5],
                                        OP.mult, OP.subtract).then_inc(sV, 1)
                    if 0 <= it - 2 < NU:
                        u = it - 2
                        v.wait_ge(sS, SABS[u] + 1)
                        if u >= 2:
                            v.wait_ge(sT, TQP[u - 2] + 8)
                        _qpipe_v(v, VPIPE[u], axq9[:, (u % 2) * D:(u % 2) * D + D],
                                 c19, c29, qa9, scb9, qi19, qi29,
                                 sc89, dq9, rdq9, dqh9, mgkc,
                                 sccol[:, 6:7], sccol[:, 8:9], 64,
                                 xqt9[:, (u % 2) * D:(u % 2) * D + D], sg9[:, (u % 2) * D:(u % 2) * D + D],
                                 sS, SSGN[u] + 1)
            N['V'] = NV_END

            bar()
            b.pop()

            b.push()
            wg9 = b.sb("wg9_%d" % hf, [P, 2 * 8 * 512], BF16)
            wu9 = b.sb("wu9_%d" % hf, [P, 2 * 8 * 512], BF16)
            hs9 = b.sb("hs9_%d" % hf, [P, 2 * 1024], F32)
            ht9 = b.sb("ht9_%d" % hf, [P, 2 * 1024], F32)
            axb = b.sb("axb_%d" % hf, [P, 2 * 1024], F32)
            sgb = b.sb("sgb_%d" % hf, [P, 2 * 1024], BF16)
            c1b = b.sb("c1b_%d" % hf, [P, 1024], F32)
            c2b = b.sb("c2b_%d" % hf, [P, 1024], F32)
            hq9 = b.sb("hq9_%d" % hf, [P, 2 * 1024], BF16)
            hqs = b.sb("hqs_%d" % hf, [P, 2 * 1024], BF16)
            qab = b.sb("qab_%d" % hf, [P, 64], F32)
            scbb = b.sb("scbb_%d" % hf, [P, 64], F32)
            qi1b = b.sb("qi1b_%d" % hf, [P, 64], I32)
            qi2b = b.sb("qi2b_%d" % hf, [P, 64], I32)
            sc8b = b.sb("sc8b_%d" % hf, [P, 64], F32)
            dqb = b.sb("dqb_%d" % hf, [P, 64], F32)
            rdqb = b.sb("rdqb_%d" % hf, [P, 64], F32)
            dqhb = b.sb("dqhb_%d" % hf, [P, 64], F32)
            pg9a = b.ps("pg9a_%d" % hf, [P, 512], F32)
            pg9b = b.ps("pg9b_%d" % hf, [P, 512], F32)
            pu9a = b.ps("pu9a_%d" % hf, [P, 512], F32)
            pu9b = b.ps("pu9b_%d" % hf, [P, 512], F32)
            pq90 = b.ps("pq90_%d" % hf, [P, P], BF16)
            pq91 = b.ps("pq91_%d" % hf, [P, P], BF16)
            sWG0 = b.sem("sWG0_%d" % hf)
            sWG1 = b.sem("sWG1_%d" % hf)
            s9H0 = b.sem("s9H0_%d" % hf)
            s9H1 = b.sem("s9H1_%d" % hf)
            T0 = N['T']; V0 = N['V']; S0 = N['S']
            NB_ = 128
            NP_ = 64
            VH = {}; VPIPE = {}
            pos = V0
            for it in range(NB_ + 2):
                if it < NB_:
                    VH[it] = pos; pos += 1
                if it % 2 == 0 and 0 <= it // 2 - 1 < NP_:
                    VPIPE[it // 2 - 1] = pos; pos += NPIPE
            NV_END = pos
            SSI = {}; SAB = {}; SSG = {}; SHE8 = {}
            pos = S0
            for it in range(NB_ + 4):
                if it < NB_:
                    SSI[it] = pos; pos += 1
                    if it % 2 == 1:
                        SAB[it // 2] = pos; pos += 1
                        SSG[it // 2] = pos; pos += 1
                if it % 2 == 0 and 0 <= it // 2 - 2 < NP_:
                    SHE8[it // 2 - 2] = pos; pos += 8
            NS_END = pos
            TG = {}; TU = {}; TP8 = {}
            pos = T0
            for it in range(NB_ + 4):
                if it < NB_:
                    TG[it] = pos; pos += 1
                    TU[it] = pos; pos += 1
                if it % 2 == 0 and 0 <= it // 2 - 2 < NP_:
                    TP8[it // 2 - 2] = pos; pos += 8
            NT_END = pos

            @blk.sync
            def _(sync, TG=TG, TU=TU):
                sync.wait_ge(sC, N['CAGU'])
                for io8 in range(8):
                    if io8 >= 2:
                        sync.wait_ge(sT, TU[(io8 - 2) * 16 + 15] + 1)
                    sl = io8 % 2
                    sync.dma_start(
                        out=wg9[:, sl * 8 * 512:(sl + 1) * 8 * 512].rearrange(
                            "p (hc q) -> p hc q", q=512),
                        in_=agg_out[io8 * 1024:(io8 + 1) * 1024, :].rearrange(
                            "(hc p) q -> p hc q", p=P)
                    ).then_inc(sWG0 if sl == 0 else sWG1, 16)
                    sync.dma_start(
                        out=wu9[:, sl * 8 * 512:(sl + 1) * 8 * 512].rearrange(
                            "p (hc q) -> p hc q", q=512),
                        in_=agu_out[io8 * 1024:(io8 + 1) * 1024, :].rearrange(
                            "(hc p) q -> p hc q", p=P)
                    ).then_inc(sWG0 if sl == 0 else sWG1, 16)

            @blk.tensor
            def _(t, TG=TG, TU=TU, TP8=TP8, SSI=SSI, SHE8=SHE8, VH=VH, VPIPE=VPIPE):
                for it in range(NB_ + 4):
                    if it < NB_:
                        u = it
                        io8, tt = divmod(u, 16)
                        sl = io8 % 2
                        us = u % 2
                        t.wait_ge(sWG0 if sl == 0 else sWG1, 32 * (io8 // 2 + 1))
                        if u >= 2:
                            t.wait_ge(sS, SSI[u - 2] + 1)
                        pg = pg9a if us == 0 else pg9b
                        for hc in range(8):
                            mi = t.matmul(pg[:],
                                          xqT9[:, hc * 2048 + tt * P:hc * 2048 + (tt + 1) * P],
                                          wg9[:, sl * 8 * 512 + hc * 512:sl * 8 * 512 + (hc + 1) * 512],
                                          start=(hc == 0), stop=(hc == 7))
                        mi.then_inc(sT, 1)
                        if u >= 2:
                            t.wait_ge(sV, VH[u - 2] + 1)
                        pu = pu9a if us == 0 else pu9b
                        for hc in range(8):
                            mi = t.matmul(pu[:],
                                          xqT9[:, hc * 2048 + tt * P:hc * 2048 + (tt + 1) * P],
                                          wu9[:, sl * 8 * 512 + hc * 512:sl * 8 * 512 + (hc + 1) * 512],
                                          start=(hc == 0), stop=(hc == 7))
                        mi.then_inc(sT, 1)
                    if it % 2 == 0 and 0 <= it // 2 - 2 < NP_:
                        p = it // 2 - 2
                        ps2 = p % 2
                        t.wait_ge(sV, VPIPE[p] + NPIPE)
                        for i8 in range(8):
                            if i8 >= 2:
                                t.wait_ge(sS, SHE8[p] + (i8 - 2) + 1)
                            elif p >= 1:
                                t.wait_ge(sS, SHE8[p - 1] + (i8 + 6) + 1)
                            pm = pq90 if i8 % 2 == 0 else pq91
                            t.transpose(pm[:], hq9[:, ps2 * 1024 + i8 * P:ps2 * 1024 + (i8 + 1) * P],
                                        eye[:]).then_inc(sT, 1)
            N['T'] = NT_END

            @blk.scalar
            def _(sc, TG=TG, TU=TU, TP8=TP8, SSI=SSI, SAB=SAB, SSG=SSG, SHE8=SHE8,
                  VH=VH, VPIPE=VPIPE):
                for it in range(NB_ + 4):
                    if it < NB_:
                        u = it
                        us = u % 2
                        ps2 = (u // 2) % 2
                        sc.wait_ge(sT, TG[u] + 1)
                        if u >= 4:
                            sc.wait_ge(sV, VH[u - 4] + 1)
                        pg = pg9a if us == 0 else pg9b
                        sc.activation(hs9[:, ps2 * 1024 + us * 512:ps2 * 1024 + (us + 1) * 512],
                                      pg[:], AF.Silu).then_inc(sS, 1)
                        if u % 2 == 1:
                            p = u // 2
                            sc.wait_ge(sV, VH[u] + 1)
                            if p >= 2:
                                sc.wait_ge(sV, VPIPE[p - 2] + 17)
                            sc.activation(axb[:, ps2 * 1024:(ps2 + 1) * 1024],
                                          ht9[:, ps2 * 1024:(ps2 + 1) * 1024],
                                          AF.Abs).then_inc(sS, 1)
                            if p >= 2:
                                sc.wait_ge(sV, VPIPE[p - 2] + NPIPE)
                            sc.activation(sgb[:, ps2 * 1024:(ps2 + 1) * 1024],
                                          ht9[:, ps2 * 1024:(ps2 + 1) * 1024],
                                          AF.Sign).then_inc(sS, 1)
                    if it % 2 == 0 and 0 <= it // 2 - 2 < NP_:
                        p = it // 2 - 2
                        ps2 = p % 2
                        io8, tt = divmod(2 * p, 16)
                        for i8 in range(8):
                            sc.wait_ge(sT, TP8[p] + i8 + 1)
                            if i8 == 0 and p >= 2:
                                sc.wait_ge(s9H0 if p % 2 == 0 else s9H1,
                                           32 * (p // 2))
                            pm = pq90 if i8 % 2 == 0 else pq91
                            sc.activation(hqs[:, ps2 * 1024 + i8 * P:ps2 * 1024 + (i8 + 1) * P],
                                          pm[:], AF.Copy).then_inc(sS, 1)
                        sc.wait_ge(sS, SHE8[p] + 8)
                        for uh in range(2):
                            sc.dma_start(
                                out=hqbuf[io8 * 512:(io8 + 1) * 512,
                                          TB + (tt + uh) * P:TB + (tt + uh + 1) * P].rearrange(
                                    "(a p) q -> p a q", p=P),
                                in_=hqs[:, ps2 * 1024 + uh * 512:ps2 * 1024 + (uh + 1) * 512].rearrange(
                                    "p (a q) -> p a q", q=P)
                            ).then_inc(s9H0 if p % 2 == 0 else s9H1, 16)
            N['S'] = NS_END

            @blk.vector
            def _(v, TG=TG, TU=TU, TP8=TP8, SSI=SSI, SAB=SAB, SSG=SSG,
                  VH=VH, VPIPE=VPIPE):
                for it in range(NB_ + 2):
                    if it < NB_:
                        u = it
                        us = u % 2
                        ps2 = (u // 2) % 2
                        v.wait_ge(sS, SSI[u] + 1)
                        v.wait_ge(sT, TU[u] + 1)
                        if u >= 4:
                            v.wait_ge(sS, SAB[u // 2 - 2] + 1)
                            v.wait_ge(sS, SSG[u // 2 - 2] + 1)
                        pu = pu9a if us == 0 else pu9b
                        v.tensor_tensor(ht9[:, ps2 * 1024 + us * 512:ps2 * 1024 + (us + 1) * 512],
                                        hs9[:, ps2 * 1024 + us * 512:ps2 * 1024 + (us + 1) * 512],
                                        pu[:], OP.mult).then_inc(sV, 1)
                    if it % 2 == 0 and 0 <= it // 2 - 1 < NP_:
                        p = it // 2 - 1
                        ps2 = p % 2
                        v.wait_ge(sS, SAB[p] + 1)
                        if p >= 2:
                            v.wait_ge(sT, TP8[p - 2] + 8)
                        _qpipe_v(v, VPIPE[p], axb[:, ps2 * 1024:(ps2 + 1) * 1024],
                                 c1b, c2b, qab, scbb, qi1b, qi2b, sc8b, dqb, rdqb,
                                 dqhb, mgkc,
                                 sccol[:, 7:8], sccol[:, 9:10], 64,
                                 hq9[:, ps2 * 1024:(ps2 + 1) * 1024],
                                 sgb[:, ps2 * 1024:(ps2 + 1) * 1024],
                                 sS, SSG[p] + 1)
            N['V'] = NV_END
            DMAT.update({("sWG0_%d" % hf): 32 * 4, ("sWG1_%d" % hf): 32 * 4,
                         ("s9H0_%d" % hf): 32 * 32, ("s9H1_%d" % hf): 32 * 32})

            bar()
            b.pop()
            b.pop()

            b.push()
            dwq9 = b.sb("dwq9_%d" % hf, [P, 32 * 1024], BF16)
            hqT9 = b.sb("hqT9_%d" % hf, [P, 2 * 32 * 512], BF16)
            pd90 = b.ps("pd90_%d" % hf, [P, 512], F32)
            pd91 = b.ps("pd91_%d" % hf, [P, 512], F32)
            sD9 = b.sem("sD9_%d" % hf)
            s9K0 = b.sem("s9K0_%d" % hf)
            s9K1 = b.sem("s9K1_%d" % hf)
            T0 = N['T']; V0 = N['V']; S0 = N['S']

            @blk.sync
            def _(sync, T0=T0, V0=V0, S0=S0):
                sync.wait_ge(sC, N['CAGD'])
                for oc in range(8):
                    sync.dma_start(
                        out=dwq9[:].rearrange("p (ic q) -> p ic q", q=1024)[:, :, oc * P:(oc + 1) * P],
                        in_=agd_out[oc * IFF:(oc + 1) * IFF, :].rearrange(
                            "(ic p) q -> p ic q", p=P)
                    ).then_inc(sD9, 16)
                for tgr in range(4):
                    if tgr >= 2:
                        sync.wait_ge(sV, V0 + (tgr - 2) * 8 + 8)
                    sl = tgr % 2
                    sync.dma_start(
                        out=hqT9[:, sl * 32 * 512:(sl + 1) * 32 * 512].rearrange(
                            "p (ic q) -> p ic q", q=512),
                        in_=hqbuf[:, TB + tgr * 512:TB + (tgr + 1) * 512].rearrange(
                            "(ic p) q -> p ic q", p=P)
                    ).then_inc(s9K0 if sl == 0 else s9K1, 16)

            @blk.tensor
            def _(t, T0=T0, V0=V0, S0=S0):
                t.wait_ge(sD9, 16 * 8)
                for u2 in range(32):
                    tgr, oc = divmod(u2, 8)
                    sl = tgr % 2
                    t.wait_ge(s9K0 if sl == 0 else s9K1, 16 * (tgr // 2 + 1))
                    if u2 >= 2:
                        t.wait_ge(sV, V0 + (u2 - 2) + 1)
                    pm = pd90 if oc % 2 == 0 else pd91
                    for ic in range(32):
                        mi = t.matmul(pm[:],
                                      dwq9[:, ic * 1024 + oc * P:ic * 1024 + (oc + 1) * P],
                                      hqT9[:, sl * 32 * 512 + ic * 512:sl * 32 * 512 + (ic + 1) * 512],
                                      start=(ic == 0), stop=(ic == 31))
                    mi.then_inc(sT, 1)
            N['T'] = T0 + 32

            @blk.vector
            def _(v, T0=T0, V0=V0, S0=S0):
                for u2 in range(32):
                    tgr, oc = divmod(u2, 8)
                    v.wait_ge(sT, T0 + u2 + 1)
                    pm = pd90 if oc % 2 == 0 else pd91
                    dst = stream[:, oc * HALF + TB + tgr * 512:oc * HALF + TB + (tgr + 1) * 512]
                    v.tensor_tensor(dst, dst, pm[:], OP.add).then_inc(sV, 1)
            N['V'] = V0 + 32
            DMAT.update({("sD9_%d" % hf): 16 * 8, ("s9K0_%d" % hf): 16 * 2,
                         ("s9K1_%d" % hf): 16 * 2})

            bar()
            b.pop()

    if BUILD_UPTO >= 11:
        b.push()
        of32 = b.sb("of32", [P, 2 * D], F32)
        po0 = b.ps("po0", [P, P], BF16)
        po1 = b.ps("po1", [P, P], BF16)
        sOUT0 = b.sem("sOUT0")
        sOUT1 = b.sem("sOUT1")
        T0 = N['T']; V0 = N['V']; S0 = N['S']

        @blk.tensor
        def _(t):
            for tt in range(32):
                for dc in range(8):
                    i = tt * 8 + dc
                    if i >= 2:
                        t.wait_ge(sS, S0 + (i - 2) + 1)
                    pm = po0 if i % 2 == 0 else po1
                    t.transpose(pm[:], stream[:, dc * HALF + tt * P:dc * HALF + (tt + 1) * P],
                                eye[:]).then_inc(sT, 1)
        N['T'] = T0 + 256

        @blk.scalar
        def _(sc):
            for tt in range(32):
                for dc in range(8):
                    i = tt * 8 + dc
                    sc.wait_ge(sT, T0 + i + 1)
                    if dc == 0 and tt >= 2:
                        sc.wait_ge(sOUT0 if tt % 2 == 0 else sOUT1, 16 * (tt // 2))
                    pm = po0 if i % 2 == 0 else po1
                    sc.activation(of32[:, (tt % 2) * D + dc * P:(tt % 2) * D + (dc + 1) * P],
                                  pm[:], AF.Copy).then_inc(sS, 1)
        N['S'] = S0 + 256

        @blk.sync
        def _(sync):
            for tt in range(32):
                sync.wait_ge(sS, S0 + tt * 8 + 8)
                sync.dma_start(out=out_d[tt * P:(tt + 1) * P, :],
                               in_=of32[:, (tt % 2) * D:(tt % 2 + 1) * D]
                               ).then_inc(sOUT0 if tt % 2 == 0 else sOUT1, 16)
            sync.wait_ge(sOUT0, 16 * 16)
            sync.wait_ge(sOUT1, 16 * 16)
        DMAT.update(sOUT0=16 * 16, sOUT1=16 * 16)

        bar()
        b.pop()

    if b.taps:
        @blk.sync
        def _(sync):
            sync.wait_ge(sQ, N['Q'])
            sync.wait_ge(sG, N['G'])
            if BUILD_UPTO >= 3:
                sync.wait_ge(b._sems["sGC0"], 16 * (NH * 16 // 2))
                sync.wait_ge(b._sems["sGC1"], 16 * (NH * 16 // 2))
            nf = 0
            tp = b.taps.get("qkv")
            if tp is not None:
                sync.dma_start(out=tp[0:64, :], in_=kbuf[0:64, 0:S]).then_inc(sC, 16); nf += 16
                sync.dma_start(out=tp[64:128, :], in_=vbuf[0:64, 0:S]).then_inc(sC, 16); nf += 16
                sync.dma_start(out=tp[128:192, :], in_=qbuf[0:64, 0:S]).then_inc(sC, 16); nf += 16
            tp = b.taps.get("gates")
            if tp is not None:
                sync.dma_start(out=tp[:], in_=gcol[:]).then_inc(sC, 16); nf += 16
            tp = b.taps.get("comp")
            if tp is not None and BUILD_UPTO >= 2:
                for h in range(NH):
                    src = ckTaE if h % 2 == 0 else ckTaO
                    sync.dma_start(out=tp[:, h * NBLK:(h + 1) * NBLK],
                                   in_=src[:, (h // 2) * NBLK:(h // 2 + 1) * NBLK]).then_inc(sC, 16)
                    nf += 16
            tp = b.taps.get("mks")
            if tp is not None and BUILD_UPTO >= 2:
                sync.dma_start(out=tp[:], in_=mksr[:]).then_inc(sC, 16); nf += 16
            tp = b.taps.get("kwin")
            if tp is not None and BUILD_UPTO >= 2:
                for h in range(NH):
                    src = kwinE if h % 2 == 0 else kwinO
                    sync.dma_start(out=tp[:, h * WIN:(h + 1) * WIN],
                                   in_=src[:, (h // 2) * WIN:(h // 2 + 1) * WIN]).then_inc(sC, 16)
                    nf += 16
            tp = b.taps.get("cvtok")
            if tp is not None and BUILD_UPTO >= 2:
                sync.dma_start(out=tp[:], in_=cvtok[:]).then_inc(sC, 16); nf += 16
            tp = b.taps.get("wvtok")
            if tp is not None and BUILD_UPTO >= 2:
                sync.dma_start(out=tp[:], in_=wvtok[:]).then_inc(sC, 16); nf += 16
            tp = b.taps.get("svtok")
            if tp is not None and BUILD_UPTO >= 4:
                sync.dma_start(out=tp[:], in_=svtok[:]).then_inc(sC, 16); nf += 16
            tp = b.taps.get("p1")
            if tp is not None and BUILD_UPTO >= 3:
                sync.dma_start(out=tp[:], in_=compbuf[:, :]).then_inc(sC, 16); nf += 16
            tp = b.taps.get("bi")
            if tp is not None and BUILD_UPTO >= 3:
                sync.dma_start(out=tp[0:P, 0:4], in_=bi4[:]).then_inc(sC, 16); nf += 16
            tp = b.taps.get("birow")
            if tp is not None and BUILD_UPTO >= 3:
                sync.dma_start(out=tp[:], in_=bi_row2[:]).then_inc(sC, 16); nf += 16
            tp = b.taps.get("sel")
            if tp is not None and BUILD_UPTO >= 3:
                sync.dma_start(out=tp[:], in_=selidx[:]).then_inc(sC, 16); nf += 16
            tp = b.taps.get("stream0")
            if tp is not None and BUILD_UPTO >= 6:
                sync.dma_start(out=tp[:], in_=stream[:]).then_inc(sC, 16); nf += 16
            tp = b.taps.get("stream1")
            if tp is not None and BUILD_UPTO >= 7:
                sync.dma_start(out=tp[:], in_=stream[:]).then_inc(sC, 16); nf += 16
            tp = b.taps.get("stream2")
            if tp is not None and BUILD_UPTO >= 8:
                sync.dma_start(out=tp[:], in_=stream[:]).then_inc(sC, 16); nf += 16
            tp = b.taps.get("wq")
            if tp is not None and BUILD_UPTO >= 9:
                sync.dma_start(out=tp[:], in_=wqsl[:]).then_inc(sC, 16); nf += 16
            tp = b.taps.get("sccolt")
            if tp is not None and BUILD_UPTO >= 9:
                sync.dma_start(out=tp[:, 0:10], in_=sccol[:, 0:10]).then_inc(sC, 16); nf += 16
            tp = b.taps.get("stream3")
            if tp is not None and BUILD_UPTO >= 10:
                sync.dma_start(out=tp[:], in_=stream[:]).then_inc(sC, 16); nf += 16
            tp = b.taps.get("att")
            if tp is not None and BUILD_UPTO >= 5:
                sync.wait_ge(b._sems["sAT0"], 16 * (NH * 16 // 2))
                sync.wait_ge(b._sems["sAT1"], 16 * (NH * 16 // 2))
                sync.dma_start(out=tp[:], in_=attfull[:, :]).then_inc(sC, 16); nf += 16
            sync.wait_ge(sC, N['C'] + nf)
            N['C'] += nf

    b.handles = dict(locals())
    return b


def _prep_core_inputs(inputs, c):
    f32 = lambda a: np.ascontiguousarray(a, dtype=np.float32)
    bi, hg = divmod(c, 2)
    x = f32(inputs["L0_attn_x"][bi])
    cols = slice(hg * 512, (hg + 1) * 512)
    wqkv = np.concatenate([f32(inputs["L0_attn_Wq"])[:, cols],
                           f32(inputs["L0_attn_Wk"])[:, cols],
                           f32(inputs["L0_attn_Wv"])[:, cols]], axis=1)
    wo = f32(inputs["L0_attn_Wo"])
    wo = np.concatenate([wo[hg * 512:(hg + 1) * 512], wo[(1 - hg) * 512:(2 - hg) * 512]], axis=0)
    pcol = lambda v: np.ascontiguousarray(np.asarray(v, np.float32).reshape(-1, P).T)
    pr = np.arange(P)
    d = dict(
        x=x,
        wqkv=np.ascontiguousarray(wqkv),
        wck=f32(inputs["L0_attn_Wck"]),
        wcv=f32(inputs["L0_attn_Wcv"]),
        bckv=np.stack([f32(inputs["L0_attn_bck"]), f32(inputs["L0_attn_bcv"])], axis=1),
        wg=f32(inputs["L0_attn_Wg"]),
        bg8=np.tile(f32(inputs["L0_attn_bg"])[None, :], (NH, 1)),
        wo=wo,
        w0t=np.ascontiguousarray(f32(inputs["L0_ffn_weight"]).T),
        w_in=f32(inputs["L1_attn_W_in"]),
        w_g1=f32(inputs["L1_attn_W_g"]),
        w_out=f32(inputs["L1_attn_W_out"]),
        b_in=pcol(inputs["L1_attn_b_in"]),
        b_g1=pcol(inputs["L1_attn_b_g"]),
        b_sp=pcol(inputs["L1_attn_b_sp"]),
        b_out=pcol(inputs["L1_attn_b_out"]),
        nw=pcol(inputs["L1_attn_norm_w"]),
        nb=pcol(inputs["L1_attn_norm_b"]),
        gate_sl=f32(inputs["L1_ffn_gate_w"])[c * 512:(c + 1) * 512],
        up_sl=f32(inputs["L1_ffn_up_w"])[c * 512:(c + 1) * 512],
        down_sl=f32(inputs["L1_ffn_down_w"])[c * P:(c + 1) * P],
        sinsd=np.array([[float(np.asarray(inputs["L1_ffn_s_in"]).ravel()[0]),
                        float(np.asarray(inputs["L1_ffn_s_in_down"]).ravel()[0]),
                        0.0, 0.0]], np.float32),
        eyed=np.eye(P, dtype=np.float32),
        ar64=np.arange(HD, dtype=np.float32)[None, :],
        posidx=np.stack([
            hg * HALF + pr,
            (1 - hg) * 512 + pr,
            pr * S + (1 - hg) * HALF,
            pr * S + hg * HALF,
        ], axis=1).astype(np.int32),
    )
    return d


def kernel(**inputs):
    b = build_nc()
    in_maps = [_prep_core_inputs(inputs, c) for c in range(8)]
    res = run_bass_kernel_spmd(b.nc, in_maps, core_ids=list(range(8)))
    b.ctx.close()
    out = np.zeros((4, S, D), dtype=np.float32)
    for c in range(8):
        bi, hg = divmod(c, 2)
        out[bi, hg * HALF:(hg + 1) * HALF] = res.results[c]["out"]
    return out


# version 08
# speedup vs baseline: 1.0022x; 1.0022x over previous
import numpy as np
import ml_dtypes
from contextlib import ExitStack

import concourse.bass as bass
import concourse.mybir as mybir
from concourse.bass_core_rust import IndirectOffsetOnAxis
from concourse.bass_utils import run_bass_kernel_spmd

F32 = mybir.dt.float32
BF16 = mybir.dt.bfloat16
I32 = mybir.dt.int32
U32 = mybir.dt.uint32
AX = mybir.AxisListType.X
AF = mybir.ActivationFunctionType
OP = mybir.AluOpType

P = 128
S = 8192
D = 1024
HD = 64
NH = 8
NBLK = 511
NSEL = 1024
WIN = 512
DIN = 2048
IFF = 4096
HALF = S // 2
NG = 16
GT = 512

MAGICF = 1.6154731354610118e+38
NPIPE = 20
PAIRS = [[0, 1], [2, 3], [4, 5], [6, 7]]
ALL8 = [[0, 1, 2, 3, 4, 5, 6, 7]]

DEBUG_TAPS: list = []
BUILD_UPTO = 99


class B:
    def __init__(self):
        self.nc = bass.Bass("TRN2", target_bir_lowering=False)
        self.ctx = ExitStack()
        self.taps = {}
        self._sems = {}
        self.scopes = [self.ctx]

    def din(self, name, shape, dt=F32):
        return self.nc.dram_tensor(name, shape, dt, kind="ExternalInput").ap()

    def sb(self, name, shape, dt):
        return self.scopes[-1].enter_context(self.nc.sbuf_tensor(name, shape, dt))

    def ps(self, name, shape, dt=F32):
        return self.scopes[-1].enter_context(self.nc.psum_tensor(name, shape, dt))

    def push(self):
        es = ExitStack()
        self.scopes.append(es)
        return es

    def pop(self):
        self.scopes.pop().close()

    def sem(self, name):
        if name not in self._sems:
            self._sems[name] = self.ctx.enter_context(self.nc.semaphore(name))
        return self._sems[name]

    def tap(self, name, shape, dt=F32):
        if name in DEBUG_TAPS:
            self.taps[name] = self.nc.dram_tensor("tap_" + name, shape, dt,
                                                  kind="ExternalOutput").ap()
        return self.taps.get(name)


def build_nc():
    b = B()
    nc = b.nc

    x_d = b.din("x", [S, D])
    wqkv_d = b.din("wqkv", [D, 1536])
    wck_d = b.din("wck", [2048, HD])
    wcv_d = b.din("wcv", [2048, HD])
    bckv_d = b.din("bckv", [HD, 2])
    wg_d = b.din("wg", [HD, 3])
    bg8_d = b.din("bg8", [NH, 3])
    wo_d = b.din("wo", [D, D])
    w0t_d = b.din("w0t", [D, D])
    w_in_d = b.din("w_in", [D, DIN])
    w_g1_d = b.din("w_g1", [DIN, DIN])
    w_out_d = b.din("w_out", [DIN, D])
    b_in_d = b.din("b_in", [P, 16])
    b_g1_d = b.din("b_g1", [P, 16])
    b_sp_d = b.din("b_sp", [P, 16])
    b_out_d = b.din("b_out", [P, 8])
    nw_d = b.din("nw", [P, 8])
    nb_d = b.din("nb", [P, 8])
    gate_sl_d = b.din("gate_sl", [512, D])
    up_sl_d = b.din("up_sl", [512, D])
    down_sl_d = b.din("down_sl", [P, IFF])
    sins_d = b.din("sinsd", [1, 4])
    eye_d = b.din("eyed", [P, P])
    ar64_d = b.din("ar64", [1, HD])
    posidx_d = b.din("posidx", [P, 4], I32)

    out_d = nc.dram_tensor("out", [HALF, D], F32, kind="ExternalOutput").ap()

    qbuf = nc.dram_tensor("qbuf", [65, NH * S], BF16).ap()
    kbuf = nc.dram_tensor("kbuf", [HD, NH * S], BF16).ap()
    vbuf = nc.dram_tensor("vbuf", [HD, NH * S], BF16).ap()
    compbuf = nc.dram_tensor("compbuf", [HD, NH * S], BF16).ap()
    attfull = nc.dram_tensor("attfull", [512, S], BF16).ap()
    wbfg = nc.dram_tensor("wbfg", [P, 16 * DIN], BF16).ap()
    wbfi = nc.dram_tensor("wbfi", [P, 8 * DIN], BF16).ap()
    wbfo = nc.dram_tensor("wbfo", [P, 16 * D], BF16).ap()
    bi_in = nc.dram_tensor("bi_in", [1, 512], F32).ap()
    bi_out = nc.dram_tensor("bi_out", [1, 512], F32).ap()
    am_in = nc.dram_tensor("am_in", [1, 4], F32).ap()
    am_out = nc.dram_tensor("am_out", [1, 4], F32).ap()
    gtmp = nc.dram_tensor("gtmp", [NH, 3], BF16).ap()
    mtmp = nc.dram_tensor("mtmp", [NH, 2], F32).ap()
    agg_in = nc.dram_tensor("agg_in", [D, 512], BF16).ap()
    agg_out = nc.dram_tensor("agg_out", [8 * D, 512], BF16).ap()
    agu_in = nc.dram_tensor("agu_in", [D, 512], BF16).ap()
    agu_out = nc.dram_tensor("agu_out", [8 * D, 512], BF16).ap()
    agd_in = nc.dram_tensor("agd_in", [IFF, P], BF16).ap()
    agd_out = nc.dram_tensor("agd_out", [8 * IFF, P], BF16).ap()
    hqbuf = nc.dram_tensor("hqbuf", [IFF, HALF], BF16).ap()
    amsc = nc.dram_tensor("amsc", [1, 16], F32).ap()
    att_in = nc.dram_tensor("att_in", [512, HALF], BF16).ap()
    att_out = nc.dram_tensor("att_out", [1024, HALF], BF16).ap()

    b.tap("qkv", [192, S], BF16)
    b.tap("gates", [HD, 24])
    b.tap("comp", [65, NH * NBLK], BF16)
    b.tap("mks", [1, NH])
    b.tap("kwin", [65, NH * WIN], BF16)
    b.tap("cvtok", [P, NH * 4 * 65], BF16)
    b.tap("wvtok", [P, NH * 4 * 65], BF16)
    b.tap("svtok", [P, NH * 8 * 65], BF16)
    b.tap("p1", [HD, NH * S], BF16)
    b.tap("bi", [P, 4])
    b.tap("birow", [1, 512])
    b.tap("sel", [P, 8], I32)
    b.tap("att", [512, S], BF16)
    b.tap("stream0", [P, 8 * HALF], BF16)
    b.tap("stream1", [P, 8 * HALF], BF16)
    b.tap("stream2", [P, 8 * HALF], BF16)
    b.tap("wq", [P, 12 * 1024], BF16)
    b.tap("sccolt", [P, 16])
    b.tap("stream3", [P, 8 * HALF], BF16)

    eye = b.sb("eye_sb", [P, P], BF16)
    hsel = b.sb("hsel", [P, 2], BF16)
    ones_row = b.sb("ones_row", [1, P], BF16)
    ones1k = b.sb("ones1k", [1, 1024], BF16)
    gcol = b.sb("gcol", [HD, 24], F32)
    grow = b.sb("grow", [1, 24], BF16)
    mksr = b.sb("mksr", [1, NH], F32)
    negmk = b.sb("negmk", [1, NH], F32)
    gsum = b.sb("gsum", [P, 4 * NG], F32)
    gsum4 = b.sb("gsum4", [P, 4], F32)
    qmean8 = b.sb("qmean8", [HD, NH], BF16)
    gates_s = b.sb("gates_s", [NH, 3], F32)
    gates_sb = b.sb("gates_sb", [NH, 3], BF16)
    swg = b.sb("swg", [HD, 3], BF16)
    sbg8 = b.sb("sbg8", [NH, 3], F32)
    sbckv = b.sb("sbckv", [HD, 2], F32)
    sins = b.sb("sins", [1, 4], F32)
    smeta = b.sb("smeta", [P, 4], I32)
    sar64 = b.sb("sar64", [1, HD], F32)
    kmax = b.sb("kmax", [HD, NH], F32)
    ckmax = b.sb("ckmax", [HD, NH], F32)
    kmax_b = b.sb("kmax_b", [HD, NH], BF16)
    mkst = b.sb("mkst", [NH, 2], F32)
    selidx = b.sb("selidx", [P, 8], I32)
    b_in_sb = b.sb("b_in_sb", [P, 16], F32)
    b_g1_sb = b.sb("b_g1_sb", [P, 16], F32)
    b_sp_sb = b.sb("b_sp_sb", [P, 16], F32)
    b_out_sb = b.sb("b_out_sb", [P, 8], F32)
    nw_sb = b.sb("nw_sb", [P, 8], F32)
    nb_sb = b.sb("nb_sb", [P, 8], F32)
    ones_c128 = b.sb("ones_c128", [P, 1], BF16)
    mgkc = b.sb("mgkc", [P, 64], F32)
    sccol = b.sb("sccol", [P, 16], F32)
    bi4 = b.sb("bi4", [P, 4], F32)
    b.push()
    selpos = b.sb("selpos", [1, NSEL], F32)
    bi_row = b.sb("bi_row", [1, 512], F32)
    bi_row2 = b.sb("bi_row2", [1, 512], F32)
    ckTaE = b.sb("ckTaE", [65, 4 * NBLK], BF16)
    ckTaO = b.sb("ckTaO", [65, 4 * NBLK], BF16)
    cvtok = b.sb("cvtok", [P, NH * 4 * 65], BF16)
    kwinE = b.sb("kwinE", [65, 4 * WIN], BF16)
    kwinO = b.sb("kwinO", [65, 4 * WIN], BF16)
    wvtok = b.sb("wvtok", [P, NH * 4 * 65], BF16)
    kselE = b.sb("kselE", [65, 4 * NSEL], BF16)
    kselO = b.sb("kselO", [65, 4 * NSEL], BF16)
    svtok = b.sb("svtok", [P, NH * 8 * 65], BF16)
    bislot = b.sb("bislot", [P, 4 * NG], F32)

    sW = b.sem("sW")
    sQ = b.sem("sQ")
    sG = b.sem("sG")
    sT = b.sem("sT")
    sV = b.sem("sV")
    sV2 = b.sem("sV2")
    sV3 = b.sem("sV3")
    sS = b.sem("sS")
    sS2 = b.sem("sS2")
    sC = b.sem("sC")
    sX0 = b.sem("sX0")
    sX1 = b.sem("sX1")
    sK0 = b.sem("sK0")
    sWQ0 = b.sem("sWQ0")
    sWQ1 = b.sem("sWQ1")
    sX2 = b.sem("sX2")
    sX3 = b.sem("sX3")
    sK1 = b.sem("sK1")

    blk = b.ctx.enter_context(nc.Block())
    sB = b.sem("sB")
    barn = [0]
    DMAT = {}

    def bar():
        barn[0] += 1
        tgt = 5 * barn[0]
        waits = list(DMAT.items())

        @blk.sync
        def _(e):
            for sem_, val_ in waits:
                e.wait_ge(b._sems[sem_], val_)
            e.drain(); e.sem_inc(sB, 1); e.wait_ge(sB, tgt)

        @blk.vector
        def _(e):
            e.drain(); e.sem_inc(sB, 1); e.wait_ge(sB, tgt)

        @blk.scalar
        def _(e):
            e.drain(); e.sem_inc(sB, 1); e.wait_ge(sB, tgt)

        @blk.tensor
        def _(e):
            e.drain(); e.sem_inc(sB, 1); e.wait_ge(sB, tgt)

        @blk.gpsimd
        def _(e):
            e.drain(); e.sem_inc(sB, 1); e.wait_ge(sB, tgt)

    N = dict(W=0, Q=0, G=0, T=0, V=0, V2=0, V3=0, S=0, S2=0, C=0)

    def _qpipe_v(v, pos0, ax, c1, c2, qa, scb, qi1, qi2, sc8q, dqq, rdqq, dqhq,
                 mgk, sc6ap, rgsap, nb, outap, sgnap, sgn_sem, sgn_val):
        wid = nb * 16
        r32 = qi1[:, 0:nb].bitcast(F32)
        t32 = qi2[:, 0:nb].bitcast(F32)
        v.wait_ge(sV, pos0)
        v.tensor_reduce(qa[:, 0:nb], ax[:, 0:wid].rearrange("p (n b) -> p n b", b=16),
                        AX, OP.max).then_inc(sV, 1)
        v.wait_ge(sV, pos0 + 1)
        v.tensor_scalar(scb[:, 0:nb], qa[:, 0:nb], sc6ap, 448.0, OP.mult,
                        OP.min).then_inc(sV, 1)
        v.wait_ge(sV, pos0 + 2)
        v.tensor_scalar(qi1[:, 0:nb], scb[:, 0:nb].bitcast(I32), 20, 1,
                        OP.logical_shift_right, OP.bitwise_and).then_inc(sV, 1)
        v.wait_ge(sV, pos0 + 3)
        v.scalar_tensor_tensor(qi2[:, 0:nb], scb[:, 0:nb].bitcast(I32), 524287,
                               qi1[:, 0:nb], OP.add, OP.add).then_inc(sV, 1)
        v.wait_ge(sV, pos0 + 4)
        v.tensor_single_scalar(sc8q[:, 0:nb].bitcast(I32), qi2[:, 0:nb], -1048576,
                               OP.bitwise_and).then_inc(sV, 1)
        v.wait_ge(sV, pos0 + 5)
        v.tensor_scalar_mul(dqq[:, 0:nb], sc8q[:, 0:nb], rgsap).then_inc(sV, 1)
        v.wait_ge(sV, pos0 + 6)
        v.tensor_scalar_max(rdqq[:, 0:nb], dqq[:, 0:nb], 1e-30).then_inc(sV, 1)
        v.wait_ge(sV, pos0 + 7)
        v.reciprocal(rdqq[:, 0:nb], rdqq[:, 0:nb]).then_inc(sV, 1)
        v.wait_ge(sV, pos0 + 6)
        v.tensor_scalar_mul(dqhq[:, 0:nb], dqq[:, 0:nb], 0.5).then_inc(sV, 1)
        v.wait_ge(sV, pos0 + 9)
        v.tensor_tensor(ax[:, 0:wid].rearrange("p (n b) -> p n b", b=16),
                        ax[:, 0:wid].rearrange("p (n b) -> p n b", b=16),
                        rdqq[:, 0:nb].unsqueeze(2).broadcast_to([P, nb, 16]),
                        OP.mult).then_inc(sV, 1)
        v.wait_ge(sV, pos0 + 10)
        v.tensor_single_scalar(c1[:, 0:wid], ax[:, 0:wid], 0.25, OP.is_ge).then_inc(sV, 1)
        for ki, mid in enumerate((0.75, 1.25, 1.75)):
            v.wait_ge(sV, pos0 + 11 + ki)
            v.scalar_tensor_tensor(c1[:, 0:wid], ax[:, 0:wid], mid, c1[:, 0:wid],
                                   OP.is_ge, OP.add).then_inc(sV, 1)
        v.wait_ge(sV, pos0 + 10)
        v.tensor_scalar(c2[:, 0:wid], ax[:, 0:wid], 5.0, 2.0, OP.is_ge,
                        OP.mult).then_inc(sV, 1)
        for ki, mid in enumerate((3.5, 2.5)):
            v.wait_ge(sV, pos0 + 15 + ki)
            v.scalar_tensor_tensor(c2[:, 0:wid], ax[:, 0:wid], mid, c2[:, 0:wid],
                                   OP.is_ge, OP.add).then_inc(sV, 1)
        v.wait_ge(sV, pos0 + 17)
        v.scalar_tensor_tensor(c1[:, 0:wid], c2[:, 0:wid], 2.0, c1[:, 0:wid],
                               OP.mult, OP.add).then_inc(sV, 1)
        v.wait_ge(sV, pos0 + 18)
        v.tensor_tensor(c1[:, 0:wid].rearrange("p (n b) -> p n b", b=16),
                        c1[:, 0:wid].rearrange("p (n b) -> p n b", b=16),
                        dqhq[:, 0:nb].unsqueeze(2).broadcast_to([P, nb, 16]),
                        OP.mult).then_inc(sV, 1)
        v.wait_ge(sV, pos0 + 19)
        v.wait_ge(sgn_sem, sgn_val)
        v.tensor_tensor(outap, c1[:, 0:wid], sgnap, OP.mult).then_inc(sV, 1)
        return pos0 + NPIPE



    wckv_sb = b.sb("wckv_sb", [HD, 2 * 32 * HD], BF16)
    b.push()
    stg0 = b.sb("stg0", [P, 2304], F32)

    @blk.sync
    def _(sync):
        for ap_out, ap_in in [
            (smeta[:], posidx_d[:]), (sins[:], sins_d[:]), (sar64[:], ar64_d[:]),
        ]:
            sync.dma_start(out=ap_out, in_=ap_in).then_inc(sW, 16); N['W'] += 16
        sync.dma_start(out=stg0[:, 0:P], in_=eye_d[:]).then_inc(sW, 16); N['W'] += 16
        for c in range(16):
            sync.dma_start(out=stg0[:, P + c * HD:P + (c + 1) * HD],
                           in_=wck_d[c * P:(c + 1) * P, :]).then_inc(sW, 16); N['W'] += 16
            sync.dma_start(out=stg0[:, P + (16 + c) * HD:P + (17 + c) * HD],
                           in_=wcv_d[c * P:(c + 1) * P, :]).then_inc(sW, 16); N['W'] += 16
        sync.dma_start(out=stg0[0:HD, 2176:2178], in_=bckv_d[:]).then_inc(sW, 16); N['W'] += 16
        sync.dma_start(out=stg0[0:HD, 2178:2181], in_=wg_d[:]).then_inc(sW, 16); N['W'] += 16
        sync.dma_start(out=stg0[0:NH, 2181:2184], in_=bg8_d[:]).then_inc(sW, 16); N['W'] += 16
        for ap_out, ap_in in [(b_in_sb[:], b_in_d[:]), (b_g1_sb[:], b_g1_d[:]),
                              (b_sp_sb[:], b_sp_d[:]), (b_out_sb[:], b_out_d[:]),
                              (nw_sb[:], nw_d[:]), (nb_sb[:], nb_d[:])]:
            sync.dma_start(out=ap_out, in_=ap_in).then_inc(sW, 16); N['W'] += 16

    @blk.vector
    def _(v):
        v.wait_ge(sW, N['W'])
        v.tensor_copy(eye[:], stg0[:, 0:P])
        for j in range(32):
            v.tensor_copy(wckv_sb[0:HD, j * HD:(j + 1) * HD],
                          stg0[(j % 2) * HD:(j % 2) * HD + HD, P + (j // 2) * HD:P + (j // 2 + 1) * HD])
            v.tensor_copy(wckv_sb[0:HD, 2048 + j * HD:2048 + (j + 1) * HD],
                          stg0[(j % 2) * HD:(j % 2) * HD + HD, P + (16 + j // 2) * HD:P + (17 + j // 2) * HD])
        v.tensor_copy(sbckv[:], stg0[0:HD, 2176:2178])
        v.tensor_copy(swg[:], stg0[0:HD, 2178:2181])
        v.tensor_copy(sbg8[:], stg0[0:NH, 2181:2184])
        v.memset(hsel[0:64, 0:1], 1.0)
        v.memset(hsel[64:128, 0:1], 0.0)
        v.memset(hsel[0:64, 1:2], 0.0)
        v.memset(hsel[64:128, 1:2], 1.0)
        v.memset(ones_row[:], 1.0)
        v.memset(ones1k[:], 1.0)
        v.memset(ones_c128[:], 1.0)
        v.memset(mgkc[:], MAGICF)
        v.memset(cvtok[:].rearrange("p (s c) -> p s c", c=65)[:, :, 64:65], 1.0)
        v.memset(cvtok[:].rearrange("p (s c) -> p s c", c=65)[:, :, 0:64], 0.0)
        v.memset(svtok[:].rearrange("p (s c) -> p s c", c=65)[:, :, 0:64], 0.0)
        v.memset(wvtok[:].rearrange("p (s c) -> p s c", c=65)[:, :, 64:65], 1.0)
        v.memset(svtok[:].rearrange("p (s c) -> p s c", c=65)[:, :, 64:65], 1.0)

    bar()
    b.pop()

    if BUILD_UPTO >= 1:
        b.push()
        xstg = b.sb("xstg", [P, 4 * D], F32)
        xb = b.sb("xb", [P, 4 * D], BF16)
        xTg = b.sb("xTg", [P, 2 * 8 * GT], BF16)
        qstg = b.sb("qstg", [P, 4 * GT], BF16)
        kstg = b.sb("kstg", [P, 4 * GT], BF16)
        vstg = b.sb("vstg", [P, 4 * GT], BF16)
        qnstg = b.sb("qnstg", [2, 4 * GT], BF16)
        qsqs = b.sb("qsqs", [P, 2 * GT], BF16)
        wqkv_sb = b.sb("wqkv_sb", [P, 8 * 1536], BF16)
        wq_st = b.sb("wq_st", [P, 2 * 1536], F32)

        ptr = b.ps("ptr", [P, 4 * 1024], BF16)
        pmm0 = b.ps("pmm0", [P, GT], F32)
        pmm1 = b.ps("pmm1", [P, GT], F32)
        pqn0 = b.ps("pqn0", [2, GT], F32)
        pqn1 = b.ps("pqn1", [2, GT], F32)

        W0 = N['W']; V0 = N['V']; T0 = N['T']; V20 = N['V2']; V30 = N['V3']
        S0 = N['S']; S20 = N['S2']; Q0 = N['Q']
        PH1_TE_ORDER = [('c', 0), ('c', 1), ('c', 2), ('q', 0), ('c', 3), ('q', 1),
                        ('c', 4), ('q', 2), ('c', 5), ('q', 3), ('c', 6), ('c', 7),
                        ('c', 8), ('c', 9), ('c', 10), ('c', 11)]
        _chain_pos = {oc: 33 + i for i, (k, oc) in enumerate(PH1_TE_ORDER) if k == 'c'}
        _qn_pos = {oc: 33 + i for i, (k, oc) in enumerate(PH1_TE_ORDER) if k == 'q'}

        def TPc(g, oc):
            return T0 + g * 48 + _chain_pos[oc]

        def TPq(g, oc):
            return T0 + g * 48 + _qn_pos[oc]

        @blk.sync
        def _(sync):
            for c in range(8):
                if c >= 2:
                    sync.wait_ge(sV, V0 + (c - 2) + 1)
                sync.dma_start(out=wq_st[:, (c % 2) * 1536:(c % 2 + 1) * 1536],
                               in_=wqkv_d[c * P:(c + 1) * P, :]).then_inc(sWQ0 if c % 2 == 0 else sWQ1, 16)
            xsems = [sX0, sX1, sX2, sX3]
            for g in range(NG):
                for tt in range(4):
                    i = 4 * g + tt
                    if i >= 4:
                        sync.wait_ge(sV, V0 + 8 + (i - 4) + 1)
                    sync.dma_start(out=xstg[:, (i % 4) * D:(i % 4 + 1) * D],
                                   in_=x_d[i * P:(i + 1) * P, :]).then_inc(xsems[i % 4], 16)

        @blk.vector
        def _(v):
            for c in range(8):
                v.wait_ge(sWQ0 if c % 2 == 0 else sWQ1, 16 * (c // 2 + 1))
                v.tensor_copy(wqkv_sb[:, c * 1536:(c + 1) * 1536],
                              wq_st[:, (c % 2) * 1536:(c % 2 + 1) * 1536]).then_inc(sV, 1)
            xsems = [sX0, sX1, sX2, sX3]
            for g in range(NG):
                gs = g % 2
                for tt in range(4):
                    i = 4 * g + tt
                    v.wait_ge(xsems[i % 4], 16 * (i // 4 + 1))
                    if i >= 4:
                        gp, tp_ = divmod(i - 4, 4)
                        v.wait_ge(sT, T0 + gp * 48 + tp_ * 8 + 8)
                    v.tensor_copy(xb[:, (i % 4) * D:(i % 4 + 1) * D],
                                  xstg[:, (i % 4) * D:(i % 4 + 1) * D]).then_inc(sV, 1)
                for m in range(8):
                    v.wait_ge(sT, T0 + g * 48 + 4 * (m + 1))
                    if g >= 2:
                        v.wait_ge(sT, TPc(g - 2, 11))
                    tt0 = (4 * m) // 8
                    dc0 = (4 * m) % 8
                    src = ptr[:].rearrange("p (s q) -> p s q", q=1024)[:, :, 0:P]
                    dst = xTg[:].rearrange("p (s q) -> p s q", q=P)[
                        :, gs * 32 + dc0 * 4 + tt0: gs * 32 + (dc0 + 3) * 4 + tt0 + 1: 4, :]
                    v.tensor_copy(dst, src).then_inc(sV2, 1)
        N['V'] = V0 + 8 + 64

        @blk.tensor
        def _(t):
            for g in range(NG):
                gs = g % 2
                for tt in range(4):
                    i = 4 * g + tt
                    t.wait_ge(sV, V0 + 8 + i + 1)
                    for dc in range(8):
                        k = tt * 8 + dc
                        j = g * 32 + k
                        slot = j % 4
                        if j >= 4:
                            t.wait_ge(sV2, V20 + (j - 4) // 4 + 1)
                        t.transpose(ptr[:, slot * 1024:slot * 1024 + P],
                                    xb[:, (i % 4) * D + dc * P:(i % 4) * D + (dc + 1) * P],
                                    eye[:]).then_inc(sT, 1)
                t.wait_ge(sV2, V20 + g * 8 + 8)
                if g == 0:
                    t.wait_ge(sV, V0 + 8)
                for item in PH1_TE_ORDER:
                    kind, oc = item
                    if kind == 'c':
                        c_g = g * 12 + oc
                        if c_g >= 2:
                            pg, poc = divmod(c_g - 2, 12)
                            t.wait_ge(sS, S0 + pg * 12 + poc + 1)
                            if poc < 4:
                                t.wait_ge(sV3, V30 + pg * 4 + poc + 1)
                        pm = pmm0 if (oc % 2 == 0) else pmm1
                        for dc in range(8):
                            mi = t.matmul(pm[:],
                                          wqkv_sb[:, dc * 1536 + oc * P:dc * 1536 + (oc + 1) * P],
                                          xTg[:, gs * 8 * GT + dc * GT:(gs * 8 + dc + 1) * GT],
                                          start=(dc == 0), stop=(dc == 7))
                        mi.then_inc(sT, 1)
                    else:
                        qn_g = g * 4 + oc
                        t.wait_ge(sV3, V30 + qn_g + 1)
                        if qn_g >= 2:
                            t.wait_ge(sS2, S20 + (qn_g - 2) + 1)
                        pq = pqn0 if (oc % 2 == 0) else pqn1
                        t.matmul(pq[:], hsel[:], qsqs[:, (oc % 2) * GT:(oc % 2 + 1) * GT],
                                 start=True, stop=True).then_inc(sT, 1)
        N['T'] = T0 + NG * 48

        N['V2'] = V20 + NG * 8

        PH1_SC_ORDER = [('o', 0), ('o', 1), ('o', 2), ('n', 0), ('o', 3), ('n', 1),
                        ('o', 4), ('n', 2), ('o', 5), ('n', 3), ('o', 6), ('o', 7),
                        ('o', 8), ('o', 9), ('o', 10), ('o', 11)]

        @blk.scalar
        def _(sc):
            nS = S0
            nS2 = S20
            for g in range(NG):
                if g >= 1:
                    sc.wait_ge(sQ, Q0 + 16 * 32 * g)
                for kind, oc in PH1_SC_ORDER:
                    if kind == 'o':
                        sc.wait_ge(sT, TPc(g, oc))
                        pm = pmm0 if (oc % 2 == 0) else pmm1
                        if oc < 4:
                            sc.activation(qstg[:, oc * GT:(oc + 1) * GT], pm[:], AF.Copy,
                                          scale=0.125,
                                          accum_out=gsum[:, oc * NG + g:oc * NG + g + 1]
                                          ).then_inc(sS, 1)
                            nS += 1
                            qn_g = g * 4 + oc
                            if qn_g >= 2:
                                gp2, op2 = divmod(qn_g - 2, 4)
                                sc.wait_ge(sT, TPq(gp2, op2))
                            sc.activation(qsqs[:, (oc % 2) * GT:(oc % 2 + 1) * GT],
                                          pm[:], AF.Square).then_inc(sV3, 1)
                            sc.wait_ge(sS, nS)
                            sc.dma_start(out=qbuf[0:64, (2 * oc) * S + g * GT:(2 * oc) * S + (g + 1) * GT],
                                         in_=qstg[0:64, oc * GT:(oc + 1) * GT]).then_inc(sQ, 16)
                            sc.dma_start(out=qbuf[0:64, (2 * oc + 1) * S + g * GT:(2 * oc + 1) * S + (g + 1) * GT],
                                         in_=qstg[64:128, oc * GT:(oc + 1) * GT]).then_inc(sQ, 16)
                        else:
                            kv = oc - 4
                            stg, buf = (kstg, kbuf) if kv < 4 else (vstg, vbuf)
                            h2 = kv % 4
                            sc.activation(stg[:, h2 * GT:(h2 + 1) * GT], pm[:], AF.Copy).then_inc(sS, 1)
                            nS += 1
                            sc.wait_ge(sS, nS)
                            sc.dma_start(out=buf[0:64, (2 * h2) * S + g * GT:(2 * h2) * S + (g + 1) * GT],
                                         in_=stg[0:64, h2 * GT:(h2 + 1) * GT]).then_inc(sQ, 16)
                            sc.dma_start(out=buf[0:64, (2 * h2 + 1) * S + g * GT:(2 * h2 + 1) * S + (g + 1) * GT],
                                         in_=stg[64:128, h2 * GT:(h2 + 1) * GT]).then_inc(sQ, 16)
                    else:
                        oc_ = oc
                        sc.wait_ge(sT, TPq(g, oc_))
                        pq = pqn0 if (oc_ % 2 == 0) else pqn1
                        sc.activation(qnstg[:, oc_ * GT:(oc_ + 1) * GT], pq[:], AF.Sqrt).then_inc(sS2, 1)
                        nS2 += 1
                        sc.wait_ge(sS2, nS2)
                        sc.dma_start(out=qbuf[64:65, (2 * oc_) * S + g * GT:(2 * oc_) * S + (g + 1) * GT],
                                     in_=qnstg[0:1, oc_ * GT:(oc_ + 1) * GT]).then_inc(sQ, 16)
                        sc.dma_start(out=qbuf[64:65, (2 * oc_ + 1) * S + g * GT:(2 * oc_ + 1) * S + (g + 1) * GT],
                                     in_=qnstg[1:2, oc_ * GT:(oc_ + 1) * GT]).then_inc(sQ, 16)
        N['S'] = S0 + NG * 12
        N['S2'] = S20 + NG * 4
        N['V3'] = V30 + NG * 4
        N['Q'] = Q0 + 16 * NG * 32
        DMAT.update(sQ=N['Q'], sX0=16 * 16, sX1=16 * 16, sX2=16 * 16, sX3=16 * 16,
                    sWQ0=16 * 4, sWQ1=16 * 4, sW=N['W'])

        bar()
        b.pop()

        b.push()
        pxa = b.ps("pxa", [P, 512], F32)

        @blk.vector
        def _(v):
            v.tensor_reduce(gsum4[:], gsum[:].rearrange("p (o g) -> p o g", g=NG),
                            AX, OP.add).then_inc(sV, 1)
            v.wait_ge(sV, N['V'] + 1)
            for oc in range(4):
                for j in range(2):
                    ci = v.tensor_copy(qmean8[:, 2 * oc + j:2 * oc + j + 1],
                                       gsum4[j * 64:(j + 1) * 64, oc:oc + 1])
            ci.then_inc(sV, 1)
        N['V'] += 2

        @blk.tensor
        def _(t):
            t.wait_ge(sV, N['V'])
            t.matmul(pxa[0:NH, 0:3], qmean8[:], swg[:], start=True, stop=True).then_inc(sT, 1)
        N['T'] += 1

        @blk.vector
        def _(v):
            v.wait_ge(sT, N['T'])
            v.scalar_tensor_tensor(gates_s[:], pxa[0:NH, 0:3], 1.0 / 1024.0, sbg8[:],
                                   OP.mult, OP.add).then_inc(sV, 1)
        N['V'] += 1

        @blk.scalar
        def _(sc):
            sc.wait_ge(sV, N['V'])
            sc.activation(gates_s[:], gates_s[:], AF.Sigmoid).then_inc(sS, 1)
        N['S'] += 1

        @blk.vector
        def _(v):
            v.wait_ge(sS, N['S'])
            v.tensor_copy(gates_sb[:], gates_s[:]).then_inc(sV, 1)
        N['V'] += 1

        @blk.gpsimd
        def _(gp):
            gp.wait_ge(sV, N['V'])
            gp.dma_start(out=gtmp[:], in_=gates_sb[:]).then_inc(sG, 16)
            gp.wait_ge(sG, N['G'] + 16)
            gp.dma_start(out=grow[0:1, :],
                         in_=gtmp[:].rearrange("p q -> (p q)").unsqueeze(0)).then_inc(sG, 16)
        N['G'] += 32

        @blk.tensor
        def _(t):
            t.wait_ge(sG, N['G'])
            t.matmul(pxa[0:HD, 8:32], ones_row[0:1, 0:HD], grow[:],
                     start=True, stop=True).then_inc(sT, 1)
        N['T'] += 1

        @blk.scalar
        def _(sc):
            sc.wait_ge(sT, N['T'])
            sc.activation(gcol[:], pxa[0:HD, 8:32], AF.Copy).then_inc(sS, 1)
        N['S'] += 1

        bar()
        b.pop()

    if BUILD_UPTO >= 2:
        b.push()
        kring = b.sb("kring", [HD, 2 * S], BF16)
        vring = b.sb("vring", [HD, 2 * S], BF16)
        cvst = b.sb("cvst", [HD, 512], BF16)
        pck = b.ps("pck", [HD, NBLK], F32)
        pcv = b.ps("pcv", [HD, NBLK], F32)
        ptp0 = b.ps("ptp0", [P, 1024], BF16)
        ptp1 = b.ps("ptp1", [P, 1024], BF16)

        W0 = N['W']; T0 = N['T']; V0 = N['V']; S0 = N['S']

        def vpos_tp_evac(gt):
            return V0 + (gt // 8) * 11 + 3 + (gt % 8) + 1

        @blk.sync
        def _(sync):
            sync.wait_ge(sQ, N['Q'])
            for h in range(NH):
                if h >= 2:
                    sync.wait_ge(sT, T0 + (h - 1) * 10)
                    sync.wait_ge(sV, V0 + (h - 1) * 11)
                sl = (h % 2) * S
                sk = sK0 if h % 2 == 0 else sK1
                sync.dma_start(out=kring[:, sl:sl + S], in_=kbuf[:, h * S:(h + 1) * S]).then_inc(sk, 16)
                sync.dma_start(out=vring[:, sl:sl + S], in_=vbuf[:, h * S:(h + 1) * S]).then_inc(sk, 16)

        @blk.tensor
        def _(t):
            for h in range(NH):
                sl = (h % 2) * S
                t.wait_ge(sK0 if h % 2 == 0 else sK1, 32 * (h // 2 + 1))
                if h >= 1:
                    t.wait_ge(sS, S0 + (h - 1) * 2 + 1)
                for j in range(32):
                    mi = t.matmul(pck[:],
                                  wckv_sb[0:HD, j * HD:(j + 1) * HD],
                                  kring[:, sl + j:sl + j + 16 * (NBLK - 1) + 1:16],
                                  start=(j == 0), stop=(j == 31))
                mi.then_inc(sT, 1)
                if h >= 1:
                    t.wait_ge(sS, S0 + (h - 1) * 2 + 2)
                for j in range(32):
                    mi = t.matmul(pcv[:],
                                  wckv_sb[0:HD, 2048 + j * HD:2048 + (j + 1) * HD],
                                  vring[:, sl + j:sl + j + 16 * (NBLK - 1) + 1:16],
                                  start=(j == 0), stop=(j == 31))
                mi.then_inc(sT, 1)
                for ch in range(4):
                    gt = h * 8 + ch
                    if gt >= 2:
                        t.wait_ge(sV, vpos_tp_evac(gt - 2))
                    pt = ptp0 if gt % 2 == 0 else ptp1
                    t.transpose(pt[:, 0:HD],
                                vring[0:HD, sl + S - WIN + ch * P:sl + S - WIN + (ch + 1) * P],
                                eye[0:HD, 0:HD]).then_inc(sT, 1)
                t.wait_ge(sS, S0 + h * 2 + 2)
                for ch in range(4):
                    gt = h * 8 + 4 + ch
                    ln = min(P, NBLK - ch * P)
                    if gt >= 2:
                        t.wait_ge(sV, vpos_tp_evac(gt - 2))
                    pt = ptp0 if gt % 2 == 0 else ptp1
                    t.transpose(pt[0:ln, 0:HD], cvst[0:HD, ch * P:ch * P + ln],
                                eye[0:HD, 0:HD]).then_inc(sT, 1)
        N['T'] = T0 + NH * 10

        @blk.scalar
        def _(sc):
            for h in range(NH):
                sc.wait_ge(sT, T0 + h * 10 + 1)
                dst = ckTaE if h % 2 == 0 else ckTaO
                sc.activation(dst[0:HD, (h // 2) * NBLK:(h // 2 + 1) * NBLK], pck[:],
                              AF.Identity, bias=sbckv[:, 0:1]).then_inc(sS, 1)
                sc.wait_ge(sT, T0 + h * 10 + 2)
                if h >= 1:
                    sc.wait_ge(sT, T0 + (h - 1) * 10 + 10)
                sc.activation(cvst[0:HD, 0:NBLK], pcv[:], AF.Identity,
                              bias=sbckv[:, 1:2]).then_inc(sS, 1)
        N['S'] = S0 + NH * 2

        @blk.vector
        def _(v):
            for h in range(NH):
                sl = (h % 2) * S
                v.wait_ge(sK0 if h % 2 == 0 else sK1, 32 * (h // 2 + 1))
                dstw = kwinE if h % 2 == 0 else kwinO
                v.tensor_copy(dstw[0:HD, (h // 2) * WIN:(h // 2 + 1) * WIN],
                              kring[:, sl + S - WIN:sl + S]).then_inc(sV, 1)
                v.reduce_max(kmax[:, h:h + 1],
                             kring[:, sl:sl + S].unsqueeze(1), AX,
                             apply_absolute_value=True).then_inc(sV, 1)
                v.wait_ge(sS, S0 + h * 2 + 1)
                dst = ckTaE if h % 2 == 0 else ckTaO
                v.reduce_max(ckmax[:, h:h + 1],
                             dst[0:HD, (h // 2) * NBLK:(h // 2 + 1) * NBLK].unsqueeze(1),
                             AX, apply_absolute_value=True).then_inc(sV, 1)
                for ch in range(4):
                    v.wait_ge(sT, T0 + h * 10 + 2 + ch + 1)
                    pt = ptp0 if (h * 8 + ch) % 2 == 0 else ptp1
                    v.tensor_copy(wvtok[:, (h * 4 + ch) * 65:(h * 4 + ch) * 65 + HD],
                                  pt[:, 0:HD]).then_inc(sV, 1)
                for ch in range(4):
                    ln = min(P, NBLK - ch * P)
                    v.wait_ge(sT, T0 + h * 10 + 6 + ch + 1)
                    pt = ptp0 if (h * 8 + 4 + ch) % 2 == 0 else ptp1
                    v.tensor_copy(cvtok[0:ln, (h * 4 + ch) * 65:(h * 4 + ch) * 65 + HD],
                                  pt[0:ln, 0:HD]).then_inc(sV, 1)
        N['V'] = V0 + NH * 11
        DMAT.update(sK0=32 * 4, sK1=32 * 4)

        bar()
        b.pop()

        b.push()
        pxc = b.ps("pxc", [NH, HD], BF16)

        @blk.vector
        def _(v):
            v.tensor_tensor(kmax[:, 0:NH], kmax[:, 0:NH], ckmax[:, 0:NH], OP.max).then_inc(sV, 1)
            v.wait_ge(sV, N['V'] + 1)
            v.tensor_copy(kmax_b[:], kmax[:, 0:NH]).then_inc(sV, 1)
        N['V'] += 2

        @blk.tensor
        def _(t):
            t.wait_ge(sV, N['V'])
            t.transpose(pxc[:], kmax_b[:], eye[0:HD, 0:HD]).then_inc(sT, 1)
        N['T'] += 1

        @blk.vector
        def _(v):
            v.wait_ge(sT, N['T'])
            v.reduce_max(mkst[:, 0:1], pxc[:].unsqueeze(1), AX).then_inc(sV, 1)
        N['V'] += 1

        @blk.gpsimd
        def _(gp):
            gp.wait_ge(sV, N['V'])
            gp.dma_start(out=mtmp[:, 0:1], in_=mkst[:, 0:1], allow_slow_non_contiguous=True).then_inc(sG, 16)
            gp.wait_ge(sG, N['G'] + 16)
            gp.dma_start(out=mksr[0:1, :],
                         in_=mtmp[:, 0:1].rearrange("p q -> (p q)").unsqueeze(0),
                         allow_slow_non_contiguous=True).then_inc(sG, 16)
        N['G'] += 32

        @blk.vector
        def _(v):
            v.wait_ge(sG, N['G'])
            v.tensor_scalar_mul(negmk[:], mksr[:], -0.125).then_inc(sV, 1)
        N['V'] += 1

        @blk.scalar
        def _(sc):
            sc.wait_ge(sV, N['V'])
            for h in range(NH):
                dst = ckTaE if h % 2 == 0 else ckTaO
                sc.activation(dst[64:65, (h // 2) * NBLK:(h // 2 + 1) * NBLK],
                              ones1k[0:1, 0:NBLK], AF.Copy, scale=negmk[0:1, h:h + 1])
                dstw = kwinE if h % 2 == 0 else kwinO
                ai = sc.activation(dstw[64:65, (h // 2) * WIN:(h // 2 + 1) * WIN],
                                   ones1k[0:1, 0:WIN], AF.Copy, scale=negmk[0:1, h:h + 1])
            ai.then_inc(sS, 1)
        N['S'] += 1

        bar()
        b.pop()


    if BUILD_UPTO >= 3:
        b.push()
        qtbuf = b.sb("qtbuf", [65, 2 * GT], BF16)
        PT = b.sb("PT", [P, 2 * 4 * GT], BF16)
        rz = b.sb("rz", [1, 2 * GT], F32)
        rzb = b.sb("rzb", [1, 2 * GT], BF16)
        sbc = b.sb("sbc", [P, 2 * GT], BF16)
        cmpn = b.sb("cmpn", [HD, 2 * GT], BF16)
        scr = b.sb("scr", [P, 2 * GT], BF16)
        biacc = b.sb("biacc", [P, 64], F32)
        bislot = bislot_ref = None
        bislot = b.sb("bislot3", [P, 64], F32)
        psc0 = b.ps("psc0", [P, GT], F32)
        psc1 = b.ps("psc1", [P, GT], F32)
        pav0 = b.ps("pav0", [65, GT], F32)
        pav1 = b.ps("pav1", [65, GT], F32)
        pbc0 = b.ps("pbc0", [P, GT], F32)
        pbc1 = b.ps("pbc1", [P, GT], F32)

        W0 = N['W']; T0 = N['T']; V0 = N['V']; S0 = N['S']; G0 = N['G']; Q0 = N['Q']
        NIT = NH * 16
        sQT0 = b.sem("sQT0")
        sQT1 = b.sem("sQT1")
        sGC0 = b.sem("sGC0")
        sGC1 = b.sem("sGC1")

        def Tpos(i, j):
            return T0 + i * 6 + j

        def Spos(i, j):
            return S0 + i * 5 + j

        def Vpos(i, j):
            h, qt = divmod(i, 16)
            return V0 + 1 + h * 113 + qt * 7 + j

        def Vheadadd(h):
            return V0 + 1 + h * 113 + 113

        @blk.sync
        def _(sync):
            sync.wait_ge(sQ, N['Q'])
            for i in range(NIT):
                h, qt = divmod(i, 16)
                sq = sQT0 if i % 2 == 0 else sQT1
                if i >= 2:
                    sync.wait_ge(sT, Tpos(i - 2, 4))
                sync.dma_start(out=qtbuf[:, (i % 2) * GT:(i % 2 + 1) * GT],
                               in_=qbuf[0:65, h * S + qt * GT:h * S + (qt + 1) * GT]
                               ).then_inc(sq, 16)

        @blk.tensor
        def _(t):
            for i in range(NIT):
                h, qt = divmod(i, 16)
                ck = ckTaE if h % 2 == 0 else ckTaO
                cb = (h // 2) * NBLK
                sq = sQT0 if i % 2 == 0 else sQT1
                t.wait_ge(sq, 16 * (i // 2 + 1))
                for ch in range(4):
                    ln = min(P, NBLK - ch * P)
                    ps = psc0 if ch % 2 == 0 else psc1
                    if i > 0 or ch >= 2:
                        pi, pch = (i, ch - 2) if ch >= 2 else (i - 1, ch + 2)
                        t.wait_ge(sS, Spos(pi, pch + 1))
                    t.matmul(ps[0:ln, :], ck[0:65, cb + ch * P:cb + ch * P + ln],
                             qtbuf[:, (i % 2) * GT:(i % 2 + 1) * GT],
                             start=True, stop=True).then_inc(sT, 1)
                pav = pav0 if i % 2 == 0 else pav1
                t.wait_ge(sS, Spos(i, 4))
                if i >= 2:
                    t.wait_ge(sV, Vpos(i - 2, 3))
                for ch in range(4):
                    ln = min(P, NBLK - ch * P)
                    mi = t.matmul(pav[:], cvtok[0:ln, (h * 4 + ch) * 65:(h * 4 + ch + 1) * 65],
                                  PT[0:ln, (i % 2) * 4 * GT + ch * GT:(i % 2) * 4 * GT + (ch + 1) * GT],
                                  start=(ch == 0), stop=(ch == 3))
                mi.then_inc(sT, 1)
                pbc = pbc0 if i % 2 == 0 else pbc1
                t.wait_ge(sV, Vpos(i, 2))
                if i >= 2:
                    t.wait_ge(sS, Spos(i - 2, 5))
                t.matmul(pbc[:], ones_row[0:1, 0:P],
                         rzb[0:1, (i % 2) * GT:(i % 2 + 1) * GT],
                         start=True, stop=True).then_inc(sT, 1)
        N['T'] = T0 + NIT * 6

        @blk.scalar
        def _(sc):
            for i in range(NIT):
                for ch in range(4):
                    ln = min(P, NBLK - ch * P)
                    ps = psc0 if ch % 2 == 0 else psc1
                    sc.wait_ge(sT, Tpos(i, ch + 1))
                    if i >= 2:
                        sc.wait_ge(sT, Tpos(i - 2, 5))
                        sc.wait_ge(sV, Vpos(i - 2, 7))
                    sc.activation(PT[0:ln, (i % 2) * 4 * GT + ch * GT:(i % 2) * 4 * GT + (ch + 1) * GT],
                                  ps[0:ln, :], AF.Exp).then_inc(sS, 1)
                pbc = pbc0 if i % 2 == 0 else pbc1
                sc.wait_ge(sT, Tpos(i, 6))
                if i >= 2:
                    sc.wait_ge(sV, Vpos(i - 2, 7))
                sc.activation(sbc[:, (i % 2) * GT:(i % 2 + 1) * GT], pbc[:],
                              AF.Copy).then_inc(sS, 1)
        N['S'] = S0 + NIT * 5

        @blk.vector
        def _(v):
            v.memset(bislot[:], 0.0)
            v.memset(biacc[:], 0.0).then_inc(sV, 1)
            for i in range(NIT):
                h, qt = divmod(i, 16)
                pav = pav0 if i % 2 == 0 else pav1
                v.wait_ge(sT, Tpos(i, 5))
                if i >= 2:
                    v.wait_ge(sT, Tpos(i - 2, 6))
                v.reciprocal(rz[0:1, (i % 2) * GT:(i % 2 + 1) * GT],
                             pav[64:65, :]).then_inc(sV, 1)
                v.wait_ge(sV, Vpos(i, 1))
                v.tensor_copy(rzb[0:1, (i % 2) * GT:(i % 2 + 1) * GT],
                              rz[0:1, (i % 2) * GT:(i % 2 + 1) * GT]).then_inc(sV, 1)
                v.wait_ge(sS, Spos(i, 5))
                if i >= 2:
                    v.wait_ge(sGC0 if i % 2 == 0 else sGC1, 16 * ((i - 2) // 2 + 1))
                v.tensor_tensor(cmpn[:, (i % 2) * GT:(i % 2 + 1) * GT],
                                pav[0:HD, :], sbc[0:HD, (i % 2) * GT:(i % 2 + 1) * GT],
                                OP.mult).then_inc(sV, 1)
                if i == 0:
                    v.wait_ge(sV, V0 + 1)
                for ch in range(4):
                    ln = min(P, NBLK - ch * P)
                    if ch >= 1:
                        v.wait_ge(sV, Vpos(i, 3 + ch))
                    v.scalar_tensor_tensor(
                        scr[0:ln, (i % 2) * GT:(i % 2 + 1) * GT],
                        PT[0:ln, (i % 2) * 4 * GT + ch * GT:(i % 2) * 4 * GT + (ch + 1) * GT],
                        1.0,
                        sbc[0:ln, (i % 2) * GT:(i % 2 + 1) * GT],
                        OP.mult, OP.mult,
                        accum_out=bislot[0:ln, ch * 16 + qt:ch * 16 + qt + 1]).then_inc(sV, 1)
                if qt == 15:
                    v.wait_ge(sV, Vpos(i, 7))
                    v.tensor_tensor(biacc[:], biacc[:], bislot[:], OP.add).then_inc(sV, 1)
        N['V'] = V0 + 1 + NH * 113

        @blk.gpsimd
        def _(gp):
            for i in range(NIT):
                h, qt = divmod(i, 16)
                gp.wait_ge(sV, Vpos(i, 3))
                gp.dma_start(out=compbuf[0:HD, h * S + qt * GT:h * S + (qt + 1) * GT],
                             in_=cmpn[:, (i % 2) * GT:(i % 2 + 1) * GT]
                             ).then_inc(sGC0 if i % 2 == 0 else sGC1, 16)
        DMAT.update(sGC0=16 * (NIT // 2), sGC1=16 * (NIT // 2),
                    sQT0=16 * (NIT // 2), sQT1=16 * (NIT // 2))

        bar()
        b.pop()

        b.push()
        m8a = b.sb("m8a", [1, 8], F32)
        m8b = b.sb("m8b", [1, 8], F32)
        i8 = b.sb("i8", [1, 16], U32)
        idxf = b.sb("idxf", [1, 16], F32)
        t16 = b.sb("t16", [1, 16], F32)
        msk = b.sb("msk", [1, 512], F32)
        selposi = b.sb("selposi", [1, NSEL], I32)
        bid = nc.dram_tensor("bid", [P, 4], F32).ap()


        @blk.vector
        def _(v):
            v.tensor_reduce(bi4[:], biacc[:].rearrange("p (c q) -> p c q", q=16),
                            AX, OP.add).then_inc(sV, 1)
        N['V'] += 1

        @blk.gpsimd
        def _(gp):
            gp.wait_ge(sV, N['V'])
            gp.dma_start(out=bid[:], in_=bi4[:]).then_inc(sG, 16)
            gp.wait_ge(sG, N['G'] + 16)
            gp.dma_start(out=bi_row[:].rearrange("o (a p) -> o a p", a=4),
                         in_=bid[:].transpose([1, 0]).unsqueeze(0),
                         allow_slow_non_contiguous=True).then_inc(sG, 16)
        N['G'] += 32

        @blk.vector
        def _(v):
            v.wait_ge(sG, N['G'])
            v.memset(bi_row[0:1, NBLK:512], 0.0).then_inc(sV, 1)
        N['V'] += 1

        @blk.gpsimd
        def _(gp):
            gp.wait_ge(sV, N['V'])
            gp.dma_start(out=bi_in[:], in_=bi_row[:]).then_inc(sG, 16)
            gp.wait_ge(sG, N['G'] + 16)
            gp.collective_compute("AllReduce", OP.add, replica_groups=PAIRS,
                                  ins=[bi_in.opt()], outs=[bi_out.opt()]).then_inc(sC, 1)
            gp.wait_ge(sC, N['C'] + 1)
            gp.dma_start(out=bi_row2[:], in_=bi_out[:]).then_inc(sG, 16)
        N['G'] += 32
        N['C'] += 1

        @blk.vector
        def _(v):
            base = N['V']
            v.wait_ge(sG, N['G'])
            v.max(m8a[:], bi_row2[0:1, 0:NBLK]).then_inc(sV, 1)
            v.wait_ge(sV, base + 1)
            v.max_index(i8[0:1, 0:8], m8a[:], bi_row2[0:1, 0:NBLK]).then_inc(sV, 1)
            v.wait_ge(sV, base + 2)
            v.tensor_scalar(msk[0:1, 0:NBLK], bi_row2[0:1, 0:NBLK], m8a[0:1, 7:8],
                            -1e30, OP.is_ge, OP.mult).then_inc(sV, 1)
            v.wait_ge(sV, base + 3)
            v.tensor_tensor(msk[0:1, 0:NBLK], msk[0:1, 0:NBLK], bi_row2[0:1, 0:NBLK],
                            OP.add).then_inc(sV, 1)
            v.wait_ge(sV, base + 4)
            v.max(m8b[:], msk[0:1, 0:NBLK]).then_inc(sV, 1)
            v.wait_ge(sV, base + 5)
            v.max_index(i8[0:1, 8:16], m8b[:], msk[0:1, 0:NBLK]).then_inc(sV, 1)
            v.wait_ge(sV, base + 6)
            v.tensor_copy(idxf[:], i8[:]).then_inc(sV, 1)
            v.wait_ge(sV, base + 7)
            v.tensor_scalar_mul(t16[:], idxf[:], 16.0).then_inc(sV, 1)
            v.wait_ge(sV, base + 8)
            v.tensor_tensor(selpos[:].rearrange("o (a r) -> o a r", r=HD),
                            t16[:].unsqueeze(2).broadcast_to([1, 16, HD]),
                            sar64[:].unsqueeze(1).broadcast_to([1, 16, HD]),
                            OP.add).then_inc(sV, 1)
            v.wait_ge(sV, base + 9)
            v.tensor_scalar_min(selpos[:], selpos[:], float(S - 1)).then_inc(sV, 1)
            v.wait_ge(sV, base + 10)
            v.tensor_copy(selposi[:], selpos[:]).then_inc(sV, 1)
        N['V'] += 11

        @blk.gpsimd
        def _(gp):
            gp.wait_ge(sV, N['V'])
            for c in range(8):
                gp.dma_start(out=selidx[:, c:c + 1],
                             in_=selposi[0:1, c * P:(c + 1) * P].rearrange("o (p q) -> o p q", q=1),
                             allow_slow_non_contiguous=True).then_inc(sG, 16)
        N['G'] += 16 * 8

        bar()
        b.pop()


    if BUILD_UPTO >= 4:
        b.push()
        xss = b.sb("xss", [P, 2 * D], F32)
        xsb = b.sb("xsb", [P, 2 * D], BF16)
        xselT = b.sb("xselT", [P, 8 * NSEL], BF16)
        wkv_sb = b.sb("wkv_sb", [P, 8 * NSEL], BF16)
        wkv_st = b.sb("wkv_st", [P, 2 * NSEL], F32)
        vst4 = b.sb("vst4", [HD, 4 * GT], BF16)
        ptr4 = b.ps("ptr4", [P, 4 * 1024], BF16)
        pk40 = b.ps("pk40", [P, GT], F32)
        pk41 = b.ps("pk41", [P, GT], F32)
        ptv40 = b.ps("ptv40", [P, 1024], BF16)
        ptv41 = b.ps("ptv41", [P, 1024], BF16)
        sW40 = b.sem("sW40")
        sW41 = b.sem("sW41")
        sGX0 = b.sem("sGX0")
        sGX1 = b.sem("sGX1")
        T0 = N['T']; V0 = N['V']; S0 = N['S']; G0 = N['G']; V20 = N['V2']

        def T4tp(c, dc):
            return T0 + c * 8 + dc + 1

        PH4_TE = []
        for _j in range(10):
            PH4_TE.append(('c', _j))
        for _jv in range(6):
            PH4_TE.append(('v', _jv))
            PH4_TE.append(('c', 10 + _jv))
        PH4_TE.append(('v', 6))
        PH4_TE.append(('v', 7))
        _pos4 = {}
        _p = 64
        for _it in PH4_TE:
            if _it[0] == 'c':
                _p += 1
                _pos4[_it] = _p
            else:
                _pos4[_it] = _p
                _p += 8

        def T4chain(j):
            return T0 + _pos4[('c', j)]

        def T4vtp(jv, k):
            return T0 + _pos4[('v', jv)] + k + 1

        def S4(j):
            return S0 + j + 1

        def V4evac(g):
            return V20 + 16 + g + 1

        @blk.sync
        def _(sync):
            for c in range(8):
                if c >= 2:
                    sync.wait_ge(sV, V0 + (c - 2) + 1)
                sync.dma_start(out=wkv_st[:, (c % 2) * D:(c % 2 + 1) * D],
                               in_=wqkv_d[c * P:(c + 1) * P, 512:1536]
                               ).then_inc(sW40 if c % 2 == 0 else sW41, 16)

        @blk.gpsimd
        def _(gp):
            gp.wait_ge(sG, G0)
            for c in range(8):
                if c >= 2:
                    gp.wait_ge(sV, V0 + 8 + (c - 2) + 1)
                gp.indirect_dma_start(
                    out=xss[:, (c % 2) * D:(c % 2 + 1) * D], out_offset=None,
                    in_=x_d[:], in_offset=IndirectOffsetOnAxis(selidx[:, c:c + 1], 0),
                ).then_inc(sGX0 if c % 2 == 0 else sGX1, 16)

        @blk.vector
        def _(v):
            for c in range(8):
                v.wait_ge(sW40 if c % 2 == 0 else sW41, 16 * (c // 2 + 1))
                v.tensor_copy(wkv_sb[:, c * NSEL:(c + 1) * NSEL],
                              wkv_st[:, (c % 2) * D:(c % 2 + 1) * D]).then_inc(sV, 1)
            for c in range(9):
                if c < 8:
                    v.wait_ge(sGX0 if c % 2 == 0 else sGX1, 16 * (c // 2 + 1))
                    if c >= 2:
                        v.wait_ge(sT, T4tp(c - 2, 7))
                    v.tensor_copy(xsb[:, (c % 2) * D:(c % 2 + 1) * D],
                                  xss[:, (c % 2) * D:(c % 2 + 1) * D]).then_inc(sV, 1)
                if c >= 1:
                    ce = c - 1
                    for m in range(2):
                        v.wait_ge(sT, T4tp(ce, 4 * m + 3))
                        src = ptr4[:].rearrange("p (s q) -> p s q", q=1024)[:, :, 0:P]
                        dst = xselT[:].rearrange("p (dc n) -> p dc n", n=NSEL)[
                            :, 4 * m:4 * m + 4, ce * P:(ce + 1) * P]
                        v.tensor_copy(dst, src).then_inc(sV2, 1)
            for jv in range(8):
                for k in range(8):
                    g = jv * 8 + k
                    hh, ch = divmod(k, 4)
                    ocv, half = divmod(jv, 2)
                    h = 2 * ocv + hh
                    v.wait_ge(sT, T4vtp(jv, k))
                    pt = ptv40 if g % 2 == 0 else ptv41
                    v.tensor_copy(svtok[:, (h * 8 + half * 4 + ch) * 65:
                                        (h * 8 + half * 4 + ch) * 65 + HD],
                                  pt[:, 0:HD]).then_inc(sV2, 1)
        N['V'] = V0 + 16

        @blk.tensor
        def _(t):
            for c in range(8):
                t.wait_ge(sV, V0 + 8 + c + 1)
                for dc in range(8):
                    j = c * 8 + dc
                    if j >= 4:
                        t.wait_ge(sV2, V20 + (j - 4) // 4 + 1)
                    t.transpose(ptr4[:, (j % 4) * 1024:(j % 4) * 1024 + P],
                                xsb[:, (c % 2) * D + dc * P:(c % 2) * D + (dc + 1) * P],
                                eye[:]).then_inc(sT, 1)
            t.wait_ge(sV, V0 + 8)
            t.wait_ge(sV2, V20 + 16)
            for kind, j in PH4_TE:
                if kind == 'c':
                    oc, half = divmod(j, 2)
                    if j >= 2:
                        t.wait_ge(sS, S4(j - 2))
                    pm = pk40 if j % 2 == 0 else pk41
                    for dc in range(8):
                        mi = t.matmul(pm[:],
                                      wkv_sb[:, dc * NSEL + oc * P:dc * NSEL + (oc + 1) * P],
                                      xselT[:, dc * NSEL + half * GT:dc * NSEL + (half + 1) * GT],
                                      start=(dc == 0), stop=(dc == 7))
                    mi.then_inc(sT, 1)
                else:
                    jv = j
                    t.wait_ge(sS, S4(8 + jv))
                    for k in range(8):
                        g = jv * 8 + k
                        hh, ch = divmod(k, 4)
                        if g >= 2:
                            t.wait_ge(sV2, V4evac(g - 2))
                        sp = (jv % 2) * 2 + hh
                        pt = ptv40 if g % 2 == 0 else ptv41
                        t.transpose(pt[:, 0:HD],
                                    vst4[0:HD, sp * GT + ch * P:sp * GT + ch * P + P],
                                    eye[0:HD, 0:HD]).then_inc(sT, 1)
        N['T'] = T0 + 64 + 16 + 64

        @blk.scalar
        def _(sc):
            for j in range(16):
                oc, half = divmod(j, 2)
                sc.wait_ge(sT, T4chain(j))
                pm = pk40 if j % 2 == 0 else pk41
                if oc < 4:
                    sc.activation(kselE[0:HD, oc * NSEL + half * GT:oc * NSEL + (half + 1) * GT],
                                  pm[0:64, :], AF.Copy)
                    sc.activation(kselO[0:HD, oc * NSEL + half * GT:oc * NSEL + (half + 1) * GT],
                                  pm[64:128, :], AF.Copy).then_inc(sS, 1)
                else:
                    jv = (oc - 4) * 2 + half
                    if jv >= 2:
                        sc.wait_ge(sT, T4vtp(jv - 2, 7))
                    sc.activation(vst4[0:HD, ((jv % 2) * 2) * GT:((jv % 2) * 2 + 1) * GT],
                                  pm[0:64, :], AF.Copy)
                    sc.activation(vst4[0:HD, ((jv % 2) * 2 + 1) * GT:((jv % 2) * 2 + 2) * GT],
                                  pm[64:128, :], AF.Copy).then_inc(sS, 1)
            for h in range(NH):
                dst = kselE if h % 2 == 0 else kselO
                ai = sc.activation(dst[64:65, (h // 2) * NSEL:(h // 2 + 1) * NSEL],
                                   ones1k[0:1, 0:NSEL], AF.Copy, scale=negmk[0:1, h:h + 1])
            ai.then_inc(sS, 1)
        N['S'] = S0 + 16 + 1
        N['V2'] = V20 + 16 + 64
        DMAT.update(sW40=16 * 4, sW41=16 * 4, sGX0=16 * 4, sGX1=16 * 4, sG=N['G'])

        bar()
        b.pop()
    if BUILD_UPTO >= 5:
        b.push()
        qtb5 = b.sb("qtb5", [65, 2 * GT], BF16)
        cmpl = b.sb("cmpl", [HD, 2 * GT], BF16)
        PTW = b.sb("PTW", [P, 2 * 4 * GT], BF16)
        PTS = b.sb("PTS", [P, 2 * 8 * GT], BF16)
        rzw = b.sb("rzw", [1, 2 * GT], F32)
        rgw = b.sb("rgw", [1, 2 * GT], F32)
        rbw = b.sb("rbw", [1, 2 * GT], BF16)
        rzs = b.sb("rzs", [1, 2 * GT], F32)
        rgs = b.sb("rgs", [1, 2 * GT], F32)
        rbs = b.sb("rbs", [1, 2 * GT], BF16)
        sbcw = b.sb("sbcw", [HD, 2 * GT], BF16)
        sbcs = b.sb("sbcs", [HD, 2 * GT], BF16)
        t1b = b.sb("t1b", [HD, 2 * GT], BF16)
        w1b = b.sb("w1b", [HD, 2 * GT], BF16)
        a2b = b.sb("a2b", [HD, 2 * GT], BF16)
        s1b = b.sb("s1b", [HD, 2 * GT], BF16)
        attb = b.sb("attb", [HD, 2 * GT], BF16)
        psc0 = b.ps("psc0_5", [P, GT], F32)
        psc1 = b.ps("psc1_5", [P, GT], F32)
        pavw0 = b.ps("pavw0", [65, GT], F32)
        pavw1 = b.ps("pavw1", [65, GT], F32)
        pavs0 = b.ps("pavs0", [65, GT], F32)
        pavs1 = b.ps("pavs1", [65, GT], F32)
        pbcw = b.ps("pbcw", [HD, GT], F32)
        pbcs = b.ps("pbcs", [HD, GT], F32)
        sQU0 = b.sem("sQU0")
        sQU1 = b.sem("sQU1")
        sCL0 = b.sem("sCL0")
        sCL1 = b.sem("sCL1")
        sAT0 = b.sem("sAT0")
        sAT1 = b.sem("sAT1")
        T0 = N['T']; V0 = N['V']; S0 = N['S']; G0 = N['G']
        NIT = NH * 16

        def Tp(i, j):
            return T0 + i * 16 + j

        def Sp(i, j):
            return S0 + i * 14 + j

        def Vp(i, j):
            return V0 + i * 11 + j

        def SposExpU(u):
            return S0 + (u // 12) * 14 + (u % 12) + 1

        @blk.sync
        def _(sync):
            for i in range(NIT):
                h, qt = divmod(i, 16)
                if i >= 2:
                    sync.wait_ge(sT, Tp(i - 2, 12))
                sync.dma_start(out=qtb5[:, (i % 2) * GT:(i % 2 + 1) * GT],
                               in_=qbuf[0:65, h * S + qt * GT:h * S + (qt + 1) * GT]
                               ).then_inc(sQU0 if i % 2 == 0 else sQU1, 16)

        @blk.gpsimd
        def _(gp):
            gp.wait_ge(b._sems["sGC0"], 16 * (NIT // 2))
            gp.wait_ge(b._sems["sGC1"], 16 * (NIT // 2))
            for i in range(NIT):
                h, qt = divmod(i, 16)
                if i >= 2:
                    gp.wait_ge(sV, Vp(i - 2, 7))
                gp.dma_start(out=cmpl[:, (i % 2) * GT:(i % 2 + 1) * GT],
                             in_=compbuf[0:HD, h * S + qt * GT:h * S + (qt + 1) * GT]
                             ).then_inc(sCL0 if i % 2 == 0 else sCL1, 16)
                gp.wait_ge(sV, Vp(i, 11))
                gp.dma_start(out=attfull[h * HD:(h + 1) * HD, qt * GT:(qt + 1) * GT],
                             in_=attb[:, (i % 2) * GT:(i % 2 + 1) * GT]
                             ).then_inc(sAT0 if i % 2 == 0 else sAT1, 16)

        @blk.tensor
        def _(t):
            for i in range(NIT):
                h, qt = divmod(i, 16)
                kw = kwinE if h % 2 == 0 else kwinO
                ks = kselE if h % 2 == 0 else kselO
                t.wait_ge(sQU0 if i % 2 == 0 else sQU1, 16 * (i // 2 + 1))
                for ch in range(4):
                    u = i * 12 + ch
                    if u >= 2:
                        t.wait_ge(sS, SposExpU(u - 2))
                    ps = psc0 if u % 2 == 0 else psc1
                    t.matmul(ps[:], kw[0:65, (h // 2) * WIN + ch * P:(h // 2) * WIN + (ch + 1) * P],
                             qtb5[:, (i % 2) * GT:(i % 2 + 1) * GT],
                             start=True, stop=True).then_inc(sT, 1)
                for ch in range(8):
                    u = i * 12 + 4 + ch
                    if u >= 2:
                        t.wait_ge(sS, SposExpU(u - 2))
                    ps = psc0 if u % 2 == 0 else psc1
                    t.matmul(ps[:], ks[0:65, (h // 2) * NSEL + ch * P:(h // 2) * NSEL + (ch + 1) * P],
                             qtb5[:, (i % 2) * GT:(i % 2 + 1) * GT],
                             start=True, stop=True).then_inc(sT, 1)
                pavw = pavw0 if i % 2 == 0 else pavw1
                t.wait_ge(sS, Sp(i, 4))
                if i >= 2:
                    t.wait_ge(sV, Vp(i - 2, 8))
                for ch in range(4):
                    mi = t.matmul(pavw[:], wvtok[:, (h * 4 + ch) * 65:(h * 4 + ch + 1) * 65],
                                  PTW[:, (i % 2) * 4 * GT + ch * GT:(i % 2) * 4 * GT + (ch + 1) * GT],
                                  start=(ch == 0), stop=(ch == 3))
                mi.then_inc(sT, 1)
                pavs = pavs0 if i % 2 == 0 else pavs1
                t.wait_ge(sS, Sp(i, 12))
                if i >= 2:
                    t.wait_ge(sV, Vp(i - 2, 10))
                for ch in range(8):
                    mi = t.matmul(pavs[:], svtok[:, (h * 8 + ch) * 65:(h * 8 + ch + 1) * 65],
                                  PTS[:, (i % 2) * 8 * GT + ch * GT:(i % 2) * 8 * GT + (ch + 1) * GT],
                                  start=(ch == 0), stop=(ch == 7))
                mi.then_inc(sT, 1)
                t.wait_ge(sV, Vp(i, 3))
                if i >= 1:
                    t.wait_ge(sS, Sp(i - 1, 13))
                t.matmul(pbcw[:], ones_row[0:1, 0:HD], rbw[0:1, (i % 2) * GT:(i % 2 + 1) * GT],
                         start=True, stop=True).then_inc(sT, 1)
                t.wait_ge(sV, Vp(i, 6))
                if i >= 1:
                    t.wait_ge(sS, Sp(i - 1, 14))
                t.matmul(pbcs[:], ones_row[0:1, 0:HD], rbs[0:1, (i % 2) * GT:(i % 2 + 1) * GT],
                         start=True, stop=True).then_inc(sT, 1)
        N['T'] = T0 + NIT * 16

        @blk.scalar
        def _(sc):
            for i in range(NIT):
                for ch in range(4):
                    sc.wait_ge(sT, Tp(i, ch + 1))
                    if i >= 2 and ch == 0:
                        sc.wait_ge(sT, Tp(i - 2, 13))
                    ps = psc0 if (i * 12 + ch) % 2 == 0 else psc1
                    sc.activation(PTW[:, (i % 2) * 4 * GT + ch * GT:(i % 2) * 4 * GT + (ch + 1) * GT],
                                  ps[:], AF.Exp).then_inc(sS, 1)
                for ch in range(8):
                    sc.wait_ge(sT, Tp(i, 4 + ch + 1))
                    if i >= 2 and ch == 0:
                        sc.wait_ge(sT, Tp(i - 2, 14))
                    ps = psc0 if (i * 12 + 4 + ch) % 2 == 0 else psc1
                    sc.activation(PTS[:, (i % 2) * 8 * GT + ch * GT:(i % 2) * 8 * GT + (ch + 1) * GT],
                                  ps[:], AF.Exp).then_inc(sS, 1)
                sc.wait_ge(sT, Tp(i, 15))
                if i >= 2:
                    sc.wait_ge(sV, Vp(i - 2, 8))
                sc.activation(sbcw[:, (i % 2) * GT:(i % 2 + 1) * GT], pbcw[:],
                              AF.Copy).then_inc(sS, 1)
                sc.wait_ge(sT, Tp(i, 16))
                if i >= 2:
                    sc.wait_ge(sV, Vp(i - 2, 10))
                sc.activation(sbcs[:, (i % 2) * GT:(i % 2 + 1) * GT], pbcs[:],
                              AF.Copy).then_inc(sS, 1)
        N['S'] = S0 + NIT * 14

        @blk.vector
        def _(v):
            for i in range(NIT):
                h, qt = divmod(i, 16)
                sl = (i % 2) * GT
                pavw = pavw0 if i % 2 == 0 else pavw1
                pavs = pavs0 if i % 2 == 0 else pavs1
                v.wait_ge(sT, Tp(i, 13))
                v.reciprocal(rzw[0:1, sl:sl + GT], pavw[64:65, :]).then_inc(sV, 1)
                v.wait_ge(sV, Vp(i, 1))
                v.tensor_scalar_mul(rgw[0:1, sl:sl + GT], rzw[0:1, sl:sl + GT],
                                    gcol[0:1, h * 3 + 2:h * 3 + 3]).then_inc(sV, 1)
                v.wait_ge(sV, Vp(i, 2))
                v.tensor_copy(rbw[0:1, sl:sl + GT], rgw[0:1, sl:sl + GT]).then_inc(sV, 1)
                v.wait_ge(sT, Tp(i, 14))
                v.reciprocal(rzs[0:1, sl:sl + GT], pavs[64:65, :]).then_inc(sV, 1)
                v.wait_ge(sV, Vp(i, 4))
                v.tensor_scalar_mul(rgs[0:1, sl:sl + GT], rzs[0:1, sl:sl + GT],
                                    gcol[0:1, h * 3 + 1:h * 3 + 2]).then_inc(sV, 1)
                v.wait_ge(sV, Vp(i, 5))
                v.tensor_copy(rbs[0:1, sl:sl + GT], rgs[0:1, sl:sl + GT]).then_inc(sV, 1)
                v.wait_ge(sCL0 if i % 2 == 0 else sCL1, 16 * (i // 2 + 1))
                v.tensor_scalar_mul(t1b[:, sl:sl + GT], cmpl[:, sl:sl + GT],
                                    gcol[0:HD, h * 3:h * 3 + 1]).then_inc(sV, 1)
                v.wait_ge(sS, Sp(i, 13))
                v.tensor_tensor(w1b[:, sl:sl + GT], pavw[0:HD, :],
                                sbcw[:, sl:sl + GT], OP.mult).then_inc(sV, 1)
                v.wait_ge(sV, Vp(i, 8))
                v.tensor_tensor(a2b[:, sl:sl + GT], t1b[:, sl:sl + GT],
                                w1b[:, sl:sl + GT], OP.add).then_inc(sV, 1)
                v.wait_ge(sS, Sp(i, 14))
                v.tensor_tensor(s1b[:, sl:sl + GT], pavs[0:HD, :],
                                sbcs[:, sl:sl + GT], OP.mult).then_inc(sV, 1)
                v.wait_ge(sV, Vp(i, 10))
                if i >= 2:
                    v.wait_ge(sAT0 if i % 2 == 0 else sAT1, 16 * ((i - 2) // 2 + 1))
                v.tensor_tensor(attb[:, sl:sl + GT], a2b[:, sl:sl + GT],
                                s1b[:, sl:sl + GT], OP.add).then_inc(sV, 1)
        N['V'] = V0 + NIT * 11
        N['G'] = G0
        DMAT.update(sQU0=16 * (NIT // 2), sQU1=16 * (NIT // 2),
                    sCL0=16 * (NIT // 2), sCL1=16 * (NIT // 2),
                    sAT0=16 * (NIT // 2), sAT1=16 * (NIT // 2))

        bar()
        b.pop()

        attx = b.sb("attx", [P, HALF], BF16)

        @blk.gpsimd
        def _(gp):
            gp.wait_ge(sAT0, 16 * (NIT // 2))
            gp.wait_ge(sAT1, 16 * (NIT // 2))
            ng = N['G']
            for c4 in range(4):
                gp.indirect_dma_start(
                    out=attx[:, :], out_offset=None,
                    in_=attfull[:],
                    in_offset=IndirectOffsetOnAxis(smeta[:, 2:3], 1),
                    element_offset=c4 * P * S,
                ).then_inc(sG, 16)
                ng += 16
                gp.wait_ge(sG, ng)
                gp.dma_start(out=att_in[c4 * P:(c4 + 1) * P, :], in_=attx[:, :]).then_inc(sG, 16)
                ng += 16
                gp.wait_ge(sG, ng)
            gp.collective_compute("AllGather", OP.bypass, replica_groups=PAIRS,
                                  ins=[att_in.opt()], outs=[att_out.opt()]).then_inc(sC, 1)
            gp.wait_ge(sC, N['C'] + 1)
        N['G'] += 128
        N['C'] += 1

    if BUILD_UPTO >= 9:
        b.push()
        seyef = b.sb("seyef", [P, P], F32)
        wslf = b.sb("wslf", [P, 2 * D], F32)
        wqsl = b.sb("wqsl", [P, 12 * D], BF16)
        amx = b.sb("amx", [P, 3], F32)
        amc = b.sb("amc", [P, 1], F32)
        arow = b.sb("arow", [1, 4], F32)
        srow = b.sb("srow", [1, 16], F32)
        zrow = b.sb("zrow", [1, 16], F32)
        ax85 = b.sb("ax85", [P, D], F32)
        sg85 = b.sb("sg85", [P, D], BF16)
        c185 = b.sb("c185", [P, D], F32)
        c285 = b.sb("c285", [P, D], F32)
        qa8 = b.sb("qa8", [P, 64], F32)
        scb8 = b.sb("scb8", [P, 64], F32)
        qi18 = b.sb("qi18", [P, 64], I32)
        qi28 = b.sb("qi28", [P, 64], I32)
        sc88 = b.sb("sc88", [P, 64], F32)
        dq8 = b.sb("dq8", [P, 64], F32)
        rdq8 = b.sb("rdq8", [P, 64], F32)
        dqh8 = b.sb("dqh8", [P, 64], F32)
        gstg = b.sb("gstg", [P, 2 * 512], BF16)
        dstg = b.sb("dstg", [P, 2 * P], BF16)
        pamr = b.ps("pamr", [3, P], F32)
        pt80 = b.ps("pt80", [P, P], BF16)
        pt81 = b.ps("pt81", [P, P], BF16)
        sA0 = b.sem("sA0")
        sA1 = b.sem("sA1")
        sEY = b.sem("sEY")
        sAW = b.sem("sAW")
        sAR = b.sem("sAR")
        sBC = b.sem("sBC")
        sAG0 = b.sem("sAG0")
        sAG1 = b.sem("sAG1")
        T0 = N['T']; V0 = N['V']; S0 = N['S']; C0 = N['C']
        CHSRC = ([gate_sl_d[t * P:(t + 1) * P, :] for t in range(4)] +
                 [up_sl_d[t * P:(t + 1) * P, :] for t in range(4)] +
                 [down_sl_d[:, t * D:(t + 1) * D] for t in range(4)])

        def VQ(ch, k):
            return V0 + 31 + NPIPE * ch + k

        def sAw(ld):
            return (sA0 if ld % 2 == 0 else sA1), 16 * (ld // 2 + 1)

        @blk.sync
        def _(sync):
            sync.dma_start(out=seyef[:], in_=eye_d[:]).then_inc(sEY, 16)
            for ld in range(12):
                if ld >= 2:
                    sync.wait_ge(sV, V0 + 2 + 2 * (ld - 2))
                sem_, _ = sAw(ld)
                sync.dma_start(out=wslf[:, (ld % 2) * D:(ld % 2 + 1) * D],
                               in_=CHSRC[ld]).then_inc(sem_, 16)
            sync.wait_ge(sC, C0 + 1)
            sync.dma_start(out=arow[:], in_=am_out[:]).then_inc(sAW, 16)
            for ch in range(12):
                ld = 12 + ch
                if ch < 2:
                    sync.wait_ge(sV, V0 + 25)
                else:
                    sync.wait_ge(sS, S0 + 2 * (ch - 2) + 2)
                sem_, _ = sAw(ld)
                sync.dma_start(out=wslf[:, (ch % 2) * D:(ch % 2 + 1) * D],
                               in_=CHSRC[ch]).then_inc(sem_, 16)
            for gu in range(16):
                sync.wait_ge(sS, S0 + 24 + gu * 4 + 4)
                dst = agg_in if gu < 8 else agu_in
                hc = gu % 8
                sync.dma_start(out=dst[hc * P:(hc + 1) * P, :],
                               in_=gstg[:, (gu % 2) * 512:(gu % 2 + 1) * 512]
                               ).then_inc(sAG0 if gu % 2 == 0 else sAG1, 16)
            for ic in range(32):
                sync.wait_ge(sS, S0 + 24 + 64 + ic + 1)
                sync.dma_start(out=agd_in[ic * P:(ic + 1) * P, :],
                               in_=dstg[:, (ic % 2) * P:(ic % 2 + 1) * P]
                               ).then_inc(sAG0 if ic % 2 == 0 else sAG1, 16)

        @blk.vector
        def _(v):
            v.memset(zrow[:], 0.0).then_inc(sV, 1)
            for ch in range(12):
                sem_, cnt = sAw(ch)
                v.wait_ge(sem_, cnt)
                if ch >= 1:
                    v.wait_ge(sV, V0 + 3 + 2 * (ch - 1))
                v.tensor_reduce(amc[:], wslf[:, (ch % 2) * D:(ch % 2 + 1) * D],
                                AX, OP.max, apply_absolute_value=True).then_inc(sV, 1)
                v.wait_ge(sV, V0 + 2 + 2 * ch)
                j = ch // 4
                if ch % 4 == 0:
                    v.tensor_copy(amx[:, j:j + 1], amc[:]).then_inc(sV, 1)
                else:
                    v.tensor_tensor(amx[:, j:j + 1], amx[:, j:j + 1], amc[:],
                                    OP.max).then_inc(sV, 1)
            v.wait_ge(sT, T0 + 1)
            v.tensor_reduce(amc[0:3, 0:1], pamr[:], AX, OP.max).then_inc(sV, 1)
            v.wait_ge(sAW, 16)
            v.reciprocal(srow[0:1, 10:13], arow[0:1, 0:3]).then_inc(sV, 1)
            v.wait_ge(sV, V0 + 27)
            v.tensor_scalar_mul(srow[0:1, 0:3], srow[0:1, 10:13], 448.0).then_inc(sV, 1)
            v.tensor_scalar_mul(srow[0:1, 3:6], arow[0:1, 0:3],
                                1.0 / 2688.0).then_inc(sV, 1)
            v.tensor_scalar_mul(srow[0:1, 6:8], sins[0:1, 0:2], 1.0 / 6.0).then_inc(sV, 1)
            v.reciprocal(srow[0:1, 8:10], sins[0:1, 0:2]).then_inc(sV, 1)
            for ch in range(12):
                j = ch // 4
                if ch == 0:
                    v.wait_ge(sBC, 16)
                v.wait_ge(sS, S0 + 2 * ch + 1)
                v.wait_ge(sS, S0 + 2 * ch + 2)
                _qpipe_v(v, VQ(ch, 0), ax85, c185, c285, qa8, scb8, qi18, qi28,
                         sc88, dq8, rdq8, dqh8, mgkc,
                         sccol[:, j:j + 1], sccol[:, 3 + j:4 + j], 64,
                         wqsl[:, ch * D:(ch + 1) * D], sg85[:],
                         sS, S0 + 2 * ch + 2)
        N['V'] = V0 + 31 + 12 * NPIPE

        @blk.scalar
        def _(sc):
            sc.wait_ge(sV, V0 + 31)
            sc.dma_start(out=amsc[0:1, 0:10], in_=srow[0:1, 0:10]).then_inc(sAW, 16)
            for ch in range(12):
                sem_, cnt = sAw(12 + ch)
                sc.wait_ge(sem_, cnt)
                if ch >= 1:
                    sc.wait_ge(sV, VQ(ch - 1, 17))
                sc.activation(ax85[:], wslf[:, (ch % 2) * D:(ch % 2 + 1) * D],
                              AF.Abs).then_inc(sS, 1)
                if ch >= 1:
                    sc.wait_ge(sV, VQ(ch - 1, NPIPE))
                sc.activation(sg85[:], wslf[:, (ch % 2) * D:(ch % 2 + 1) * D],
                              AF.Sign).then_inc(sS, 1)
            for gu in range(16):
                hc = gu % 8
                for t in range(4):
                    i = gu * 4 + t
                    sc.wait_ge(sT, T0 + 1 + i + 1)
                    if t == 0 and gu >= 2:
                        sc.wait_ge(sAG0 if gu % 2 == 0 else sAG1,
                                   16 * (gu // 2))
                    pm = pt80 if i % 2 == 0 else pt81
                    sc.activation(gstg[:, (gu % 2) * 512 + t * P:(gu % 2) * 512 + (t + 1) * P],
                                  pm[0:P, 0:P], AF.Copy).then_inc(sS, 1)
            for ic in range(32):
                i = 64 + ic
                sc.wait_ge(sT, T0 + 1 + i + 1)
                if ic >= 2:
                    sc.wait_ge(sAG0 if ic % 2 == 0 else sAG1,
                               16 * (8 + ic // 2))
                pm = pt80 if i % 2 == 0 else pt81
                sc.activation(dstg[:, (ic % 2) * P:(ic % 2 + 1) * P],
                              pm[0:P, 0:P], AF.Copy).then_inc(sS, 1)
        N['S'] = S0 + 24 + 96

        @blk.tensor
        def _(t):
            t.wait_ge(sV, V0 + 25)
            t.wait_ge(sEY, 16)
            t.transpose(pamr[:], amx[:], seyef[:]).then_inc(sT, 1)
            for i in range(96):
                if i < 64:
                    gu, tt = divmod(i, 4)
                    ch = (gu % 8) // 2 * 0 + (tt if gu < 8 else 4 + tt)
                    hc = gu % 8
                    src = wqsl[:, ch * D + hc * P:ch * D + (hc + 1) * P]
                else:
                    ic = i - 64
                    ch = 8 + ic // 8
                    src = wqsl[:, ch * D + (ic % 8) * P:ch * D + (ic % 8 + 1) * P]
                t.wait_ge(sV, VQ(ch, NPIPE))
                if i >= 2:
                    t.wait_ge(sS, S0 + 24 + (i - 2) + 1)
                pm = pt80 if i % 2 == 0 else pt81
                t.transpose(pm[0:P, 0:P], src, eye[:]).then_inc(sT, 1)
        N['T'] = T0 + 1 + 96

        @blk.gpsimd
        def _(gp):
            gp.wait_ge(sV, V0 + 1)
            gp.dma_start(out=am_in[0:1, 3:4], in_=zrow[0:1, 0:1]).then_inc(sAR, 16)
            gp.wait_ge(sV, V0 + 26)
            gp.dma_start(out=am_in[0:1, 0:3].rearrange("o p -> p o"),
                         in_=amc[0:3, 0:1],
                         allow_slow_non_contiguous=True).then_inc(sAR, 16)
            gp.wait_ge(sAR, 16 * 2)
            gp.collective_compute("AllReduce", OP.max, replica_groups=ALL8,
                                  ins=[am_in.opt()], outs=[am_out.opt()]).then_inc(sC, 1)
            gp.wait_ge(sAW, 16 * 2)
            gp.dma_start(out=sccol[:, 0:10], in_=amsc[0:1, 0:10].broadcast_to((P, 10)),
                         allow_slow_non_contiguous=True).then_inc(sBC, 16)
        N['C'] = C0 + 1
        N['AGW'] = (16 * 24, 16 * 24)
        DMAT.update(sA0=16 * 12, sA1=16 * 12, sAR=16 * 2, sAW=16 * 2, sEY=16, sBC=16,
                    sAG0=16 * 24, sAG1=16 * 24)

        b.pop()

        bar()
    if BUILD_UPTO >= 6:
        b.pop()
        stream = b.sb("stream", [P, 8 * HALF], BF16)
        b.push()
        attall = b.sb("attall", [P, 8 * HALF], BF16)
        wo_sb = b.sb("wo_sb", [P, 8 * D], BF16)
        wo_st = b.sb("wo_st", [P, 2 * D], F32)
        xst6 = b.sb("xst6", [P, 2 * D], F32)
        xb6 = b.sb("xb6", [P, 2 * D], BF16)
        ptr6 = b.ps("ptr6", [P, 4 * 1024], BF16)
        pw0 = b.ps("pw0", [P, GT], F32)
        pw1 = b.ps("pw1", [P, GT], F32)
        sX60 = b.sem("sX60")
        sX61 = b.sem("sX61")
        sW60 = b.sem("sW60")
        sW61 = b.sem("sW61")
        sAL0 = b.sem("sAL0")
        T0 = N['T']; V0 = N['V']; S0 = N['S']; G0 = N['G']
        NT6 = HALF // P

        def T6tp(i, dc):
            return T0 + i * 8 + dc + 1

        def T6chain(u):
            return T0 + NT6 * 8 + u + 1

        @blk.gpsimd
        def _(gp):
            for hc in range(4):
                gp.indirect_dma_start(
                    out=attall[:, hc * HALF:(hc + 1) * HALF], out_offset=None,
                    in_=attfull[:], in_offset=IndirectOffsetOnAxis(smeta[:, 3:4], 1),
                    element_offset=hc * P * S).then_inc(sAL0, 16)
            for hc in range(4):
                gp.indirect_dma_start(
                    out=attall[:, (4 + hc) * HALF:(5 + hc) * HALF], out_offset=None,
                    in_=att_out[:], in_offset=IndirectOffsetOnAxis(smeta[:, 1:2], 0),
                    element_offset=hc * P * HALF).then_inc(sAL0, 16)
            for i in range(NT6):
                if i >= 2:
                    gp.wait_ge(sV, V0 + 8 + (i - 2) + 1)
                gp.indirect_dma_start(
                    out=xst6[:, (i % 2) * D:(i % 2 + 1) * D], out_offset=None,
                    in_=x_d[:], in_offset=IndirectOffsetOnAxis(smeta[:, 0:1], 0),
                    element_offset=i * P * D).then_inc(sX60 if i % 2 == 0 else sX61, 16)

        @blk.sync
        def _(sync):
            for c in range(8):
                if c >= 2:
                    sync.wait_ge(sV, V0 + (c - 2) + 1)
                sync.dma_start(out=wo_st[:, (c % 2) * D:(c % 2 + 1) * D],
                               in_=wo_d[c * P:(c + 1) * P, :]
                               ).then_inc(sW60 if c % 2 == 0 else sW61, 16)

        @blk.vector
        def _(v):
            for c in range(8):
                v.wait_ge(sW60 if c % 2 == 0 else sW61, 16 * (c // 2 + 1))
                v.tensor_copy(wo_sb[:, c * D:(c + 1) * D],
                              wo_st[:, (c % 2) * D:(c % 2 + 1) * D]).then_inc(sV, 1)
            V20 = N['V2']
            for i in range(NT6 + 1):
                if i < NT6:
                    v.wait_ge(sX60 if i % 2 == 0 else sX61, 16 * (i // 2 + 1))
                    if i >= 2:
                        v.wait_ge(sT, T6tp(i - 2, 7))
                    v.tensor_copy(xb6[:, (i % 2) * D:(i % 2 + 1) * D],
                                  xst6[:, (i % 2) * D:(i % 2 + 1) * D]).then_inc(sV, 1)
                if i >= 1:
                    ce = i - 1
                    for m in range(2):
                        v.wait_ge(sT, T6tp(ce, 4 * m + 3))
                        src = ptr6[:].rearrange("p (s q) -> p s q", q=1024)[:, :, 0:P]
                        dst = stream[:].rearrange("p (dc n) -> p dc n", n=HALF)[
                            :, 4 * m:4 * m + 4, ce * P:(ce + 1) * P]
                        v.tensor_copy(dst, src).then_inc(sV2, 1)
            for u in range(64):
                tg, dc = divmod(u, 8)
                v.wait_ge(sT, T6chain(u))
                pm = pw0 if u % 2 == 0 else pw1
                v.tensor_tensor(stream[:, dc * HALF + tg * GT:dc * HALF + (tg + 1) * GT],
                                stream[:, dc * HALF + tg * GT:dc * HALF + (tg + 1) * GT],
                                pm[:], OP.add).then_inc(sV, 1)
        N['V'] = V0 + 8 + NT6 + 64

        @blk.tensor
        def _(t):
            V20 = N['V2']
            for i in range(NT6):
                t.wait_ge(sV, V0 + 8 + i + 1)
                for dc in range(8):
                    j = i * 8 + dc
                    if j >= 4:
                        t.wait_ge(sV2, V20 + (j - 4) // 4 + 1)
                    t.transpose(ptr6[:, (j % 4) * 1024:(j % 4) * 1024 + P],
                                xb6[:, (i % 2) * D + dc * P:(i % 2) * D + (dc + 1) * P],
                                eye[:]).then_inc(sT, 1)
            t.wait_ge(sV, V0 + 8)
            t.wait_ge(sAL0, 16 * 8)
            for u in range(64):
                tg, dc = divmod(u, 8)
                if u >= 2:
                    t.wait_ge(sV, V0 + 8 + NT6 + (u - 2) + 1)
                if u == 0:
                    t.wait_ge(sV2, V20 + 2 * NT6)
                pm = pw0 if u % 2 == 0 else pw1
                for hc in range(8):
                    mi = t.matmul(pm[:], wo_sb[:, hc * D + dc * P:hc * D + (dc + 1) * P],
                                  attall[:, hc * HALF + tg * GT:hc * HALF + (tg + 1) * GT],
                                  start=(hc == 0), stop=(hc == 7))
                mi.then_inc(sT, 1)
        N['T'] = T0 + NT6 * 8 + 64
        N['V2'] = N['V2'] + 2 * NT6
        DMAT.update(sX60=16 * (NT6 // 2), sX61=16 * (NT6 // 2), sW60=16 * 4, sW61=16 * 4,
                    sAL0=16 * 8)

        bar()
        b.pop()

    if BUILD_UPTO >= 7:
        b.push()
        w0sb = b.sb("w0sb", [P, 8 * D], BF16)
        w0st = b.sb("w0st", [P, 2 * D], F32)
        xsq7 = b.sb("xsq7", [P, 8 * GT], BF16)
        xhat7 = b.sb("xhat7", [P, 8 * GT], BF16)
        mrow = b.sb("mrow", [1, GT], F32)
        vrow = b.sb("vrow", [1, GT], F32)
        varr = b.sb("varr", [1, GT], F32)
        mrr = b.sb("mrr", [1, GT], F32)
        rstdf = b.sb("rstdf", [1, GT], F32)
        sqrow = b.sb("sqrow", [1, GT], F32)
        rbr = b.sb("rbr", [1, GT], BF16)
        mbr = b.sb("mbr", [1, GT], BF16)
        rstdb = b.sb("rstdb", [P, GT], BF16)
        mrb = b.sb("mrb", [P, GT], BF16)
        psA = b.ps("psA", [1, GT], F32)
        psB = b.ps("psB", [1, GT], F32)
        pr1 = b.ps("pr1", [P, GT], F32)
        pr2 = b.ps("pr2", [P, GT], F32)
        pff0 = b.ps("pff0", [P, GT], F32)
        pff1 = b.ps("pff1", [P, GT], F32)
        sW70 = b.sem("sW70")
        sW71 = b.sem("sW71")
        T0 = N['T']; V0 = N['V']; S0 = N['S']
        NG7 = HALF // GT

        def V7(g, j):
            return V0 + g * 40 + j

        def T7(g, j):
            return T0 + g * 12 + j

        def S7(g, j):
            return S0 + 8 + g * 3 + j

        @blk.sync
        def _(sync):
            for c in range(8):
                if c >= 2:
                    sync.wait_ge(sS, S0 + (c - 2) + 1)
                sync.dma_start(out=w0st[:, (c % 2) * D:(c % 2 + 1) * D],
                               in_=w0t_d[c * P:(c + 1) * P, :]
                               ).then_inc(sW70 if c % 2 == 0 else sW71, 16)

        @blk.scalar
        def _(sc):
            for c in range(8):
                sc.wait_ge(sW70 if c % 2 == 0 else sW71, 16 * (c // 2 + 1))
                sc.activation(w0sb[:, c * D:(c + 1) * D],
                              w0st[:, (c % 2) * D:(c % 2 + 1) * D], AF.Sign).then_inc(sS, 1)
            for g in range(NG7):
                sc.wait_ge(sV, V7(g, 12))
                sc.activation(sqrow[:], varr[:], AF.Sqrt).then_inc(sS, 1)
                sc.wait_ge(sT, T7(g, 3))
                if g >= 1:
                    sc.wait_ge(sV, V7(g - 1, 32))
                sc.activation(rstdb[:], pr1[:], AF.Copy).then_inc(sS, 1)
                sc.wait_ge(sT, T7(g, 4))
                sc.activation(mrb[:], pr2[:], AF.Copy).then_inc(sS, 1)
        N['S'] = S0 + 8 + NG7 * 3

        @blk.vector
        def _(v):
            for g in range(NG7):
                for dc in range(8):
                    if g >= 1 and dc == 0:
                        v.wait_ge(sT, T7(g - 1, 2))
                    src = stream[:, dc * HALF + g * GT:dc * HALF + (g + 1) * GT]
                    v.tensor_tensor(xsq7[:, dc * GT:(dc + 1) * GT], src, src,
                                    OP.mult).then_inc(sV, 1)
                v.wait_ge(sT, T7(g, 1))
                v.tensor_scalar_mul(mrow[:], psA[:], 1.0 / 1024.0).then_inc(sV, 1)
                v.wait_ge(sT, T7(g, 2))
                v.tensor_scalar_mul(vrow[:], psB[:], 1.0 / 1024.0).then_inc(sV, 1)
                v.wait_ge(sV, V7(g, 10))
                v.tensor_tensor(mrr[:], mrow[:], mrow[:], OP.mult).then_inc(sV, 1)
                v.wait_ge(sV, V7(g, 11))
                v.scalar_tensor_tensor(varr[:], vrow[:], 1e-5, mrr[:],
                                       OP.add, OP.subtract).then_inc(sV, 1)
                v.wait_ge(sS, S7(g, 1))
                v.reciprocal(rstdf[:], sqrow[:]).then_inc(sV, 1)
                v.wait_ge(sV, V7(g, 13))
                v.tensor_tensor(mrr[:], mrow[:], rstdf[:], OP.mult).then_inc(sV, 1)
                v.wait_ge(sV, V7(g, 14))
                v.tensor_copy(rbr[:], rstdf[:]).then_inc(sV, 1)
                v.tensor_copy(mbr[:], mrr[:]).then_inc(sV, 1)
                v.wait_ge(sS, S7(g, 3))
                for dc in range(8):
                    src = stream[:, dc * HALF + g * GT:dc * HALF + (g + 1) * GT]
                    v.tensor_tensor(xhat7[:, dc * GT:(dc + 1) * GT], src, rstdb[:],
                                    OP.mult).then_inc(sV, 1)
                    v.wait_ge(sV, V7(g, 17 + 2 * dc))
                    v.tensor_tensor(xhat7[:, dc * GT:(dc + 1) * GT],
                                    xhat7[:, dc * GT:(dc + 1) * GT], mrb[:],
                                    OP.subtract).then_inc(sV, 1)
                for oc in range(8):
                    v.wait_ge(sT, T7(g, 5 + oc))
                    pm = pff0 if oc % 2 == 0 else pff1
                    dst = stream[:, oc * HALF + g * GT:oc * HALF + (g + 1) * GT]
                    v.tensor_tensor(dst, dst, pm[:], OP.add).then_inc(sV, 1)
        N['V'] = V0 + NG7 * 40

        @blk.tensor
        def _(t):
            for g in range(NG7):
                if g >= 1:
                    t.wait_ge(sV, V7(g - 1, 9))
                for dc in range(8):
                    mi = t.matmul(psA[:], ones_c128[:],
                                  stream[:, dc * HALF + g * GT:dc * HALF + (g + 1) * GT],
                                  start=(dc == 0), stop=(dc == 7))
                mi.then_inc(sT, 1)
                t.wait_ge(sV, V7(g, 8))
                if g >= 1:
                    t.wait_ge(sV, V7(g - 1, 10))
                for dc in range(8):
                    mi = t.matmul(psB[:], ones_c128[:], xsq7[:, dc * GT:(dc + 1) * GT],
                                  start=(dc == 0), stop=(dc == 7))
                mi.then_inc(sT, 1)
                t.wait_ge(sV, V7(g, 15))
                if g >= 1:
                    t.wait_ge(sS, S7(g - 1, 2))
                t.matmul(pr1[:], ones_row[0:1, 0:P], rbr[:], start=True, stop=True).then_inc(sT, 1)
                t.wait_ge(sV, V7(g, 16))
                if g >= 1:
                    t.wait_ge(sS, S7(g - 1, 3))
                t.matmul(pr2[:], ones_row[0:1, 0:P], mbr[:], start=True, stop=True).then_inc(sT, 1)
                t.wait_ge(sV, V7(g, 32))
                if g == 0:
                    t.wait_ge(sS, S0 + 8)
                for oc in range(8):
                    if g * 8 + oc >= 2:
                        gp_, op_ = divmod(g * 8 + oc - 2, 8)
                        t.wait_ge(sV, V7(gp_, 33 + op_))
                    pm = pff0 if oc % 2 == 0 else pff1
                    for ic in range(8):
                        mi = t.matmul(pm[:], w0sb[:, ic * D + oc * P:ic * D + (oc + 1) * P],
                                      xhat7[:, ic * GT:(ic + 1) * GT],
                                      start=(ic == 0), stop=(ic == 7))
                    mi.then_inc(sT, 1)
        N['T'] = T0 + NG7 * 12
        DMAT.update(sW70=16 * 4, sW71=16 * 4)

        bar()
        b.pop()

    if BUILD_UPTO >= 8:
        b.push()
        wcst = b.sb("wcst", [P, 2 * DIN], F32)
        wcb = b.sb("wcb", [P, 2 * DIN], BF16)
        sW80 = b.sem("sW80")
        sW81 = b.sem("sW81")
        sQ80 = b.sem("sQ80")
        sQ81 = b.sem("sQ81")
        V0 = N['V']
        WCHUNKS = ([(w_in_d, wbfi, c, DIN) for c in range(8)] +
                   [(w_g1_d, wbfg, c, DIN) for c in range(16)] +
                   [(w_out_d, wbfo, c, D) for c in range(16)])

        @blk.sync
        def _(sync):
            for j, (src, dst, c, wd) in enumerate(WCHUNKS):
                if j >= 2:
                    sync.wait_ge(sV, V0 + (j - 2) + 1)
                sync.dma_start(out=wcst[:, (j % 2) * DIN:(j % 2) * DIN + wd],
                               in_=src[c * P:(c + 1) * P, :]
                               ).then_inc(sW80 if j % 2 == 0 else sW81, 16)

        @blk.vector
        def _(v):
            for j, (src, dst, c, wd) in enumerate(WCHUNKS):
                v.wait_ge(sW80 if j % 2 == 0 else sW81, 16 * (j // 2 + 1))
                if j >= 2:
                    v.wait_ge(sQ80 if j % 2 == 0 else sQ81, 16 * (j // 2))
                v.tensor_copy(wcb[:, (j % 2) * DIN:(j % 2) * DIN + wd],
                              wcst[:, (j % 2) * DIN:(j % 2) * DIN + wd]).then_inc(sV, 1)
        N['V'] = V0 + len(WCHUNKS)

        @blk.scalar
        def _(sc):
            for j, (src, dst, c, wd) in enumerate(WCHUNKS):
                sc.wait_ge(sV, V0 + j + 1)
                sc.dma_start(out=dst[:, c * wd:(c + 1) * wd],
                             in_=wcb[:, (j % 2) * DIN:(j % 2) * DIN + wd]
                             ).then_inc(sQ80 if j % 2 == 0 else sQ81, 16)
        DMAT.update(sW80=16 * 20, sW81=16 * 20, sQ80=16 * 20, sQ81=16 * 20)

        bar()
        b.pop()

    if BUILD_UPTO >= 8:
        b.push()
        xh2 = b.sb("xh2", [P, 8 * GT], BF16)
        xin = b.sb("xin", [P, 16 * GT], BF16)
        tsl = b.sb("tsl", [P, 16 * GT], BF16)
        vbuf8 = b.sb("vbuf8", [P, 8 * GT], BF16)
        silus = b.sb("silus", [P, 2 * GT], BF16)
        wosc = b.sb("wosc", [P, 2 * GT], BF16)
        win8 = b.sb("win8", [P, 2 * 8 * P], BF16)
        wgg8 = b.sb("wgg8", [P, 2 * 16 * P], BF16)
        wou8 = b.sb("wou8", [P, 2 * 16 * P], BF16)
        xsq8 = b.sb("xsq8", [P, 8 * GT], BF16)
        mrow = b.sb("mrow8", [1, GT], F32)
        vrow = b.sb("vrow8", [1, GT], F32)
        varr = b.sb("varr8", [1, GT], F32)
        mrr = b.sb("mrr8", [1, GT], F32)
        rstdf = b.sb("rstdf8", [1, GT], F32)
        sqrow = b.sb("sqrow8", [1, GT], F32)
        rbr = b.sb("rbr8", [1, GT], BF16)
        mbr = b.sb("mbr8", [1, GT], BF16)
        rstdb = b.sb("rstdb8", [P, GT], BF16)
        mrb = b.sb("mrb8", [P, GT], BF16)
        psA = b.ps("psA8", [1, GT], F32)
        psB = b.ps("psB8", [1, GT], F32)
        pr1 = b.ps("pr18", [P, GT], F32)
        pr2 = b.ps("pr28", [P, GT], F32)
        pm0 = b.ps("pm08", [P, GT], F32)
        pm1 = b.ps("pm18", [P, GT], F32)
        sWI0 = b.sem("sWI0")
        sWI1 = b.sem("sWI1")
        T0 = N['T']; V0 = N['V']; S0 = N['S']
        NG8 = HALF // GT

        NVG = 104
        NTG = 48
        NSG = 46

        def V8(g, j):
            return V0 + g * NVG + j

        def T8(g, j):
            return T0 + g * NTG + j

        def S8(g, j):
            return S0 + g * NSG + j

        @blk.sync
        def _(sync):
            for g in range(NG8):
                for u in range(40):
                    gl = g * 40 + u
                    if gl >= 2:
                        pu = gl - 2
                        pg_, uu = divmod(pu, 40)
                        if uu < 16:
                            sync.wait_ge(sT, T8(pg_, 5 + uu))
                        elif uu < 32:
                            sync.wait_ge(sT, T8(pg_, 21 + (uu - 16)))
                        else:
                            sync.wait_ge(sT, T8(pg_, 37 + (uu - 32)))
                    sl = gl % 2
                    if u < 16:
                        io = u
                        sync.dma_start(
                            out=win8[:, sl * 8 * P:(sl + 1) * 8 * P],
                            in_=wbfi[:].rearrange("p (dc q) -> p dc q", q=DIN)[:, :, io * P:(io + 1) * P]
                        ).then_inc(sWI0 if sl == 0 else sWI1, 16)
                    elif u < 32:
                        oo = u - 16
                        sync.dma_start(
                            out=wgg8[:, sl * 16 * P:(sl + 1) * 16 * P],
                            in_=wbfg[:].rearrange("p (ic q) -> p ic q", q=DIN)[:, :, oo * P:(oo + 1) * P]
                        ).then_inc(sWI0 if sl == 0 else sWI1, 16)
                    else:
                        dc = u - 32
                        sync.dma_start(
                            out=wou8[:, sl * 16 * P:(sl + 1) * 16 * P],
                            in_=wbfo[:].rearrange("p (ic q) -> p ic q", q=D)[:, :, dc * P:(dc + 1) * P]
                        ).then_inc(sWI0 if sl == 0 else sWI1, 16)

        @blk.vector
        def _(v):
            for g in range(NG8):
                for dc in range(8):
                    if g >= 1 and dc == 0:
                        v.wait_ge(sT, T8(g - 1, 2))
                        v.wait_ge(sV, V8(g - 1, 104))
                    src = stream[:, dc * HALF + g * GT:dc * HALF + (g + 1) * GT]
                    v.tensor_tensor(xsq8[:, dc * GT:(dc + 1) * GT], src, src,
                                    OP.mult).then_inc(sV, 1)
                v.wait_ge(sT, T8(g, 1))
                v.tensor_scalar_mul(mrow[:], psA[:], 1.0 / 1024.0).then_inc(sV, 1)
                v.wait_ge(sT, T8(g, 2))
                v.tensor_scalar_mul(vrow[:], psB[:], 1.0 / 1024.0).then_inc(sV, 1)
                v.wait_ge(sV, V8(g, 10))
                v.tensor_tensor(mrr[:], mrow[:], mrow[:], OP.mult).then_inc(sV, 1)
                v.wait_ge(sV, V8(g, 11))
                v.scalar_tensor_tensor(varr[:], vrow[:], 1e-5, mrr[:],
                                       OP.add, OP.subtract).then_inc(sV, 1)
                v.wait_ge(sS, S8(g, 1))
                v.reciprocal(rstdf[:], sqrow[:]).then_inc(sV, 1)
                v.wait_ge(sV, V8(g, 13))
                v.tensor_tensor(mrr[:], mrow[:], rstdf[:], OP.mult).then_inc(sV, 1)
                v.wait_ge(sV, V8(g, 14))
                v.tensor_copy(rbr[:], rstdf[:]).then_inc(sV, 1)
                v.tensor_copy(mbr[:], mrr[:]).then_inc(sV, 1)
                v.wait_ge(sS, S8(g, 3))
                for dc in range(8):
                    src = stream[:, dc * HALF + g * GT:dc * HALF + (g + 1) * GT]
                    v.tensor_tensor(xh2[:, dc * GT:(dc + 1) * GT], src, rstdb[:],
                                    OP.mult).then_inc(sV, 1)
                    v.wait_ge(sV, V8(g, 17 + 2 * dc))
                    v.tensor_tensor(xh2[:, dc * GT:(dc + 1) * GT],
                                    xh2[:, dc * GT:(dc + 1) * GT], mrb[:],
                                    OP.subtract).then_inc(sV, 1)
                for oo in range(16):
                    v.wait_ge(sS, S8(g, 20 + oo))
                    v.tensor_scalar_mul(tsl[:, oo * GT:(oo + 1) * GT],
                                        silus[:, (oo % 2) * GT:(oo % 2 + 1) * GT],
                                        b_sp_sb[:, oo:oo + 1]).then_inc(sV, 1)
                for dc in range(8):
                    v.wait_ge(sS, S8(g, 36 + dc))
                    v.tensor_tensor(vbuf8[:, dc * GT:(dc + 1) * GT],
                                    xh2[:, dc * GT:(dc + 1) * GT],
                                    wosc[:, (dc % 2) * GT:(dc % 2 + 1) * GT],
                                    OP.add).then_inc(sV, 1)
                v.wait_ge(sV, V8(g, 56))
                for dc in range(8):
                    v.tensor_tensor(xsq8[:, dc * GT:(dc + 1) * GT],
                                    vbuf8[:, dc * GT:(dc + 1) * GT],
                                    vbuf8[:, dc * GT:(dc + 1) * GT], OP.mult).then_inc(sV, 1)
                v.wait_ge(sT, T8(g, 45))
                v.tensor_scalar_mul(mrow[:], psA[:], 1.0 / 1024.0).then_inc(sV, 1)
                v.wait_ge(sT, T8(g, 46))
                v.tensor_scalar_mul(vrow[:], psB[:], 1.0 / 1024.0).then_inc(sV, 1)
                v.wait_ge(sV, V8(g, 66))
                v.tensor_tensor(mrr[:], mrow[:], mrow[:], OP.mult).then_inc(sV, 1)
                v.wait_ge(sV, V8(g, 67))
                v.scalar_tensor_tensor(varr[:], vrow[:], 1e-5, mrr[:],
                                       OP.add, OP.subtract).then_inc(sV, 1)
                v.wait_ge(sS, S8(g, 44))
                v.reciprocal(rstdf[:], sqrow[:]).then_inc(sV, 1)
                v.wait_ge(sV, V8(g, 69))
                v.tensor_tensor(mrr[:], mrow[:], rstdf[:], OP.mult).then_inc(sV, 1)
                v.wait_ge(sV, V8(g, 70))
                v.tensor_copy(rbr[:], rstdf[:]).then_inc(sV, 1)
                v.tensor_copy(mbr[:], mrr[:]).then_inc(sV, 1)
                v.wait_ge(sS, S8(g, 46))
                for dc in range(8):
                    base = V8(g, 73 + 4 * dc)
                    v.tensor_tensor(xsq8[:, dc * GT:(dc + 1) * GT],
                                    vbuf8[:, dc * GT:(dc + 1) * GT], rstdb[:],
                                    OP.mult).then_inc(sV, 1)
                    v.wait_ge(sV, base)
                    v.tensor_tensor(xsq8[:, dc * GT:(dc + 1) * GT],
                                    xsq8[:, dc * GT:(dc + 1) * GT], mrb[:],
                                    OP.subtract).then_inc(sV, 1)
                    v.wait_ge(sV, base + 1)
                    v.tensor_scalar(xsq8[:, dc * GT:(dc + 1) * GT],
                                    xsq8[:, dc * GT:(dc + 1) * GT],
                                    nw_sb[:, dc:dc + 1], nb_sb[:, dc:dc + 1],
                                    OP.mult, OP.add).then_inc(sV, 1)
                    v.wait_ge(sV, base + 2)
                    dst = stream[:, dc * HALF + g * GT:dc * HALF + (g + 1) * GT]
                    v.tensor_tensor(dst, dst, xsq8[:, dc * GT:(dc + 1) * GT],
                                    OP.add).then_inc(sV, 1)
        N['V'] = V0 + NG8 * NVG

        @blk.scalar
        def _(sc):
            for g in range(NG8):
                sc.wait_ge(sV, V8(g, 12))
                sc.activation(sqrow[:], varr[:], AF.Sqrt).then_inc(sS, 1)
                sc.wait_ge(sT, T8(g, 3))
                if g >= 1:
                    sc.wait_ge(sV, V8(g - 1, 32))
                sc.activation(rstdb[:], pr1[:], AF.Copy).then_inc(sS, 1)
                sc.wait_ge(sT, T8(g, 4))
                sc.activation(mrb[:], pr2[:], AF.Copy).then_inc(sS, 1)
                for io in range(16):
                    sc.wait_ge(sT, T8(g, 5 + io))
                    pm = pm0 if io % 2 == 0 else pm1
                    sc.activation(xin[:, io * GT:(io + 1) * GT], pm[:], AF.Identity,
                                  bias=b_in_sb[:, io:io + 1]).then_inc(sS, 1)
                for oo in range(16):
                    sc.wait_ge(sT, T8(g, 21 + oo))
                    if oo >= 2:
                        sc.wait_ge(sV, V8(g, 33 + oo - 2))
                    pm = pm0 if oo % 2 == 0 else pm1
                    sc.activation(silus[:, (oo % 2) * GT:(oo % 2 + 1) * GT], pm[:],
                                  AF.Silu, bias=b_g1_sb[:, oo:oo + 1]).then_inc(sS, 1)
                for dc in range(8):
                    sc.wait_ge(sT, T8(g, 37 + dc))
                    if dc >= 2:
                        sc.wait_ge(sV, V8(g, 49 + dc - 2))
                    pm = pm0 if dc % 2 == 0 else pm1
                    sc.activation(wosc[:, (dc % 2) * GT:(dc % 2 + 1) * GT], pm[:],
                                  AF.Identity, bias=b_out_sb[:, dc:dc + 1]).then_inc(sS, 1)
                sc.wait_ge(sV, V8(g, 68))
                sc.activation(sqrow[:], varr[:], AF.Sqrt).then_inc(sS, 1)
                sc.wait_ge(sT, T8(g, 47))
                if g >= 1:
                    sc.wait_ge(sV, V8(g - 1, 73 + 4 * 7 + 1))
                sc.activation(rstdb[:], pr1[:], AF.Copy).then_inc(sS, 1)
                sc.wait_ge(sT, T8(g, 48))
                sc.activation(mrb[:], pr2[:], AF.Copy).then_inc(sS, 1)
        N['S'] = S0 + NG8 * NSG

        @blk.tensor
        def _(t):
            for g in range(NG8):
                if g >= 1:
                    t.wait_ge(sV, V8(g - 1, 9))
                for dc in range(8):
                    mi = t.matmul(psA[:], ones_c128[:],
                                  stream[:, dc * HALF + g * GT:dc * HALF + (g + 1) * GT],
                                  start=(dc == 0), stop=(dc == 7))
                mi.then_inc(sT, 1)
                t.wait_ge(sV, V8(g, 8))
                if g >= 1:
                    t.wait_ge(sV, V8(g - 1, 10))
                for dc in range(8):
                    mi = t.matmul(psB[:], ones_c128[:], xsq8[:, dc * GT:(dc + 1) * GT],
                                  start=(dc == 0), stop=(dc == 7))
                mi.then_inc(sT, 1)
                t.wait_ge(sV, V8(g, 15))
                if g >= 1:
                    t.wait_ge(sS, S8(g - 1, 2))
                t.matmul(pr1[:], ones_row[0:1, 0:P], rbr[:], start=True, stop=True).then_inc(sT, 1)
                t.wait_ge(sV, V8(g, 16))
                if g >= 1:
                    t.wait_ge(sS, S8(g - 1, 3))
                t.matmul(pr2[:], ones_row[0:1, 0:P], mbr[:], start=True, stop=True).then_inc(sT, 1)
                t.wait_ge(sV, V8(g, 32))
                for io in range(16):
                    gl = g * 40 + io
                    t.wait_ge(sWI0 if gl % 2 == 0 else sWI1, 16 * (gl // 2 + 1))
                    if io >= 2:
                        t.wait_ge(sS, S8(g, 4 + io - 2) + 0)
                    pm = pm0 if io % 2 == 0 else pm1
                    for dc in range(8):
                        mi = t.matmul(pm[:],
                                      win8[:, (gl % 2) * 8 * P + dc * P:(gl % 2) * 8 * P + (dc + 1) * P],
                                      xh2[:, dc * GT:(dc + 1) * GT],
                                      start=(dc == 0), stop=(dc == 7))
                    mi.then_inc(sT, 1)
                for oo in range(16):
                    gl = g * 40 + 16 + oo
                    t.wait_ge(sWI0 if gl % 2 == 0 else sWI1, 16 * (gl // 2 + 1))
                    if oo == 0:
                        t.wait_ge(sS, S8(g, 19))
                    if oo >= 2:
                        t.wait_ge(sS, S8(g, 20 + oo - 2))
                    pm = pm0 if oo % 2 == 0 else pm1
                    for ic in range(16):
                        mi = t.matmul(pm[:],
                                      wgg8[:, (gl % 2) * 16 * P + ic * P:(gl % 2) * 16 * P + (ic + 1) * P],
                                      xin[:, ic * GT:(ic + 1) * GT],
                                      start=(ic == 0), stop=(ic == 15))
                    mi.then_inc(sT, 1)
                for dc in range(8):
                    gl = g * 40 + 32 + dc
                    t.wait_ge(sWI0 if gl % 2 == 0 else sWI1, 16 * (gl // 2 + 1))
                    if dc == 0:
                        t.wait_ge(sV, V8(g, 48))
                    if dc >= 2:
                        t.wait_ge(sS, S8(g, 36 + dc - 2))
                    pm = pm0 if dc % 2 == 0 else pm1
                    for ic in range(16):
                        mi = t.matmul(pm[:],
                                      wou8[:, (gl % 2) * 16 * P + ic * P:(gl % 2) * 16 * P + (ic + 1) * P],
                                      tsl[:, ic * GT:(ic + 1) * GT],
                                      start=(ic == 0), stop=(ic == 15))
                    mi.then_inc(sT, 1)
                t.wait_ge(sV, V8(g, 64))
                for dc in range(8):
                    mi = t.matmul(psA[:], ones_c128[:], vbuf8[:, dc * GT:(dc + 1) * GT],
                                  start=(dc == 0), stop=(dc == 7))
                mi.then_inc(sT, 1)
                for dc in range(8):
                    mi = t.matmul(psB[:], ones_c128[:], xsq8[:, dc * GT:(dc + 1) * GT],
                                  start=(dc == 0), stop=(dc == 7))
                mi.then_inc(sT, 1)
                t.wait_ge(sV, V8(g, 71))
                if g >= 1:
                    t.wait_ge(sS, S8(g - 1, 45))
                t.matmul(pr1[:], ones_row[0:1, 0:P], rbr[:], start=True, stop=True).then_inc(sT, 1)
                t.wait_ge(sV, V8(g, 72))
                if g >= 1:
                    t.wait_ge(sS, S8(g - 1, 46))
                t.matmul(pr2[:], ones_row[0:1, 0:P], mbr[:], start=True, stop=True).then_inc(sT, 1)
        N['T'] = T0 + NG8 * NTG
        DMAT.update(sWI0=16 * (NG8 * 40 // 2), sWI1=16 * (NG8 * 40 // 2))

        if BUILD_UPTO >= 9:
            C0g = N['C']

            @blk.gpsimd
            def _(gp):
                gp.wait_ge(b._sems["sAG0"], N['AGW'][0])
                gp.wait_ge(b._sems["sAG1"], N['AGW'][1])
                gp.collective_compute("AllGather", OP.bypass, replica_groups=ALL8,
                                      ins=[agg_in.opt()], outs=[agg_out.opt()]).then_inc(sC, 1)
                gp.collective_compute("AllGather", OP.bypass, replica_groups=ALL8,
                                      ins=[agu_in.opt()], outs=[agu_out.opt()]).then_inc(sC, 1)
                gp.collective_compute("AllGather", OP.bypass, replica_groups=ALL8,
                                      ins=[agd_in.opt()], outs=[agd_out.opt()]).then_inc(sC, 1)
            N['CAGU'] = C0g + 2
            N['CAGD'] = C0g + 3
            N['C'] = C0g + 3

        bar()
        b.pop()

    if BUILD_UPTO >= 10:
        for hf in range(2):
            TB = hf * 2048
            b.push()
            xqT9 = b.sb("xqT9_%d" % hf, [P, 8 * 2048], BF16)
            b.push()
            xt9 = b.sb("xt9_%d" % hf, [P, 2 * D], BF16)
            sq9 = b.sb("sq9_%d" % hf, [P, 2 * D], F32)
            xhf9 = b.sb("xhf9_%d" % hf, [P, 2 * D], F32)
            axq9 = b.sb("axq9_%d" % hf, [P, 2 * D], F32)
            sg9 = b.sb("sg9_%d" % hf, [P, 2 * D], BF16)
            c19 = b.sb("c19_%d" % hf, [P, D], F32)
            c29 = b.sb("c29_%d" % hf, [P, D], F32)
            xqt9 = b.sb("xqt9_%d" % hf, [P, 2 * D], BF16)
            nr9 = b.sb("nr9_%d" % hf, [P, 2 * 8], F32)
            qa9 = b.sb("qa9_%d" % hf, [P, 64], F32)
            scb9 = b.sb("scb9_%d" % hf, [P, 64], F32)
            qi19 = b.sb("qi19_%d" % hf, [P, 64], I32)
            qi29 = b.sb("qi29_%d" % hf, [P, 64], I32)
            sc89 = b.sb("sc89_%d" % hf, [P, 64], F32)
            dq9 = b.sb("dq9_%d" % hf, [P, 64], F32)
            rdq9 = b.sb("rdq9_%d" % hf, [P, 64], F32)
            dqh9 = b.sb("dqh9_%d" % hf, [P, 64], F32)
            ptx0 = b.ps("ptx0_%d" % hf, [P, P], BF16)
            ptx1 = b.ps("ptx1_%d" % hf, [P, P], BF16)
            ptq0 = b.ps("ptq0_%d" % hf, [P, P], BF16)
            ptq1 = b.ps("ptq1_%d" % hf, [P, P], BF16)
            T0 = N['T']; V0 = N['V']; S0 = N['S']
            NU = 16
            VLNa = {}; VLNb = {}; VPIPE = {}
            pos = V0
            for it in range(NU + 2):
                if it < NU:
                    VLNa[it] = pos; pos += 6
                if 0 <= it - 1 < NU:
                    VLNb[it - 1] = pos; pos += 3
                if 0 <= it - 2 < NU:
                    VPIPE[it - 2] = pos; pos += NPIPE
            NV_END = pos
            SEV = {}; SSQ = {}; SSQRT = {}; SABS = {}; SSGN = {}; SXQ = {}
            pos = S0
            for it in range(NU + 2):
                if it < NU:
                    SEV[it] = pos; pos += 8
                    SSQ[it] = pos; pos += 1
                if 0 <= it - 1 < NU:
                    SSQRT[it - 1] = pos; pos += 1
                    SABS[it - 1] = pos; pos += 1
                    SSGN[it - 1] = pos; pos += 1
                if 0 <= it - 2 < NU:
                    SXQ[it - 2] = pos; pos += 8
            NS_END = pos
            TXP = {}; TQP = {}
            pos = T0
            for it in range(NU + 2):
                if it < NU:
                    TXP[it] = pos; pos += 8
                if 0 <= it - 2 < NU:
                    TQP[it - 2] = pos; pos += 8
            NT_END = pos

            @blk.tensor
            def _(t, TXP=TXP, TQP=TQP, SEV=SEV, SXQ=SXQ, VPIPE=VPIPE):
                for it in range(NU + 2):
                    if it < NU:
                        u = it
                        gt = hf * 16 + u
                        for dc in range(8):
                            if dc >= 2:
                                t.wait_ge(sS, SEV[u] + (dc - 2) + 1)
                            elif u >= 1:
                                t.wait_ge(sS, SEV[u - 1] + (dc + 6) + 1)
                            pm = ptx0 if dc % 2 == 0 else ptx1
                            t.transpose(pm[:], stream[:, dc * HALF + gt * P:dc * HALF + (gt + 1) * P],
                                        eye[:]).then_inc(sT, 1)
                    if 0 <= it - 2 < NU:
                        u = it - 2
                        t.wait_ge(sV, VPIPE[u] + NPIPE)
                        for hc in range(8):
                            if hc >= 2:
                                t.wait_ge(sS, SXQ[u] + (hc - 2) + 1)
                            elif u >= 1:
                                t.wait_ge(sS, SXQ[u - 1] + (hc + 6) + 1)
                            pm = ptq0 if hc % 2 == 0 else ptq1
                            t.transpose(pm[:], xqt9[:, (u % 2) * D + hc * P:(u % 2) * D + (hc + 1) * P],
                                        eye[:]).then_inc(sT, 1)
            N['T'] = NT_END

            @blk.scalar
            def _(sc, TXP=TXP, TQP=TQP, VLNa=VLNa, VLNb=VLNb, VPIPE=VPIPE,
                  SEV=SEV, SSQ=SSQ):
                for it in range(NU + 2):
                    if it < NU:
                        u = it
                        sl = (u % 2) * D
                        for dc in range(8):
                            sc.wait_ge(sT, TXP[u] + dc + 1)
                            if dc == 0 and u >= 2:
                                sc.wait_ge(sV, VLNb[u - 2] + 3)
                            pm = ptx0 if dc % 2 == 0 else ptx1
                            sc.activation(xt9[:, sl + dc * P:sl + (dc + 1) * P], pm[:],
                                          AF.Copy).then_inc(sS, 1)
                        if u >= 2:
                            sc.wait_ge(sV, VLNa[u - 2] + 3)
                        sc.wait_ge(sS, SEV[u] + 8)
                        sc.activation(sq9[:, sl:sl + D], xt9[:, sl:sl + D],
                                      AF.Square).then_inc(sS, 1)
                    if 0 <= it - 1 < NU:
                        u = it - 1
                        sl = (u % 2) * D
                        ns = (u % 2) * 8
                        sc.wait_ge(sV, VLNa[u] + 6)
                        sc.activation(nr9[:, ns + 6:ns + 7], nr9[:, ns + 5:ns + 6],
                                      AF.Sqrt).then_inc(sS, 1)
                        sc.wait_ge(sV, VLNb[u] + 3)
                        if u >= 2:
                            sc.wait_ge(sV, VPIPE[u - 2] + 17)
                        sc.activation(axq9[:, sl:sl + D], xhf9[:, sl:sl + D],
                                      AF.Abs).then_inc(sS, 1)
                        if u >= 2:
                            sc.wait_ge(sV, VPIPE[u - 2] + NPIPE)
                        sc.activation(sg9[:, sl:sl + D], xhf9[:, sl:sl + D],
                                      AF.Sign).then_inc(sS, 1)
                    if 0 <= it - 2 < NU:
                        u = it - 2
                        for hc in range(8):
                            sc.wait_ge(sT, TQP[u] + hc + 1)
                            pm = ptq0 if hc % 2 == 0 else ptq1
                            sc.activation(xqT9[:, hc * 2048 + u * P:hc * 2048 + (u + 1) * P],
                                          pm[:], AF.Copy).then_inc(sS, 1)
            N['S'] = NS_END

            @blk.vector
            def _(v, VLNa=VLNa, VLNb=VLNb, VPIPE=VPIPE, SEV=SEV, SSQ=SSQ,
                  SSQRT=SSQRT, SABS=SABS, SSGN=SSGN, TQP=TQP):
                for it in range(NU + 2):
                    if it < NU:
                        u = it
                        sl = (u % 2) * D
                        ns = (u % 2) * 8
                        base = VLNa[u]
                        v.wait_ge(sS, SEV[u] + 8)
                        if u >= 2:
                            v.wait_ge(sV, VLNb[u - 2] + 3)
                        v.tensor_reduce(nr9[:, ns:ns + 1], xt9[:, sl:sl + D], AX,
                                        OP.add).then_inc(sV, 1)
                        v.wait_ge(sV, base + 1)
                        v.tensor_scalar_mul(nr9[:, ns + 1:ns + 2], nr9[:, ns:ns + 1],
                                            1.0 / 1024.0).then_inc(sV, 1)
                        v.wait_ge(sS, SSQ[u] + 1)
                        v.tensor_reduce(nr9[:, ns + 2:ns + 3], sq9[:, sl:sl + D], AX,
                                        OP.add).then_inc(sV, 1)
                        v.wait_ge(sV, base + 3)
                        v.tensor_scalar(nr9[:, ns + 3:ns + 4], nr9[:, ns + 2:ns + 3],
                                        1.0 / 1024.0, 1e-5, OP.mult, OP.add).then_inc(sV, 1)
                        v.tensor_tensor(nr9[:, ns + 4:ns + 5], nr9[:, ns + 1:ns + 2],
                                        nr9[:, ns + 1:ns + 2], OP.mult).then_inc(sV, 1)
                        v.wait_ge(sV, base + 5)
                        v.tensor_tensor(nr9[:, ns + 5:ns + 6], nr9[:, ns + 3:ns + 4],
                                        nr9[:, ns + 4:ns + 5], OP.subtract).then_inc(sV, 1)
                    if 0 <= it - 1 < NU:
                        u = it - 1
                        sl = (u % 2) * D
                        ns = (u % 2) * 8
                        base = VLNb[u]
                        v.wait_ge(sS, SSQRT[u] + 1)
                        v.reciprocal(nr9[:, ns + 7:ns + 8], nr9[:, ns + 6:ns + 7]).then_inc(sV, 1)
                        v.wait_ge(sV, base + 1)
                        v.tensor_tensor(nr9[:, ns + 4:ns + 5], nr9[:, ns + 1:ns + 2],
                                        nr9[:, ns + 7:ns + 8], OP.mult).then_inc(sV, 1)
                        v.wait_ge(sV, base + 2)
                        if u >= 1:
                            v.wait_ge(sS, SSGN[u - 1] + 1)
                        v.tensor_scalar(xhf9[:, sl:sl + D], xt9[:, sl:sl + D],
                                        nr9[0:P, ns + 7:ns + 8], nr9[0:P, ns + 4:ns + 5],
                                        OP.mult, OP.subtract).then_inc(sV, 1)
                    if 0 <= it - 2 < NU:
                        u = it - 2
                        v.wait_ge(sS, SABS[u] + 1)
                        if u >= 2:
                            v.wait_ge(sT, TQP[u - 2] + 8)
                        _qpipe_v(v, VPIPE[u], axq9[:, (u % 2) * D:(u % 2) * D + D],
                                 c19, c29, qa9, scb9, qi19, qi29,
                                 sc89, dq9, rdq9, dqh9, mgkc,
                                 sccol[:, 6:7], sccol[:, 8:9], 64,
                                 xqt9[:, (u % 2) * D:(u % 2) * D + D], sg9[:, (u % 2) * D:(u % 2) * D + D],
                                 sS, SSGN[u] + 1)
            N['V'] = NV_END

            bar()
            b.pop()

            b.push()
            wg9 = b.sb("wg9_%d" % hf, [P, 2 * 8 * 512], BF16)
            wu9 = b.sb("wu9_%d" % hf, [P, 2 * 8 * 512], BF16)
            hs9 = b.sb("hs9_%d" % hf, [P, 2 * 1024], F32)
            ht9 = b.sb("ht9_%d" % hf, [P, 2 * 1024], F32)
            axb = b.sb("axb_%d" % hf, [P, 2 * 1024], F32)
            sgb = b.sb("sgb_%d" % hf, [P, 2 * 1024], BF16)
            c1b = b.sb("c1b_%d" % hf, [P, 1024], F32)
            c2b = b.sb("c2b_%d" % hf, [P, 1024], F32)
            hq9 = b.sb("hq9_%d" % hf, [P, 2 * 1024], BF16)
            hqs = b.sb("hqs_%d" % hf, [P, 2 * 1024], BF16)
            qab = b.sb("qab_%d" % hf, [P, 64], F32)
            scbb = b.sb("scbb_%d" % hf, [P, 64], F32)
            qi1b = b.sb("qi1b_%d" % hf, [P, 64], I32)
            qi2b = b.sb("qi2b_%d" % hf, [P, 64], I32)
            sc8b = b.sb("sc8b_%d" % hf, [P, 64], F32)
            dqb = b.sb("dqb_%d" % hf, [P, 64], F32)
            rdqb = b.sb("rdqb_%d" % hf, [P, 64], F32)
            dqhb = b.sb("dqhb_%d" % hf, [P, 64], F32)
            pg9a = b.ps("pg9a_%d" % hf, [P, 512], F32)
            pg9b = b.ps("pg9b_%d" % hf, [P, 512], F32)
            pu9a = b.ps("pu9a_%d" % hf, [P, 512], F32)
            pu9b = b.ps("pu9b_%d" % hf, [P, 512], F32)
            pq90 = b.ps("pq90_%d" % hf, [P, P], BF16)
            pq91 = b.ps("pq91_%d" % hf, [P, P], BF16)
            sWG0 = b.sem("sWG0_%d" % hf)
            sWG1 = b.sem("sWG1_%d" % hf)
            s9H0 = b.sem("s9H0_%d" % hf)
            s9H1 = b.sem("s9H1_%d" % hf)
            T0 = N['T']; V0 = N['V']; S0 = N['S']
            NB_ = 128
            NP_ = 64
            VH = {}; VPIPE = {}
            pos = V0
            for it in range(NB_ + 2):
                if it < NB_:
                    VH[it] = pos; pos += 1
                if it % 2 == 0 and 0 <= it // 2 - 1 < NP_:
                    VPIPE[it // 2 - 1] = pos; pos += NPIPE
            NV_END = pos
            SSI = {}; SAB = {}; SSG = {}; SHE8 = {}
            pos = S0
            for it in range(NB_ + 4):
                if it < NB_:
                    SSI[it] = pos; pos += 1
                    if it % 2 == 1:
                        SAB[it // 2] = pos; pos += 1
                        SSG[it // 2] = pos; pos += 1
                if it % 2 == 0 and 0 <= it // 2 - 2 < NP_:
                    SHE8[it // 2 - 2] = pos; pos += 8
            NS_END = pos
            TG = {}; TU = {}; TP8 = {}
            pos = T0
            for it in range(NB_ + 4):
                if it < NB_:
                    TG[it] = pos; pos += 1
                    TU[it] = pos; pos += 1
                if it % 2 == 0 and 0 <= it // 2 - 2 < NP_:
                    TP8[it // 2 - 2] = pos; pos += 8
            NT_END = pos

            @blk.sync
            def _(sync, TG=TG, TU=TU):
                sync.wait_ge(sC, N['CAGU'])
                for io8 in range(8):
                    if io8 >= 2:
                        sync.wait_ge(sT, TU[(io8 - 2) * 16 + 15] + 1)
                    sl = io8 % 2
                    sync.dma_start(
                        out=wg9[:, sl * 8 * 512:(sl + 1) * 8 * 512].rearrange(
                            "p (hc q) -> p hc q", q=512),
                        in_=agg_out[io8 * 1024:(io8 + 1) * 1024, :].rearrange(
                            "(hc p) q -> p hc q", p=P)
                    ).then_inc(sWG0 if sl == 0 else sWG1, 16)
                    sync.dma_start(
                        out=wu9[:, sl * 8 * 512:(sl + 1) * 8 * 512].rearrange(
                            "p (hc q) -> p hc q", q=512),
                        in_=agu_out[io8 * 1024:(io8 + 1) * 1024, :].rearrange(
                            "(hc p) q -> p hc q", p=P)
                    ).then_inc(sWG0 if sl == 0 else sWG1, 16)

            @blk.tensor
            def _(t, TG=TG, TU=TU, TP8=TP8, SSI=SSI, SHE8=SHE8, VH=VH, VPIPE=VPIPE):
                for it in range(NB_ + 4):
                    if it < NB_:
                        u = it
                        io8, tt = divmod(u, 16)
                        sl = io8 % 2
                        us = u % 2
                        t.wait_ge(sWG0 if sl == 0 else sWG1, 32 * (io8 // 2 + 1))
                        if u >= 2:
                            t.wait_ge(sS, SSI[u - 2] + 1)
                        pg = pg9a if us == 0 else pg9b
                        for hc in range(8):
                            mi = t.matmul(pg[:],
                                          xqT9[:, hc * 2048 + tt * P:hc * 2048 + (tt + 1) * P],
                                          wg9[:, sl * 8 * 512 + hc * 512:sl * 8 * 512 + (hc + 1) * 512],
                                          start=(hc == 0), stop=(hc == 7))
                        mi.then_inc(sT, 1)
                        if u >= 2:
                            t.wait_ge(sV, VH[u - 2] + 1)
                        pu = pu9a if us == 0 else pu9b
                        for hc in range(8):
                            mi = t.matmul(pu[:],
                                          xqT9[:, hc * 2048 + tt * P:hc * 2048 + (tt + 1) * P],
                                          wu9[:, sl * 8 * 512 + hc * 512:sl * 8 * 512 + (hc + 1) * 512],
                                          start=(hc == 0), stop=(hc == 7))
                        mi.then_inc(sT, 1)
                    if it % 2 == 0 and 0 <= it // 2 - 2 < NP_:
                        p = it // 2 - 2
                        ps2 = p % 2
                        t.wait_ge(sV, VPIPE[p] + NPIPE)
                        for i8 in range(8):
                            if i8 >= 2:
                                t.wait_ge(sS, SHE8[p] + (i8 - 2) + 1)
                            elif p >= 1:
                                t.wait_ge(sS, SHE8[p - 1] + (i8 + 6) + 1)
                            pm = pq90 if i8 % 2 == 0 else pq91
                            t.transpose(pm[:], hq9[:, ps2 * 1024 + i8 * P:ps2 * 1024 + (i8 + 1) * P],
                                        eye[:]).then_inc(sT, 1)
            N['T'] = NT_END

            @blk.scalar
            def _(sc, TG=TG, TU=TU, TP8=TP8, SSI=SSI, SAB=SAB, SSG=SSG, SHE8=SHE8,
                  VH=VH, VPIPE=VPIPE):
                for it in range(NB_ + 4):
                    if it < NB_:
                        u = it
                        us = u % 2
                        ps2 = (u // 2) % 2
                        sc.wait_ge(sT, TG[u] + 1)
                        if u >= 4:
                            sc.wait_ge(sV, VH[u - 4] + 1)
                        pg = pg9a if us == 0 else pg9b
                        sc.activation(hs9[:, ps2 * 1024 + us * 512:ps2 * 1024 + (us + 1) * 512],
                                      pg[:], AF.Silu).then_inc(sS, 1)
                        if u % 2 == 1:
                            p = u // 2
                            sc.wait_ge(sV, VH[u] + 1)
                            if p >= 2:
                                sc.wait_ge(sV, VPIPE[p - 2] + 17)
                            sc.activation(axb[:, ps2 * 1024:(ps2 + 1) * 1024],
                                          ht9[:, ps2 * 1024:(ps2 + 1) * 1024],
                                          AF.Abs).then_inc(sS, 1)
                            if p >= 2:
                                sc.wait_ge(sV, VPIPE[p - 2] + NPIPE)
                            sc.activation(sgb[:, ps2 * 1024:(ps2 + 1) * 1024],
                                          ht9[:, ps2 * 1024:(ps2 + 1) * 1024],
                                          AF.Sign).then_inc(sS, 1)
                    if it % 2 == 0 and 0 <= it // 2 - 2 < NP_:
                        p = it // 2 - 2
                        ps2 = p % 2
                        io8, tt = divmod(2 * p, 16)
                        for i8 in range(8):
                            sc.wait_ge(sT, TP8[p] + i8 + 1)
                            if i8 == 0 and p >= 2:
                                sc.wait_ge(s9H0 if p % 2 == 0 else s9H1,
                                           32 * (p // 2))
                            pm = pq90 if i8 % 2 == 0 else pq91
                            sc.activation(hqs[:, ps2 * 1024 + i8 * P:ps2 * 1024 + (i8 + 1) * P],
                                          pm[:], AF.Copy).then_inc(sS, 1)
                        sc.wait_ge(sS, SHE8[p] + 8)
                        for uh in range(2):
                            sc.dma_start(
                                out=hqbuf[io8 * 512:(io8 + 1) * 512,
                                          TB + (tt + uh) * P:TB + (tt + uh + 1) * P].rearrange(
                                    "(a p) q -> p a q", p=P),
                                in_=hqs[:, ps2 * 1024 + uh * 512:ps2 * 1024 + (uh + 1) * 512].rearrange(
                                    "p (a q) -> p a q", q=P)
                            ).then_inc(s9H0 if p % 2 == 0 else s9H1, 16)
            N['S'] = NS_END

            @blk.vector
            def _(v, TG=TG, TU=TU, TP8=TP8, SSI=SSI, SAB=SAB, SSG=SSG,
                  VH=VH, VPIPE=VPIPE):
                for it in range(NB_ + 2):
                    if it < NB_:
                        u = it
                        us = u % 2
                        ps2 = (u // 2) % 2
                        v.wait_ge(sS, SSI[u] + 1)
                        v.wait_ge(sT, TU[u] + 1)
                        if u >= 4:
                            v.wait_ge(sS, SAB[u // 2 - 2] + 1)
                            v.wait_ge(sS, SSG[u // 2 - 2] + 1)
                        pu = pu9a if us == 0 else pu9b
                        v.tensor_tensor(ht9[:, ps2 * 1024 + us * 512:ps2 * 1024 + (us + 1) * 512],
                                        hs9[:, ps2 * 1024 + us * 512:ps2 * 1024 + (us + 1) * 512],
                                        pu[:], OP.mult).then_inc(sV, 1)
                    if it % 2 == 0 and 0 <= it // 2 - 1 < NP_:
                        p = it // 2 - 1
                        ps2 = p % 2
                        v.wait_ge(sS, SAB[p] + 1)
                        if p >= 2:
                            v.wait_ge(sT, TP8[p - 2] + 8)
                        _qpipe_v(v, VPIPE[p], axb[:, ps2 * 1024:(ps2 + 1) * 1024],
                                 c1b, c2b, qab, scbb, qi1b, qi2b, sc8b, dqb, rdqb,
                                 dqhb, mgkc,
                                 sccol[:, 7:8], sccol[:, 9:10], 64,
                                 hq9[:, ps2 * 1024:(ps2 + 1) * 1024],
                                 sgb[:, ps2 * 1024:(ps2 + 1) * 1024],
                                 sS, SSG[p] + 1)
            N['V'] = NV_END
            DMAT.update({("sWG0_%d" % hf): 32 * 4, ("sWG1_%d" % hf): 32 * 4,
                         ("s9H0_%d" % hf): 32 * 32, ("s9H1_%d" % hf): 32 * 32})

            bar()
            b.pop()
            b.pop()

            b.push()
            dwq9 = b.sb("dwq9_%d" % hf, [P, 32 * 1024], BF16)
            hqT9 = b.sb("hqT9_%d" % hf, [P, 2 * 32 * 512], BF16)
            pd90 = b.ps("pd90_%d" % hf, [P, 512], F32)
            pd91 = b.ps("pd91_%d" % hf, [P, 512], F32)
            sD9 = b.sem("sD9_%d" % hf)
            s9K0 = b.sem("s9K0_%d" % hf)
            s9K1 = b.sem("s9K1_%d" % hf)
            T0 = N['T']; V0 = N['V']; S0 = N['S']

            @blk.sync
            def _(sync, T0=T0, V0=V0, S0=S0):
                sync.wait_ge(sC, N['CAGD'])
                for oc in range(8):
                    sync.dma_start(
                        out=dwq9[:].rearrange("p (ic q) -> p ic q", q=1024)[:, :, oc * P:(oc + 1) * P],
                        in_=agd_out[oc * IFF:(oc + 1) * IFF, :].rearrange(
                            "(ic p) q -> p ic q", p=P)
                    ).then_inc(sD9, 16)
                for tgr in range(4):
                    if tgr >= 2:
                        sync.wait_ge(sV, V0 + (tgr - 2) * 8 + 8)
                    sl = tgr % 2
                    sync.dma_start(
                        out=hqT9[:, sl * 32 * 512:(sl + 1) * 32 * 512].rearrange(
                            "p (ic q) -> p ic q", q=512),
                        in_=hqbuf[:, TB + tgr * 512:TB + (tgr + 1) * 512].rearrange(
                            "(ic p) q -> p ic q", p=P)
                    ).then_inc(s9K0 if sl == 0 else s9K1, 16)

            @blk.tensor
            def _(t, T0=T0, V0=V0, S0=S0):
                t.wait_ge(sD9, 16 * 8)
                for u2 in range(32):
                    tgr, oc = divmod(u2, 8)
                    sl = tgr % 2
                    t.wait_ge(s9K0 if sl == 0 else s9K1, 16 * (tgr // 2 + 1))
                    if u2 >= 2:
                        t.wait_ge(sV, V0 + (u2 - 2) + 1)
                    pm = pd90 if oc % 2 == 0 else pd91
                    for ic in range(32):
                        mi = t.matmul(pm[:],
                                      dwq9[:, ic * 1024 + oc * P:ic * 1024 + (oc + 1) * P],
                                      hqT9[:, sl * 32 * 512 + ic * 512:sl * 32 * 512 + (ic + 1) * 512],
                                      start=(ic == 0), stop=(ic == 31))
                    mi.then_inc(sT, 1)
            N['T'] = T0 + 32

            @blk.vector
            def _(v, T0=T0, V0=V0, S0=S0):
                for u2 in range(32):
                    tgr, oc = divmod(u2, 8)
                    v.wait_ge(sT, T0 + u2 + 1)
                    pm = pd90 if oc % 2 == 0 else pd91
                    dst = stream[:, oc * HALF + TB + tgr * 512:oc * HALF + TB + (tgr + 1) * 512]
                    v.tensor_tensor(dst, dst, pm[:], OP.add).then_inc(sV, 1)
            N['V'] = V0 + 32
            DMAT.update({("sD9_%d" % hf): 16 * 8, ("s9K0_%d" % hf): 16 * 2,
                         ("s9K1_%d" % hf): 16 * 2})

            bar()
            b.pop()

    if BUILD_UPTO >= 11:
        b.push()
        of32 = b.sb("of32", [P, 2 * D], F32)
        po0 = b.ps("po0", [P, P], BF16)
        po1 = b.ps("po1", [P, P], BF16)
        sOUT0 = b.sem("sOUT0")
        sOUT1 = b.sem("sOUT1")
        T0 = N['T']; V0 = N['V']; S0 = N['S']

        @blk.tensor
        def _(t):
            for tt in range(32):
                for dc in range(8):
                    i = tt * 8 + dc
                    if i >= 2:
                        t.wait_ge(sS, S0 + (i - 2) + 1)
                    pm = po0 if i % 2 == 0 else po1
                    t.transpose(pm[:], stream[:, dc * HALF + tt * P:dc * HALF + (tt + 1) * P],
                                eye[:]).then_inc(sT, 1)
        N['T'] = T0 + 256

        @blk.scalar
        def _(sc):
            for tt in range(32):
                for dc in range(8):
                    i = tt * 8 + dc
                    sc.wait_ge(sT, T0 + i + 1)
                    if dc == 0 and tt >= 2:
                        sc.wait_ge(sOUT0 if tt % 2 == 0 else sOUT1, 16 * (tt // 2))
                    pm = po0 if i % 2 == 0 else po1
                    sc.activation(of32[:, (tt % 2) * D + dc * P:(tt % 2) * D + (dc + 1) * P],
                                  pm[:], AF.Copy).then_inc(sS, 1)
        N['S'] = S0 + 256

        @blk.sync
        def _(sync):
            for tt in range(32):
                sync.wait_ge(sS, S0 + tt * 8 + 8)
                sync.dma_start(out=out_d[tt * P:(tt + 1) * P, :],
                               in_=of32[:, (tt % 2) * D:(tt % 2 + 1) * D]
                               ).then_inc(sOUT0 if tt % 2 == 0 else sOUT1, 16)
            sync.wait_ge(sOUT0, 16 * 16)
            sync.wait_ge(sOUT1, 16 * 16)
        DMAT.update(sOUT0=16 * 16, sOUT1=16 * 16)

        bar()
        b.pop()

    if b.taps:
        @blk.sync
        def _(sync):
            sync.wait_ge(sQ, N['Q'])
            sync.wait_ge(sG, N['G'])
            if BUILD_UPTO >= 3:
                sync.wait_ge(b._sems["sGC0"], 16 * (NH * 16 // 2))
                sync.wait_ge(b._sems["sGC1"], 16 * (NH * 16 // 2))
            nf = 0
            tp = b.taps.get("qkv")
            if tp is not None:
                sync.dma_start(out=tp[0:64, :], in_=kbuf[0:64, 0:S]).then_inc(sC, 16); nf += 16
                sync.dma_start(out=tp[64:128, :], in_=vbuf[0:64, 0:S]).then_inc(sC, 16); nf += 16
                sync.dma_start(out=tp[128:192, :], in_=qbuf[0:64, 0:S]).then_inc(sC, 16); nf += 16
            tp = b.taps.get("gates")
            if tp is not None:
                sync.dma_start(out=tp[:], in_=gcol[:]).then_inc(sC, 16); nf += 16
            tp = b.taps.get("comp")
            if tp is not None and BUILD_UPTO >= 2:
                for h in range(NH):
                    src = ckTaE if h % 2 == 0 else ckTaO
                    sync.dma_start(out=tp[:, h * NBLK:(h + 1) * NBLK],
                                   in_=src[:, (h // 2) * NBLK:(h // 2 + 1) * NBLK]).then_inc(sC, 16)
                    nf += 16
            tp = b.taps.get("mks")
            if tp is not None and BUILD_UPTO >= 2:
                sync.dma_start(out=tp[:], in_=mksr[:]).then_inc(sC, 16); nf += 16
            tp = b.taps.get("kwin")
            if tp is not None and BUILD_UPTO >= 2:
                for h in range(NH):
                    src = kwinE if h % 2 == 0 else kwinO
                    sync.dma_start(out=tp[:, h * WIN:(h + 1) * WIN],
                                   in_=src[:, (h // 2) * WIN:(h // 2 + 1) * WIN]).then_inc(sC, 16)
                    nf += 16
            tp = b.taps.get("cvtok")
            if tp is not None and BUILD_UPTO >= 2:
                sync.dma_start(out=tp[:], in_=cvtok[:]).then_inc(sC, 16); nf += 16
            tp = b.taps.get("wvtok")
            if tp is not None and BUILD_UPTO >= 2:
                sync.dma_start(out=tp[:], in_=wvtok[:]).then_inc(sC, 16); nf += 16
            tp = b.taps.get("svtok")
            if tp is not None and BUILD_UPTO >= 4:
                sync.dma_start(out=tp[:], in_=svtok[:]).then_inc(sC, 16); nf += 16
            tp = b.taps.get("p1")
            if tp is not None and BUILD_UPTO >= 3:
                sync.dma_start(out=tp[:], in_=compbuf[:, :]).then_inc(sC, 16); nf += 16
            tp = b.taps.get("bi")
            if tp is not None and BUILD_UPTO >= 3:
                sync.dma_start(out=tp[0:P, 0:4], in_=bi4[:]).then_inc(sC, 16); nf += 16
            tp = b.taps.get("birow")
            if tp is not None and BUILD_UPTO >= 3:
                sync.dma_start(out=tp[:], in_=bi_row2[:]).then_inc(sC, 16); nf += 16
            tp = b.taps.get("sel")
            if tp is not None and BUILD_UPTO >= 3:
                sync.dma_start(out=tp[:], in_=selidx[:]).then_inc(sC, 16); nf += 16
            tp = b.taps.get("stream0")
            if tp is not None and BUILD_UPTO >= 6:
                sync.dma_start(out=tp[:], in_=stream[:]).then_inc(sC, 16); nf += 16
            tp = b.taps.get("stream1")
            if tp is not None and BUILD_UPTO >= 7:
                sync.dma_start(out=tp[:], in_=stream[:]).then_inc(sC, 16); nf += 16
            tp = b.taps.get("stream2")
            if tp is not None and BUILD_UPTO >= 8:
                sync.dma_start(out=tp[:], in_=stream[:]).then_inc(sC, 16); nf += 16
            tp = b.taps.get("wq")
            if tp is not None and BUILD_UPTO >= 9:
                sync.dma_start(out=tp[:], in_=wqsl[:]).then_inc(sC, 16); nf += 16
            tp = b.taps.get("sccolt")
            if tp is not None and BUILD_UPTO >= 9:
                sync.dma_start(out=tp[:, 0:10], in_=sccol[:, 0:10]).then_inc(sC, 16); nf += 16
            tp = b.taps.get("stream3")
            if tp is not None and BUILD_UPTO >= 10:
                sync.dma_start(out=tp[:], in_=stream[:]).then_inc(sC, 16); nf += 16
            tp = b.taps.get("att")
            if tp is not None and BUILD_UPTO >= 5:
                sync.wait_ge(b._sems["sAT0"], 16 * (NH * 16 // 2))
                sync.wait_ge(b._sems["sAT1"], 16 * (NH * 16 // 2))
                sync.dma_start(out=tp[:], in_=attfull[:, :]).then_inc(sC, 16); nf += 16
            sync.wait_ge(sC, N['C'] + nf)
            N['C'] += nf

    b.handles = dict(locals())
    return b


def _prep_core_inputs(inputs, c):
    f32 = lambda a: np.ascontiguousarray(a, dtype=np.float32)
    bi, hg = divmod(c, 2)
    x = f32(inputs["L0_attn_x"][bi])
    cols = slice(hg * 512, (hg + 1) * 512)
    wqkv = np.concatenate([f32(inputs["L0_attn_Wq"])[:, cols],
                           f32(inputs["L0_attn_Wk"])[:, cols],
                           f32(inputs["L0_attn_Wv"])[:, cols]], axis=1)
    wo = f32(inputs["L0_attn_Wo"])
    wo = np.concatenate([wo[hg * 512:(hg + 1) * 512], wo[(1 - hg) * 512:(2 - hg) * 512]], axis=0)
    pcol = lambda v: np.ascontiguousarray(np.asarray(v, np.float32).reshape(-1, P).T)
    pr = np.arange(P)
    d = dict(
        x=x,
        wqkv=np.ascontiguousarray(wqkv),
        wck=f32(inputs["L0_attn_Wck"]),
        wcv=f32(inputs["L0_attn_Wcv"]),
        bckv=np.stack([f32(inputs["L0_attn_bck"]), f32(inputs["L0_attn_bcv"])], axis=1),
        wg=f32(inputs["L0_attn_Wg"]),
        bg8=np.tile(f32(inputs["L0_attn_bg"])[None, :], (NH, 1)),
        wo=wo,
        w0t=np.ascontiguousarray(f32(inputs["L0_ffn_weight"]).T),
        w_in=f32(inputs["L1_attn_W_in"]),
        w_g1=f32(inputs["L1_attn_W_g"]),
        w_out=f32(inputs["L1_attn_W_out"]),
        b_in=pcol(inputs["L1_attn_b_in"]),
        b_g1=pcol(inputs["L1_attn_b_g"]),
        b_sp=pcol(inputs["L1_attn_b_sp"]),
        b_out=pcol(inputs["L1_attn_b_out"]),
        nw=pcol(inputs["L1_attn_norm_w"]),
        nb=pcol(inputs["L1_attn_norm_b"]),
        gate_sl=f32(inputs["L1_ffn_gate_w"])[c * 512:(c + 1) * 512],
        up_sl=f32(inputs["L1_ffn_up_w"])[c * 512:(c + 1) * 512],
        down_sl=f32(inputs["L1_ffn_down_w"])[c * P:(c + 1) * P],
        sinsd=np.array([[float(np.asarray(inputs["L1_ffn_s_in"]).ravel()[0]),
                        float(np.asarray(inputs["L1_ffn_s_in_down"]).ravel()[0]),
                        0.0, 0.0]], np.float32),
        eyed=np.eye(P, dtype=np.float32),
        ar64=np.arange(HD, dtype=np.float32)[None, :],
        posidx=np.stack([
            hg * HALF + pr,
            (1 - hg) * 512 + pr,
            pr * S + (1 - hg) * HALF,
            pr * S + hg * HALF,
        ], axis=1).astype(np.int32),
    )
    return d


def kernel(**inputs):
    b = build_nc()
    in_maps = [_prep_core_inputs(inputs, c) for c in range(8)]
    res = run_bass_kernel_spmd(b.nc, in_maps, core_ids=list(range(8)))
    b.ctx.close()
    out = np.zeros((4, S, D), dtype=np.float32)
    for c in range(8):
        bi, hg = divmod(c, 2)
        out[bi, hg * HALF:(hg + 1) * HALF] = res.results[c]["out"]
    return out
